# version 01
# baseline (speedup 1.0000x reference)
import numpy as np
import ml_dtypes
import concourse.bass as bass
import concourse.mybir as mybir
from concourse.bass_utils import run_bass_kernel_spmd

F32 = mybir.dt.float32
BF16 = mybir.dt.bfloat16
I32 = mybir.dt.int32
AF = mybir.ActivationFunctionType
OP = mybir.AluOpType
AX = mybir.AxisListType
BFnp = ml_dtypes.bfloat16

D = 1024
NT = 2048
NDC = 8
TT = 512
NTT = 4
TB = 16
TA = 128
NA = 16
NST = 64
EPS = 1e-5
CONSTS = [EPS, -1.0, 1.0 / 7.5, -0.5, 1.0 / D, 0.5, 1.0, 128.0, 0.0]

ENGS = ("sync", "gpsimd", "scalar", "vector", "tensor")


class Ctr:
    def __init__(self, h):
        self.h = h
        self.n = 0


class _CP(dict):
    stopped = False

    def __missing__(self, key):
        if self.stopped:
            return (None, 0)
        raise KeyError(key)


class Prog:
    def __init__(self):
        self.ops = {e: [] for e in ENGS}
        self.cp = _CP()
        self.stopped = False

    def stop(self):
        self.stopped = True
        self.cp.stopped = True

    def step(self, eng, waits=(), sem=None, k=1):
        if self.stopped:
            return lambda fn: None
        wl = []
        for w in waits:
            ctr, v = self.cp[w]
            if v > 0 and ctr is not None:
                wl.append((ctr.h, v))
        if sem is not None:
            sem.n += k

        def deco(fn):
            def run(e):
                for h, v in wl:
                    e.wait_ge(h, v)
                inst = fn(e)
                if sem is not None:
                    assert inst is not None
                    inst.then_inc(sem.h, k)
            self.ops[eng].append(run)
            return None
        return deco

    def mark(self, name, ctr):
        if not self.stopped:
            self.cp[name] = (ctr, ctr.n)


def build_nc(debug=(), stop_after=None):
    nc = bass.Bass("TRN2", target_bir_lowering=False)

    u_h = nc.dram_tensor("u_fm", [D, NT], F32, kind="ExternalInput")
    qT_h = nc.dram_tensor("qT", [6, D, D], I32, kind="ExternalInput")
    scT_h = nc.dram_tensor("scT", [6, D, D], F32, kind="ExternalInput")
    la_h = nc.dram_tensor("la", [6, 32, D], F32, kind="ExternalInput")
    lbT_h = nc.dram_tensor("lbT", [6, 32, D], F32, kind="ExternalInput")
    bias_h = nc.dram_tensor("bias_t", [128, 48], F32, kind="ExternalInput")
    lng_h = nc.dram_tensor("lng_t", [128, 24], F32, kind="ExternalInput")
    lnb_h = nc.dram_tensor("lnb_t", [128, 24], F32, kind="ExternalInput")
    dvec_h = nc.dram_tensor("d_t", [128, 8], F32, kind="ExternalInput")
    gam_h = nc.dram_tensor("gamma_t", [128, 8], F32, kind="ExternalInput")
    bet_h = nc.dram_tensor("beta_t", [128, 8], F32, kind="ExternalInput")
    fa_h = nc.dram_tensor("fa_t", [128, 8], F32, kind="ExternalInput")
    rw_h = nc.dram_tensor("rw_t", [128, 8], F32, kind="ExternalInput")
    ct_h = nc.dram_tensor("CT", [NST, D], F32, kind="ExternalInput")
    alog_h = nc.dram_tensor("A_log_t", [NST, 1], F32, kind="ExternalInput")
    del0_h = nc.dram_tensor("delta0", [NST, 1], F32, kind="ExternalInput")
    isf_h = nc.dram_tensor("is_first", [128, 1], F32, kind="ExternalInput")
    pm_h = nc.dram_tensor("peer_mask", [NST, 4], F32, kind="ExternalInput")
    ident_h = nc.dram_tensor("c_ident", [128, 128], BF16, kind="ExternalInput")
    esb_h = nc.dram_tensor("c_esb", [128, 576], F32, kind="ExternalInput")
    etau_h = nc.dram_tensor("c_etau", [128, 64], F32, kind="ExternalInput")
    eglob_h = nc.dram_tensor("c_eglob", [NST, NT], F32, kind="ExternalInput")
    e16_h = nc.dram_tensor("c_e16", [NST, 16], F32, kind="ExternalInput")
    e2048_h = nc.dram_tensor("c_e2048", [NST, 4], F32, kind="ExternalInput")

    out_h = nc.dram_tensor("out_fm", [D, NT], F32, kind="ExternalOutput")
    dbg = {}
    for name, shape, dt in debug:
        dbg[name] = nc.dram_tensor(name, shape, dt, kind="ExternalOutput")

    wscr = nc.dram_tensor("wscr", [6, D, D], BF16, kind="Internal")
    xscr = nc.dram_tensor("xscr", [D, NT], F32, kind="Internal")
    lamT_d = nc.dram_tensor("lamT_d", [NST, 1], F32, kind="Internal")
    cc_in = nc.dram_tensor("cc_in", [NST, D], F32, kind="Internal")
    cc_out = nc.dram_tensor("cc_out", [4 * NST, D], F32, kind="Internal")

    from contextlib import ExitStack
    ctx = ExitStack()
    P = Prog()
    step, mark = P.step, P.mark

    def SB(name, shape, dtype):
        return ctx.enter_context(nc.sbuf_tensor(name, shape, dtype))

    def PST(name, shape, dtype=F32):
        return ctx.enter_context(nc.psum_tensor(name, shape, dtype))

    def SEM(name):
        return Ctr(ctx.enter_context(nc.semaphore(name)))

    with ctx:
        bigA = SB("bigA", [128, NDC * NT], BF16)
        bigB = SB("bigB", [128, NDC * NT], BF16)
        wsb = [SB(f"wsb{i}", [128, 8192], BF16) for i in range(2)]
        ident = SB("ident", [128, 128], BF16)
        onesc = SB("onesc", [128, 1], BF16)
        onesr = SB("onesr", [1, 128], BF16)
        zrow = SB("zrow", [1, 128], BF16)
        zrow512 = SB("zrow512", [1, 512], BF16)
        cpool = SB("cpool", [128, len(CONSTS)], F32)
        for i, v in enumerate(CONSTS):
            nc.const_aps.register(F32, v, cpool[:, i:i + 1])
        biasv = SB("biasv", [128, 48], F32)
        lngv = SB("lngv", [128, 24], F32)
        lnbv = SB("lnbv", [128, 24], F32)
        dvecv = SB("dvecv", [128, 8], F32)
        gamv = SB("gamv", [128, 8], F32)
        betv = SB("betv", [128, 8], F32)
        fav = SB("fav", [128, 8], F32)
        rw2 = SB("rw2", [128, 8], F32)
        isf = SB("isf", [128, 1], F32)
        ctrep = SB("ctrep", [128, D], BF16)
        lamv = SB("lamv", [NST, 1], F32)
        lam2 = SB("lam2", [128, 1], F32)
        lam128 = SB("lam128", [NST, 1], F32)
        coefs = SB("coefs", [NST, 4], F32)
        sbL = SB("sbL", [128, 576], BF16)
        p16bd = SB("p16bd", [128, 64], BF16)
        p128L = SB("p128L", [NST, 128], BF16)
        pga = SB("pga", [NST, 128], BF16)
        eglt = SB("eglt", [NST, 128], F32)
        p16v = SB("p16v", [NST, 16], BF16)
        k16 = SB("k16", [128, NDC * 16], F32)
        tdiag = SB("tdiag", [128, TB * 128], BF16)
        scr = SB("scr", [128, 576], F32)
        xr = [SB(f"xr{i}", [128, TT], F32) for i in range(3)]
        xr2 = [SB(f"xr2{i}", [128, TT], F32) for i in range(2)]
        xw = [SB(f"xw{i}", [128, TT], F32) for i in range(2)]
        tmp64 = SB("tmp64", [NST, 32], F32)
        sig = SB("sig", [NST, D], F32)
        sigb = SB("sigb", [NST, D], BF16)
        sinit = SB("sinit", [NST, D], F32)
        sinitb = SB("sinitb", [NST, D], BF16)
        cvqa = SB("cvqa", [NST, D], BF16)
        cv2 = SB("cv2", [128, 2 * D], BF16)
        utm = [SB(f"utm{i}", [128, D], BF16) for i in range(2)]
        rowA = SB("rowA", [1, TT], F32)
        rowB = SB("rowB", [1, TT], F32)
        rowC = SB("rowC", [1, TT], BF16)
        rowD = SB("rowD", [1, TT], BF16)
        aBs = [SB(f"aBs{i}", [128, TT], BF16) for i in range(2)]
        rBs = [SB(f"rBs{i}", [128, TT], BF16) for i in range(2)]
        t1s = [SB(f"t1s{i}", [128, TT], F32) for i in range(2)]
        xbt = [SB(f"xbt{i}", [128, TT], BF16) for i in range(2)]
        sqb2 = [SB(f"sqb{i}", [128, TT], BF16) for i in range(2)]
        rstdB = [SB(f"rstdB{i}", [128, TT], BF16) for i in range(2)]
        lab = SB("lab", [32, D], BF16)
        lbb = SB("lbb", [32, D], BF16)
        wrow = SB("wrow", [128, 1], F32)
        bb16 = SB("bb16", [128, 8], BF16)
        bmean = SB("bmean", [128, 1], F32)
        lamT16 = SB("lamT16", [1, NST], F32)
        lamT16b = SB("lamT16b", [1, NST], BF16)
        lamTb = SB("lamTb", [128, NST], F32)

        def y1c(ti):
            o = 0 if ti % 2 == 0 else 8192
            return bigB[:, o:o + 4096]
        gb_sl = bigB[:, 4096:8192]
        qb_sl = [bigB[:, 12288:13312], bigB[:, 13312:14336]]
        scb_sl = [bigB[:, 14336:15360], bigB[:, 15360:16384]]

        pmm = [PST(f"pmm{i}", [128, TT], F32) for i in range(2)]
        prow = PST("prow", [1, TT], F32)
        pbc = PST("pbc", [128, TT], F32)
        pwA = PST("pwA", [128, 1024], F32)
        pwB = PST("pwB", [128, 1024], F32)

        s_in = SEM("s_in")
        s_w = SEM("s_w")
        s_xb = [SEM(f"s_xb{i}") for i in range(2)]
        s_st = SEM("s_st")
        s_qg = SEM("s_qg")
        s_pg = SEM("s_pg")
        s_cc = SEM("s_cc")
        tS = SEM("tS")
        chV = SEM("chV")
        chS = SEM("chS")

        def chain(e, inst, ch=None):
            ch = ch or chV
            ch.n += 1
            inst.then_inc(ch.h, 1)
            e.wait_ge(ch.h, ch.n)
        tT = SEM("tT")
        tV = SEM("tV")
        tG = SEM("tG")

        _p0 = [(bias_h.ap(), biasv[:, 0:48]), (lng_h.ap(), lngv[:, 0:24]),
               (lnb_h.ap(), lnbv[:, 0:24]), (gam_h.ap(), gamv[:, 0:8]),
               (bet_h.ap(), betv[:, 0:8]), (fa_h.ap(), fav[:, 0:8]),
               (rw_h.ap(), rw2[:, 0:8]), (isf_h.ap(), isf[:, 0:1]),
               (dvec_h.ap(), dvecv[:, 0:8]), (ident_h.ap(), ident[:, :]),
               (esb_h.ap(), scr[:, 0:576]),
               (alog_h.ap(), tmp64[:, 0:1]), (del0_h.ap(), tmp64[:, 1:2]),
               (pm_h.ap(), coefs[:, :]), (e16_h.ap(), tmp64[:, 4:20]),
               (e2048_h.ap(), tmp64[:, 20:24]), (etau_h.ap(), k16[:, 0:64]),
               (bass.AP(eglob_h, 0, [[NT, 64], [1, 128]]), eglt[:, :])]

        @step("sync")
        def _(e):
            for src, dst in _p0:
                e.dma_start(out=dst, in_=src).then_inc(s_in.h, 16)
            return None
        s_in.n += 16 * len(_p0)
        mark("ld_params", s_in)

        @step("gpsimd")
        def _(e):
            e.dma_start(out=ctrep[0:64, :], in_=ct_h.ap()).then_inc(s_qg.h, 16)
            e.dma_start(out=ctrep[64:128, :],
                        in_=ct_h.ap()).then_inc(s_qg.h, 16)
            return None
        s_qg.n += 32
        mark("ld_ct", s_qg)

        @step("gpsimd")
        def _(e):
            e.memset(onesc[:, :], 1.0).then_inc(tG.h, 1)
            e.memset(onesr[:, :], 1.0).then_inc(tG.h, 1)
            e.memset(zrow[:, :], 0.0).then_inc(tG.h, 1)
            e.memset(zrow512[:, :], 0.0).then_inc(tG.h, 1)
            for i, v in enumerate(CONSTS):
                e.memset(cpool[:, i:i + 1], v).then_inc(tG.h, 1)
            return None
        tG.n += 4 + len(CONSTS)
        mark("setup_g", tG)

        @step("scalar", waits=("ld_params", "setup_g"), sem=tS)
        def _(e):
            return e.activation(lamv[0:64, :], tmp64[0:64, 0:1], AF.Exp)
        mark("expA", tS)

        @step("vector", waits=("expA",), sem=tV)
        def _(e):
            chain(e, e.tensor_mul(lamv[0:64, :], lamv[0:64, :],
                                  tmp64[0:64, 1:2]))
            return e.tensor_scalar_mul(lamv[0:64, :], lamv[0:64, :], -1.0)
        mark("lamv", tV)

        @step("vector", waits=("lamv",), sem=tV)
        def _(e):
            e.tensor_copy(lam2[0:64, :], lamv[0:64, :])
            return e.tensor_copy(lam2[64:128, :], lamv[0:64, :])
        mark("lam2", tV)

        @step("scalar", waits=("lamv",), sem=tS)
        def _(e):
            e.activation(lam128[0:64, :], lamv[0:64, :], AF.Exp, scale=128.0)
            e.activation(p16v[0:64, :], tmp64[0:64, 4:20], AF.Exp,
                         scale=lamv[0:64, :])
            e.activation(p128L[0:64, :], eglt[0:64, :], AF.Exp,
                         scale=lamv[0:64, :])
            return e.activation(tmp64[0:64, 28:32], tmp64[0:64, 20:24],
                                AF.Exp, scale=lamv[0:64, :])
        mark("lams", tS)

        @step("vector", waits=("lams",), sem=tV)
        def _(e):
            e.tensor_mul(coefs[0:64, :], coefs[0:64, :], tmp64[0:64, 28:32])
            return e.tensor_scalar_mul(rw2[:, 0:8], rw2[:, 0:8], isf[:, 0:1])
        mark("coefs", tV)

        s_pg_snap = s_pg.n

        @step("gpsimd", waits=("lamv",))
        def _(e):
            e.dma_start(out=lamT_d.ap(),
                        in_=lamv[0:64, :]).then_inc(s_pg.h, 16)
            e.wait_ge(s_pg.h, s_pg_snap + 16)
            e.dma_start(out=lamT16[0:1, 0:64],
                        in_=bass.AP(lamT_d, 0, [[64, 1], [1, 64]])
                        ).then_inc(s_pg.h, 16)
            return None
        s_pg.n += 32
        mark("lamT", s_pg)

        @step("vector", waits=("lamT",), sem=tV)
        def _(e):
            return e.tensor_copy(lamT16b[0:1, :], lamT16[0:1, 0:64])
        mark("lamT16b", tV)

        @step("tensor", waits=("lamT16b", "setup_g", "ld_params"), sem=tT)
        def _(e):
            return e.matmul(pbc[:, 0:64], onesr[0:1, :],
                            lamT16b[0:1, :], start=True, stop=True)
        mark("dAbc", tT)

        @step("vector", waits=("dAbc",), sem=tV)
        def _(e):
            return e.tensor_copy(lamTb[:, :], pbc[:, 0:64])
        mark("lamTb", tV)

        @step("vector", waits=("lamTb",), sem=tV)
        def _(e):
            return e.tensor_mul(
                scr[:, 0:576], scr[:, 0:576],
                bass.AP(lamTb, 0, [[64, 128], [0, 9], [1, 64]]))
        mark("esb_dA", tV)

        @step("scalar", waits=("esb_dA",), sem=tS)
        def _(e):
            return e.activation(sbL[:, :], scr[:, 0:576], AF.Exp)
        mark("sbL", tS)

        @step("scalar", waits=("lam2",), sem=tS)
        def _(e):
            return e.activation(p16bd[:, 0:64], k16[:, 0:64], AF.Exp,
                                scale=lam2[:, 0:1])
        mark("p16bd", tS)

        for dg in range(NDC):
            w = ("lams", "ld_ct", "lamTb", "p16bd") if dg == 0 else (f"k16ev{dg-1}",)

            @step("tensor", waits=w, sem=tT)
            def _(e, dg=dg):
                return e.matmul(pbc[:, 0:16],
                                ctrep[0:64, dg * 128:dg * 128 + 128],
                                p16v[0:64, :], start=True, stop=True)
            mark(f"k16mm{dg}", tT)

            @step("vector", waits=(f"k16mm{dg}",), sem=tV)
            def _(e, dg=dg):
                return e.tensor_copy(k16[:, dg * 16:dg * 16 + 16],
                                     pbc[:, 0:16])
            mark(f"k16ev{dg}", tV)

        @step("vector", waits=(f"k16ev{NDC-1}", "ld_params"), sem=tV)
        def _(e):
            return e.tensor_add(
                bass.AP(k16, 0, [[128, 128], [16, 8], [1, 1]]),
                bass.AP(k16, 0, [[128, 128], [16, 8], [1, 1]]),
                dvecv[:, 0:8])
        mark("k16d", tV)

        s_lx = [SEM(f"s_lx{i}") for i in range(3)]
        s_lx2 = [SEM(f"s_lx2{i}") for i in range(2)]
        s_xw = [SEM(f"s_xw{i}") for i in range(2)]

        def xtile(h, ti, dc):
            return bass.AP(h, (dc * 128) * NT + ti * TT, [[NT, 128], [1, TT]])

        def emit_outer_ln(tag, src, dst, src_ready, src_is_b16=False):
            hbm = src[0] == "hbm"
            sh = src[1]
            for ti in range(NTT):
                for dc in range(NDC):
                    idx = ti * NDC + dc
                    if hbm:
                        wl = [src_ready] if idx == 0 else []
                        if idx >= 3:
                            wl.append(f"{tag}_cast{(idx-3)//NDC}_{(idx-3)%NDC}")

                        @step("sync", waits=tuple(wl), sem=s_lx[idx % 3],
                              k=16)
                        def _(e, ti=ti, dc=dc, idx=idx):
                            return e.dma_start(out=xr[idx % 3][:, :],
                                               in_=xtile(sh, ti, dc))
                        mark(f"{tag}_xl{ti}_{dc}", s_lx[idx % 3])
                        wl = [f"{tag}_xl{ti}_{dc}"]
                        if idx >= 2:
                            wl.append(f"{tag}_mm{(idx-2)//NDC}_{(idx-2)%NDC}")
                        xb = xbt[dc % 2]

                        @step("scalar", waits=tuple(wl), sem=tS)
                        def _(e, idx=idx, xb=xb):
                            return e.activation(xb[:, :], xr[idx % 3][:, :],
                                                AF.Copy)
                        mark(f"{tag}_cast{ti}_{dc}", tS)
                        xin = xb[:, :]
                        wv = [f"{tag}_cast{ti}_{dc}"]
                    elif not src_is_b16:
                        sl = sh[:, dc * NT + ti * TT:dc * NT + ti * TT + TT]
                        wl = [src_ready]
                        if idx >= 2:
                            wl.append(f"{tag}_mm{(idx-2)//NDC}_{(idx-2)%NDC}")
                        xb = xbt[dc % 2]

                        @step("scalar", waits=tuple(wl), sem=tS)
                        def _(e, sl=sl, xb=xb):
                            return e.activation(xb[:, :], sl, AF.Copy)
                        mark(f"{tag}_cast{ti}_{dc}", tS)
                        xin = xb[:, :]
                        wv = [f"{tag}_cast{ti}_{dc}"]
                    else:
                        xin = sh[:, dc * NT + ti * TT:dc * NT + ti * TT + TT]
                        wv = [src_ready]
                        if idx >= 2:
                            wv.append(f"{tag}_mm{(idx-2)//NDC}_{(idx-2)%NDC}")
                    sq = sqb2[dc % 2]

                    @step("vector", waits=tuple(wv), sem=tV)
                    def _(e, xin=xin, sq=sq):
                        return e.tensor_mul(sq[:, :], xin, xin)
                    mark(f"{tag}_sq{ti}_{dc}", tV)

                    @step("tensor", waits=(f"{tag}_sq{ti}_{dc}",), sem=tT)
                    def _(e, xin=xin, sq=sq, dc=dc):
                        e.matmul(prow[0:1, :], onesc[:, 0:1], xin,
                                 start=(dc == 0), stop=(dc == NDC - 1))
                        return e.matmul(pbc[0:1, :], onesc[:, 0:1], sq[:, :],
                                        start=(dc == 0), stop=(dc == NDC - 1))
                    mark(f"{tag}_mm{ti}_{dc}", tT)

                @step("scalar", waits=(f"{tag}_mm{ti}_{NDC-1}",), sem=tS)
                def _(e):
                    return e.activation(rowA[0:1, :], prow[0:1, :], AF.Copy,
                                        scale=1.0 / D)
                mark(f"{tag}_mu{ti}", tS)

                @step("vector", waits=(f"{tag}_mu{ti}",), sem=tV)
                def _(e):
                    chain(e, e.scalar_tensor_tensor(
                        rowB[0:1, :], rowA[0:1, :], -1.0,
                        rowA[0:1, :], OP.mult, OP.mult))
                    return e.scalar_tensor_tensor(
                        rowB[0:1, :], pbc[0:1, :], 1.0 / D, rowB[0:1, :],
                        OP.mult, OP.add)
                mark(f"{tag}_var{ti}", tV)

                @step("scalar", waits=(f"{tag}_var{ti}",), sem=tS)
                def _(e):
                    return e.activation(rowB[0:1, :], rowB[0:1, :], AF.Ln,
                                        bias=EPS)
                mark(f"{tag}_lnr{ti}", tS)

                @step("scalar", waits=(f"{tag}_lnr{ti}",), sem=tS)
                def _(e):
                    return e.activation(rowC[0:1, :], rowB[0:1, :], AF.Exp,
                                        scale=-0.5)
                mark(f"{tag}_rstd{ti}", tS)

                @step("vector", waits=(f"{tag}_rstd{ti}",), sem=tV)
                def _(e):
                    chain(e, e.tensor_mul(rowB[0:1, :], rowA[0:1, :],
                                          rowC[0:1, :]))
                    chain(e, e.tensor_scalar(rowB[0:1, :], rowB[0:1, :],
                                             -1.0, None, OP.mult))
                    return e.tensor_copy(rowD[0:1, :], rowB[0:1, :])
                mark(f"{tag}_arow{ti}", tV)

                @step("tensor", waits=(f"{tag}_arow{ti}",), sem=tT)
                def _(e):
                    e.matmul(pbc[:, :], onesr[0:1, :], rowC[0:1, :],
                             start=True, stop=True)
                    return e.matmul(pwB[:, 0:512], onesr[0:1, :],
                                    rowD[0:1, :], start=True, stop=True)
                mark(f"{tag}_bc{ti}", tT)

                @step("vector", waits=(f"{tag}_bc{ti}",), sem=tV)
                def _(e, ti=ti):
                    e.tensor_copy(rBs[ti % 2][:, :], pbc[:, :])
                    return e.tensor_copy(aBs[ti % 2][:, :], pwB[:, 0:512])
                mark(f"{tag}_bcs{ti}", tV)

                for dc in range(NDC):
                    idx = ti * NDC + dc
                    dl = dst[:, dc * NT + ti * TT:dc * NT + ti * TT + TT]
                    t1 = t1s[dc % 2]
                    if hbm:
                        wl = []
                        if idx >= 2:
                            wl.append(f"{tag}_ap{(idx-2)//NDC}_{(idx-2)%NDC}")

                        @step("sync", waits=tuple(wl),
                              sem=s_lx2[idx % 2], k=16)
                        def _(e, ti=ti, dc=dc, idx=idx):
                            return e.dma_start(out=xr2[idx % 2][:, :],
                                               in_=xtile(sh, ti, dc))
                        mark(f"{tag}_xl2{ti}_{dc}", s_lx2[idx % 2])

                        @step("vector", waits=(f"{tag}_bcs{ti}",
                                               f"{tag}_xl2{ti}_{dc}"), sem=tV)
                        def _(e, idx=idx, dl=dl, t1=t1, ti=ti):
                            chain(e, e.tensor_mul(t1[:, :],
                                                  xr2[idx % 2][:, :],
                                                  rBs[ti % 2][:, :]))
                            return e.tensor_add(dl, t1[:, :],
                                                aBs[ti % 2][:, :])
                        mark(f"{tag}_ap{ti}_{dc}", tV)
                    else:
                        sl = sh[:, dc * NT + ti * TT:dc * NT + ti * TT + TT]

                        @step("vector", waits=(f"{tag}_bcs{ti}",), sem=tV)
                        def _(e, sl=sl, dl=dl, t1=t1, ti=ti):
                            chain(e, e.tensor_mul(t1[:, :], sl,
                                                  rBs[ti % 2][:, :]))
                            return e.tensor_add(dl, t1[:, :],
                                                aBs[ti % 2][:, :])
                        mark(f"{tag}_ap{ti}_{dc}", tV)
            mark(f"{tag}_done", tV)

        def emit_update(tag, in_h, out_h2, ready, kind):
            for ti in range(NTT):
                for dc in range(NDC):
                    idx = ti * NDC + dc
                    wl = [ready] if (idx == 0 and ready) else []
                    if idx >= 3:
                        wl.append(f"{tag}_v{(idx-3)//NDC}_{(idx-3)%NDC}")

                    @step("sync", waits=tuple(wl), sem=s_lx[idx % 3], k=16)
                    def _(e, ti=ti, dc=dc, idx=idx):
                        return e.dma_start(out=xr[idx % 3][:, :],
                                           in_=xtile(in_h, ti, dc))
                    mark(f"{tag}_l{ti}_{dc}", s_lx[idx % 3])

                    wl = [f"{tag}_l{ti}_{dc}"]
                    if kind == "gb":
                        wl.append(f"lnS_ap{ti}_{dc}")
                    if idx >= 2:
                        wl.append(f"{tag}_s{(idx-2)//NDC}_{(idx-2)%NDC}")

                    @step("vector", waits=tuple(wl), sem=tV)
                    def _(e, ti=ti, dc=dc, idx=idx):
                        sl = bigA[:, dc * NT + ti * TT:dc * NT + ti * TT + TT]
                        o = xw[idx % 2]
                        if kind == "half":
                            return e.scalar_tensor_tensor(
                                o[:, :], sl, 0.5, xr[idx % 3][:, :],
                                OP.mult, OP.add)
                        chain(e, e.scalar_tensor_tensor(
                            o[:, :], sl, gamv[:, dc:dc + 1],
                            xr[idx % 3][:, :], OP.mult, OP.add))
                        return e.tensor_scalar(o[:, :], o[:, :],
                                               betv[:, dc:dc + 1], None,
                                               OP.add)
                    mark(f"{tag}_v{ti}_{dc}", tV)

                    @step("sync", waits=(f"{tag}_v{ti}_{dc}",),
                          sem=s_xw[idx % 2], k=16)
                    def _(e, ti=ti, dc=dc, idx=idx):
                        return e.dma_start(out=xtile(out_h2, ti, dc),
                                           in_=xw[idx % 2][:, :])
                    mark(f"{tag}_s{ti}_{dc}", s_xw[idx % 2])
            @step("sync", waits=(f"{tag}_s{NTT-1}_{NDC-2}",
                                 f"{tag}_s{NTT-1}_{NDC-1}"))
            def _(e):
                return None
            P.cp[f"{tag}_done"] = P.cp[f"{tag}_s{NTT-1}_{NDC-1}"]
            P.cp[f"{tag}_done2"] = P.cp[f"{tag}_s{NTT-1}_{NDC-2}"]

        def emit_prep_layer(l):
            buf = wsb[l % 2]

            qg_prev = s_qg.n
            _lw = ["ld_params"]
            if l > 0:
                _lw.append(f"prep_lora{l-1}_{NDC-1}")

            @step("gpsimd", waits=tuple(_lw))
            def _(e, l=l, qg_prev=qg_prev):
                e.wait_ge(s_qg.h, qg_prev)
                e.dma_start(out=lab[:, :],
                            in_=la_h.ap()[l]).then_inc(s_qg.h, 16)
                e.dma_start(out=lbb[:, :],
                            in_=lbT_h.ap()[l]).then_inc(s_qg.h, 16)
                return None
            s_qg.n += 32
            mark(f"lora_ld{l}", s_qg)
            for dc in range(NDC):
                qbb, scbb = qb_sl[dc % 2], scb_sl[dc % 2]
                wl = []
                if dc >= 2:
                    wl.append(f"prep_asm{l}_{dc-2}")
                elif l > 0:
                    wl.append(f"prep_asm{l-1}_{NDC-2+dc}")

                qg_prev = s_qg.n

                @step("gpsimd", waits=tuple(wl))
                def _(e, l=l, dc=dc, qbb=qbb, scbb=scbb, qg_prev=qg_prev):
                    e.wait_ge(s_qg.h, qg_prev)
                    e.dma_start(out=qbb,
                                in_=qT_h.ap()[l, dc * 128:dc * 128 + 128, :]
                                ).then_inc(s_qg.h, 16)
                    e.dma_start(
                        out=scbb,
                        in_=scT_h.ap()[l, dc * 128:dc * 128 + 128, :]
                        ).then_inc(s_qg.h, 16)
                    return None
                s_qg.n += 32
                mark(f"prep_ld{l}_{dc}", s_qg)

                wl = [f"lora_ld{l}"]
                if dc >= 1:
                    wl.append(f"prep_asm{l}_{dc-1}")
                elif l > 0:
                    wl.append(f"prep_asm{l-1}_{NDC-1}")

                @step("tensor", waits=tuple(wl), sem=tT)
                def _(e, dc=dc):
                    e.matmul(pwA[:, 0:512], lab[:, dc * 128:dc * 128 + 128],
                             lbb[:, 0:512], start=True, stop=True)
                    return e.matmul(pwA[:, 512:1024],
                                    lab[:, dc * 128:dc * 128 + 128],
                                    lbb[:, 512:1024], start=True, stop=True)
                mark(f"prep_lora{l}_{dc}", tT)

                @step("scalar", waits=(f"prep_ld{l}_{dc}",), sem=tS)
                def _(e, qbb=qbb):
                    return e.activation(qbb, qbb, AF.Copy,
                                        scale=1.0 / 7.5, bias=-1.0)
                mark(f"prep_qf{l}_{dc}", tS)

                wl = [f"prep_qf{l}_{dc}", f"prep_lora{l}_{dc}"]
                if l >= 2 and dc == 0:
                    wl.append(f"ffn_wfree{l-2}")
                    wl.append(f"wscr{l-2}")

                @step("vector", waits=tuple(wl), sem=tV)
                def _(e, l=l, dc=dc, buf=buf, qbb=qbb, scbb=scbb):
                    chain(e, e.tensor_mul(qbb, qbb, scbb))
                    sl = buf[:, dc * 1024:dc * 1024 + 1024]
                    last = e.tensor_add(sl, qbb, pwA[:, :])
                    if l % 2 == 0:
                        chain(e, last)
                        chain(e, e.tensor_reduce(wrow[:, :], sl, AX.X,
                                                 OP.add))
                        chain(e, e.tensor_scalar(wrow[:, :], wrow[:, :],
                                                 1.0 / D, None, OP.mult))
                        last = e.tensor_scalar(sl, sl, wrow[:, 0:1], None,
                                               OP.subtract)
                    return last
                mark(f"prep_asm{l}_{dc}", tV)

            if l % 2 == 0:
                @step("scalar", waits=("ld_params",), sem=tS)
                def _(e, l=l):
                    return e.activation(bb16[:, :], biasv[:, l * 8:l * 8 + 8],
                                        AF.Copy)
                mark(f"prep_bcc{l}", tS)

                @step("tensor", waits=(f"prep_bcc{l}", "setup_g"), sem=tT)
                def _(e):
                    return e.matmul(pwB[0:1, 512:520], onesc[:, 0:1],
                                    bb16[:, :], start=True, stop=True)
                mark(f"prep_bcm{l}", tT)

                @step("vector", waits=(f"prep_bcm{l}",), sem=tV)
                def _(e):
                    chain(e, e.tensor_reduce(rowA[0:1, 0:1],
                                             pwB[0:1, 512:520], AX.X,
                                             OP.add))
                    chain(e, e.tensor_scalar(rowA[0:1, 0:1], rowA[0:1, 0:1],
                                             1.0 / D, None, OP.mult))
                    return e.tensor_copy(rowD[0:1, 0:1], rowA[0:1, 0:1])
                mark(f"prep_bcr{l}", tV)

                @step("tensor", waits=(f"prep_bcr{l}",), sem=tT)
                def _(e):
                    return e.matmul(pwB[:, 520:521], onesr[0:1, :],
                                    rowD[0:1, 0:1], start=True, stop=True)
                mark(f"prep_bcb{l}", tT)

                @step("vector", waits=(f"prep_bcb{l}",), sem=tV)
                def _(e, l=l):
                    chain(e, e.tensor_copy(bmean[:, :], pwB[:, 520:521]))
                    return e.tensor_scalar(biasv[:, l * 8:l * 8 + 8],
                                           biasv[:, l * 8:l * 8 + 8],
                                           bmean[:, 0:1], None, OP.subtract)
                mark(f"prep_bce{l}", tV)
                P.cp[f"wready{l}_A"] = P.cp[f"prep_bce{l}"]
            else:
                P.cp[f"wready{l}_A"] = P.cp[f"prep_asm{l}_{NDC-1}"]

            sw_prev = s_w.n

            @step("sync", waits=(f"prep_asm{l}_{NDC-1}",), sem=s_w, k=16)
            def _(e, l=l, buf=buf, sw_prev=sw_prev):
                e.wait_ge(s_w.h, sw_prev)
                return e.dma_start(
                    out=bass.AP(wscr, l * D * D,
                                [[D, 128], [128 * D, 8], [1, D]]),
                    in_=buf[:, :])
            mark(f"wscr{l}", s_w)

        def emit_stream_layer(l):
            buf = wsb[l % 2]
            wl = [f"wscr{l}", f"ffn_wfree{l+4}" if l < 2 else
                  f"ffnC_wfree{l-2}"]

            sw_prev = s_w.n

            @step("sync", waits=tuple(wl), sem=s_w, k=16)
            def _(e, l=l, buf=buf, sw_prev=sw_prev):
                e.wait_ge(s_w.h, sw_prev)
                return e.dma_start(
                    out=buf[:, :],
                    in_=bass.AP(wscr, l * D * D,
                                [[D, 128], [128 * D, 8], [1, D]]))
            mark(f"wready{l}_C", s_w)

        def emit_ffn(call, hook=None):
            tag = f"f{call}"
            for blk in range(3):
                i0, i1 = 2 * blk, 2 * blk + 1
                for ti in range(NTT):
                    for oc in range(NDC):
                        idx = ti * NDC + oc
                        bank = pmm[idx % 2]
                        wl = [f"wready{i0}_{call}"]
                        if blk == 0:
                            wl.append(f"{tag}_hready{ti}")
                        else:
                            wl.append(f"{tag}{blk-1}_h{ti}")
                        if oc >= 2:
                            wl.append(f"{tag}{blk}_e1_{idx-2}")
                        elif ti >= 1:
                            wl.append(f"{tag}{blk}_e2_{(ti-1)*NDC+6+oc}")
                        elif blk > 0:
                            wl.append(f"{tag}{blk-1}_e2_{NTT*NDC-2+oc}")
                        if oc == 0 and ti >= 2:
                            wl.append(f"{tag}{blk}_gin{ti-2}")

                        @step("tensor", waits=tuple(wl), sem=tT)
                        def _(e, i0=i0, ti=ti, oc=oc, bank=bank):
                            buf = wsb[i0 % 2]
                            last = None
                            for dc in range(NDC):
                                last = e.matmul(
                                    bank[:, :],
                                    buf[:, dc * 1024 + oc * 128:
                                        dc * 1024 + oc * 128 + 128],
                                    bigA[:, dc * NT + ti * TT:
                                         dc * NT + ti * TT + TT],
                                    start=(dc == 0), stop=(dc == NDC - 1))
                            return last
                        mark(f"{tag}{blk}_m1_{idx}", tT)

                        @step("scalar", waits=(f"{tag}{blk}_m1_{idx}",),
                              sem=tS)
                        def _(e, i0=i0, ti=ti, oc=oc, bank=bank):
                            return e.activation(
                                y1c(ti)[:, oc * TT:oc * TT + TT], bank[:, :],
                                AF.Identity,
                                bias=biasv[:, i0 * 8 + oc:i0 * 8 + oc + 1])
                        mark(f"{tag}{blk}_e1_{idx}", tS)

                    wl = [f"{tag}{blk}_e1_{ti*NDC+NDC-1}"]
                    if ti > 0:
                        wl.append(f"{tag}{blk}_m2_{(ti-1)*NDC+NDC-1}")
                    elif blk > 0:
                        wl.append(f"{tag}{blk-1}_m2_{NTT*NDC-1}")

                    @step("vector", waits=tuple(wl), sem=tV)
                    def _(e, ti=ti):
                        return e.tensor_mul(gb_sl, y1c(ti), y1c(ti))
                    mark(f"{tag}{blk}_sq{ti}", tV)

                    @step("tensor", waits=(f"{tag}{blk}_sq{ti}",), sem=tT)
                    def _(e):
                        last = None
                        for gc in range(NDC):
                            last = e.matmul(prow[0:1, :], onesc[:, 0:1],
                                            gb_sl[:, gc * TT:gc * TT + TT],
                                            start=(gc == 0),
                                            stop=(gc == NDC - 1))
                        return last
                    mark(f"{tag}{blk}_smm{ti}", tT)

                    @step("scalar", waits=(f"{tag}{blk}_smm{ti}",), sem=tS)
                    def _(e):
                        return e.activation(rowB[0:1, :], prow[0:1, :],
                                            AF.Ln, scale=1.0 / D, bias=EPS)
                    mark(f"{tag}{blk}_lnr{ti}", tS)

                    @step("scalar", waits=(f"{tag}{blk}_lnr{ti}",), sem=tS)
                    def _(e):
                        return e.activation(rowC[0:1, :], rowB[0:1, :],
                                            AF.Exp, scale=-0.5)
                    mark(f"{tag}{blk}_rst{ti}", tS)

                    @step("tensor", waits=(f"{tag}{blk}_rst{ti}",), sem=tT)
                    def _(e):
                        return e.matmul(pbc[:, :], onesr[0:1, :],
                                        rowC[0:1, :], start=True, stop=True)
                    mark(f"{tag}{blk}_rbc{ti}", tT)

                    @step("vector", waits=(f"{tag}{blk}_rbc{ti}",), sem=tV)
                    def _(e, ti=ti):
                        return e.tensor_copy(rstdB[ti % 2][:, :], pbc[:, :])
                    mark(f"{tag}{blk}_rcp{ti}", tV)

                    @step("vector", waits=(f"{tag}{blk}_rcp{ti}",), sem=tV)
                    def _(e, ti=ti):
                        return e.tensor_mul(
                            gb_sl, y1c(ti),
                            bass.AP(rstdB[ti % 2], 0,
                                    [[TT, 128], [0, NDC], [1, TT]]))
                    mark(f"{tag}{blk}_gin{ti}", tV)

                    for oc in range(NDC):
                        @step("scalar", waits=(f"{tag}{blk}_gin{ti}",),
                              sem=tS)
                        def _(e, blk=blk, oc=oc):
                            sl = gb_sl[:, oc * TT:oc * TT + TT]
                            return e.activation(
                                sl, sl, AF.Gelu,
                                scale=lngv[:, blk * 8 + oc:blk * 8 + oc + 1],
                                bias=lnbv[:, blk * 8 + oc:blk * 8 + oc + 1])
                        mark(f"{tag}{blk}_gel{ti}_{oc}", tS)

                    for oc in range(NDC):
                        idx = ti * NDC + oc
                        bank = pmm[idx % 2]
                        wl = [f"wready{i1}_{call}",
                              f"{tag}{blk}_gel{ti}_{NDC-1}"]
                        if idx >= 2:
                            wl.append(f"{tag}{blk}_e2_{idx-2}")
                        else:
                            wl.append(f"{tag}{blk}_e1_{ti*NDC+6+idx}")

                        @step("tensor", waits=tuple(wl), sem=tT)
                        def _(e, i1=i1, oc=oc, bank=bank):
                            buf = wsb[i1 % 2]
                            last = None
                            for gc in range(NDC):
                                last = e.matmul(
                                    bank[:, :],
                                    buf[:, gc * 1024 + oc * 128:
                                        gc * 1024 + oc * 128 + 128],
                                    gb_sl[:, gc * TT:gc * TT + TT],
                                    start=(gc == 0), stop=(gc == NDC - 1))
                            return last
                        mark(f"{tag}{blk}_m2_{idx}", tT)

                        @step("vector", waits=(f"{tag}{blk}_m2_{idx}",),
                              sem=tV)
                        def _(e, i1=i1, ti=ti, oc=oc, bank=bank):
                            hl = bigA[:, oc * NT + ti * TT:
                                      oc * NT + ti * TT + TT]
                            return e.scalar_tensor_tensor(
                                hl, bank[:, :],
                                biasv[:, i1 * 8 + oc:i1 * 8 + oc + 1],
                                hl, OP.add, OP.add)
                        mark(f"{tag}{blk}_e2_{idx}", tV)
                    mark(f"{tag}{blk}_h{ti}", tV)
                key = "ffn_wfree" if call == "A" else "ffnC_wfree"
                P.cp[f"{key}{i0}"] = P.cp[f"{tag}{blk}_m1_{NTT*NDC-1}"]
                P.cp[f"{key}{i1}"] = P.cp[f"{tag}{blk}_m2_{NTT*NDC-1}"]
                if hook:
                    hook(blk)
                if stop_after == f"{tag}{blk}":
                    P.stop()
                    break
            mark(f"{tag}_done", tV)

        emit_prep_layer(0)
        emit_outer_ln("ln1", ("hbm", u_h), bigA, "ld_params")
        for ti in range(NTT):
            P.cp[f"fA_hready{ti}"] = P.cp[f"ln1_ap{ti}_{NDC-1}"]
        if stop_after == "ln1":
            P.stop()
        emit_prep_layer(1)

        def hookA(blk):
            if blk == 0:
                emit_prep_layer(2)
                emit_prep_layer(3)
            elif blk == 1:
                emit_prep_layer(4)
                emit_prep_layer(5)
        emit_ffn("A", hookA)

        emit_update("x1", u_h, xscr, "fA_done", "half")
        if stop_after == "x1":
            P.stop()

        emit_outer_ln("ln2", ("hbm", xscr), bigA, "x1_done")

        if stop_after == "ln2":
            P.stop()

        @step("vector", waits=("ln2_done",), sem=tV)
        def _(e):
            e.memset(sig[0:64, :], 0.0)
            return e.memset(sinit[0:64, :], 0.0)
        mark("sig0", tV)

        for a in range(NA):
            ti = (a * TA) // TT
            wl = [f"ln2_ap{ti}_{dg}" for dg in range(NDC)]
            if a >= 2:
                wl.append(f"sbmm{a-2}")

            @step("sync", waits=tuple(wl))
            def _(e, a=a):
                for dg in range(NDC):
                    e.dma_start(
                        out=utm[a % 2][:, dg * 128:dg * 128 + 128],
                        in_=bigA[:, dg * NT + a * TA:dg * NT + a * TA + TA],
                        transpose=True).then_inc(s_xb[a % 2].h, 16)
                return None
            s_xb[a % 2].n += 16 * NDC
            mark(f"xbar{a}", s_xb[a % 2])

            @step("vector", waits=(("sig0",) if a == 0 else
                                   (f"scan{a-1}", f"pfl{a-1}")), sem=tV)
            def _(e):
                return e.tensor_copy(sigb[0:64, :], sig[0:64, :])
            mark(f"sigb{a}", tV)

            def emit_sbp(a, p):
                pw = (pwA, pwB)[p % 2]
                wl = [f"xbar{a}", "sbL"]
                if a == 0:
                    wl.append("ln2_done")
                if p >= 2:
                    wl.append(f"cvev{a}_{p-2}")
                elif a > 0:
                    wl.append(f"cvev{a-1}_{4-p}")

                @step("tensor", waits=tuple(wl), sem=tT)
                def _(e, a=a, p=p, pw=pw):
                    if p < 4:
                        lhs = sbL[:, p * 128:p * 128 + 128]
                        e.matmul(pw[:, 0:512], lhs, utm[a % 2][:, 0:512],
                                 start=True, stop=True)
                        return e.matmul(pw[:, 512:1024], lhs,
                                        utm[a % 2][:, 512:1024],
                                        start=True, stop=True)
                    lhs = sbL[:, 512:576]
                    e.matmul(pw[0:64, 0:512], lhs, utm[a % 2][:, 0:512],
                             start=True, stop=True)
                    return e.matmul(pw[0:64, 512:1024], lhs,
                                    utm[a % 2][:, 512:1024],
                                    start=True, stop=True)
                mark(f"sbp{a}_{p}", tT)

                wl = [f"sbp{a}_{p}", "ld_ct"]
                if p < 4:
                    if p >= 2:
                        wl.append(f"ptau{a}_{p-2}")
                    elif a > 0:
                        wl.append(f"ptau{a-1}_{p+2}")

                    @step("vector", waits=tuple(wl), sem=tV)
                    def _(e, p=p, pw=pw):
                        return e.tensor_mul(
                            cv2[:, (p % 2) * 1024:(p % 2) * 1024 + 1024],
                            pw[:, :], ctrep[:, :])
                    mark(f"cvev{a}_{p}", tV)
                else:
                    @step("vector", waits=tuple(wl), sem=tV)
                    def _(e, pw=pw):
                        return e.tensor_mul(cvqa[0:64, :], pw[0:64, :],
                                            ctrep[0:64, :])
                    mark(f"cvev{a}_{p}", tV)

            def emit_pfl(a):
                wl = [f"sigb{a}", "lams"]
                if a >= 1:
                    wl.append(f"yfe{a-1}")

                @step("tensor", waits=tuple(wl), sem=tT)
                def _(e):
                    e.matmul(pmm[0][:, :], p128L[0:64, :], sigb[0:64, 0:512],
                             start=True, stop=False, skip_group_check=True)
                    return e.matmul(pmm[1][:, :], p128L[0:64, :],
                                    sigb[0:64, 512:1024], start=True,
                                    stop=False, skip_group_check=True)
                mark(f"pfl{a}", tT)

            def emit_ptau(a, p):
                @step("tensor", waits=(f"cvev{a}_{p}", "p16bd"), sem=tT)
                def _(e, p=p):
                    lhs = p16bd[:, 0:32] if p == 0 else p16bd[:, 32:64]
                    cvs = cv2[:, (p % 2) * 1024:(p % 2) * 1024 + 1024]
                    e.matmul(pmm[0][32 * p:32 * p + 32, :], lhs,
                             cvs[:, 0:512], start=False, stop=True,
                             tile_position=(0, 32 * p),
                             skip_group_check=True)
                    return e.matmul(pmm[1][32 * p:32 * p + 32, :], lhs,
                                    cvs[:, 512:1024], start=False,
                                    stop=True,
                                    tile_position=(0, 32 * p),
                                    skip_group_check=True)
                mark(f"ptau{a}_{p}", tT)

            emit_sbp(a, 0)
            emit_sbp(a, 1)
            emit_pfl(a)
            emit_ptau(a, 0)
            emit_sbp(a, 2)
            emit_ptau(a, 1)
            emit_sbp(a, 3)
            emit_ptau(a, 2)
            emit_sbp(a, 4)
            emit_ptau(a, 3)
            mark(f"sbmm{a}", tT)

            @step("vector", waits=(f"cvev{a}_4", f"sigb{a}"), sem=tV)
            def _(e):
                return e.scalar_tensor_tensor(sig[0:64, :], sig[0:64, :],
                                              lam128[0:64, 0:1],
                                              cvqa[0:64, :], OP.mult, OP.add)
            mark(f"scan{a}", tV)

            @step("scalar", waits=(f"ptau{a}_3",), sem=tS)
            def _(e, a=a):
                e.activation(bigB[:, a * 1024:a * 1024 + 512], pmm[0][:, :],
                             AF.Copy)
                return e.activation(bigB[:, a * 1024 + 512:a * 1024 + 1024],
                                    pmm[1][:, :], AF.Copy)
            mark(f"yfe{a}", tS)

        if stop_after == "pass1":
            P.stop()
        @step("gpsimd", waits=(f"scan{NA-1}",), sem=s_pg, k=16)
        def _(e):
            return e.dma_start(out=cc_in.ap(), in_=sig[0:64, :])
        mark("ccin", s_pg)

        @step("gpsimd", waits=("ccin",), sem=s_cc, k=1)
        def _(e):
            return e.collective_compute(
                "AllGather", OP.bypass,
                replica_groups=[[0, 1, 2, 3], [4, 5, 6, 7]],
                ins=[cc_in.ap().opt()], outs=[cc_out.ap().opt()])
        mark("cc", s_cc)

        for m in range(3):
            @step("gpsimd", waits=(("cc",) if m == 0 else (f"sacc{m-1}",)),
                  sem=s_pg, k=16)
            def _(e, m=m):
                return e.dma_start(out=cvqa[0:64, :],
                                   in_=cc_out.ap()[m * 64:m * 64 + 64, :])
            mark(f"peer{m}", s_pg)

            @step("vector", waits=(f"peer{m}", "coefs"), sem=tV)
            def _(e, m=m):
                return e.scalar_tensor_tensor(
                    sinit[0:64, :], cvqa[0:64, :],
                    coefs[0:64, m:m + 1], sinit[0:64, :], OP.mult, OP.add)
            mark(f"sacc{m}", tV)

        @step("vector", waits=("sacc2",), sem=tV)
        def _(e):
            return e.tensor_copy(sinitb[0:64, :], sinit[0:64, :])
        mark("sinitb", tV)

        for a in range(NA):
            pg_prev = s_pg.n

            @step("gpsimd", waits=(("ld_params",) if a == 0 else
                                   (f"pgexp{a-1}",)), sem=s_pg, k=16)
            def _(e, a=a, pg_prev=pg_prev):
                if a == 0:
                    e.wait_ge(s_pg.h, pg_prev)
                return e.dma_start(
                    out=eglt[:, :],
                    in_=bass.AP(eglob_h, a * 128, [[NT, 64], [1, 128]]))
            mark(f"pgld{a}", s_pg)

            _w = [f"pgld{a}", "lamv", f"yfe{a}"]
            if a >= 1:
                _w.append(f"pgmm{a-1}")

            @step("scalar", waits=tuple(_w), sem=tS)
            def _(e):
                return e.activation(pga[0:64, :], eglt[0:64, :], AF.Exp,
                                    scale=lamv[0:64, :])
            mark(f"pgexp{a}", tS)

            wl = [f"pgexp{a}", "sinitb"]
            if a >= 2:
                wl.append(f"gfe{a-2}")

            @step("tensor", waits=tuple(wl), sem=tT)
            def _(e, a=a):
                e.matmul(pmm[a % 2][:, :], pga[0:64, :],
                         sinitb[0:64, 0:512], start=True, stop=True)
                pw = (pwA, pwB)[a % 2]
                return e.matmul(pw[:, 0:512], pga[0:64, :],
                                sinitb[0:64, 512:1024], start=True, stop=True)
            mark(f"pgmm{a}", tT)

            @step("vector", waits=(f"pgmm{a}",), sem=tV)
            def _(e, a=a):
                sl = bigB[:, a * 1024:a * 1024 + 512]
                e.scalar_tensor_tensor(sl, pmm[a % 2][:, :], 1.0, sl,
                                       OP.mult, OP.add)
                s2 = bigB[:, a * 1024 + 512:a * 1024 + 1024]
                pw = (pwA, pwB)[a % 2]
                return e.scalar_tensor_tensor(s2, pw[:, 0:512], 1.0, s2,
                                              OP.mult, OP.add)
            mark(f"gfe{a}", tV)
        mark("yfar_done", tV)

        if stop_after == "yfar":
            P.stop()
        for dg in range(NDC):
            wl = ["k16d", "ld_params"]
            if dg > 0:
                wl.append(f"tapm{dg-1}")

            @step("vector", waits=tuple(wl), sem=tV)
            def _(e, dg=dg):
                last = None
                for k in range(TB):
                    last = e.tensor_scalar(
                        tdiag[:, k * 128:k * 128 + 128], ident[:, :],
                        k16[:, dg * 16 + k:dg * 16 + k + 1], None, OP.mult)
                return last
            mark(f"tdiag{dg}", tV)

            wl = [f"tdiag{dg}", "yfar_done"]
            if dg > 0:
                wl.append(f"mrg{dg-1}_3")

            @step("tensor", waits=tuple(wl), sem=tT)
            def _(e, dg=dg):
                for k in range(TB):
                    lhs = tdiag[:, k * 128:k * 128 + 128]
                    for w in range(4):
                        pw = (pwA, pwB)[w // 2]
                        col0 = (w % 2) * 512
                        out = pw[:, col0 + 32 * k:col0 + 512]
                        rhs = bass.AP(
                            bigA, dg * NT + w * 512,
                            [[NDC * NT, 128], [1, TB - k], [TB, 32]])
                        e.matmul(out, lhs, rhs, start=(k == 0), stop=False)
                last = None
                for w in range(4):
                    pw = (pwA, pwB)[w // 2]
                    col0 = (w % 2) * 512
                    last = e.matmul(pw[:, col0:col0 + 512], zrow[0:1, :],
                                    zrow512[0:1, :], start=False, stop=True)
                return last
            mark(f"tapm{dg}", tT)

            for w in range(4):
                wl = [f"tapm{dg}"]
                if w >= 2:
                    wl.append(f"mrg{dg}_{w-2}")
                elif dg > 0:
                    wl.append(f"mrg{dg-1}_{w+2}")

                @step("tensor", waits=tuple(wl), sem=tT)
                def _(e, dg=dg, w=w):
                    last = None
                    for j in range(4):
                        a = 4 * w + j
                        last = e.matmul(
                            pmm[w % 2][:, j * 128:j * 128 + 128],
                            bigB[:, a * 1024 + dg * 128:
                                 a * 1024 + dg * 128 + 128],
                            ident[:, :], start=True, stop=True)
                    return last
                mark(f"trans{dg}_{w}", tT)

                @step("vector", waits=(f"trans{dg}_{w}",), sem=tV)
                def _(e, dg=dg, w=w):
                    pw = (pwA, pwB)[w // 2]
                    col0 = (w % 2) * 512
                    t1 = t1s[w % 2]
                    chain(e, e.tensor_scalar(t1[:, :], pmm[w % 2][:, :],
                                             fav[:, dg:dg + 1], None,
                                             OP.mult))
                    unp = bass.AP(pw, col0, [[1024, 128], [1, 32], [32, 16]])
                    return e.scalar_tensor_tensor(
                        bigA[:, dg * NT + w * 512:dg * NT + w * 512 + 512],
                        unp, fav[:, dg:dg + 1], t1[:, :], OP.mult, OP.add)
                mark(f"mrg{dg}_{w}", tV)
        mark("y2_done", tV)

        @step("vector", waits=("y2_done", "coefs"), sem=tV)
        def _(e):
            last = None
            for dg in range(NDC):
                last = e.tensor_add(bigA[:, dg * NT:dg * NT + 1],
                                    bigA[:, dg * NT:dg * NT + 1],
                                    rw2[:, dg:dg + 1])
            return last
        mark("rw_done", tV)

        if stop_after == "y2":
            P.stop()
        emit_outer_ln("lnS", ("sbuf", bigA), bigA, "rw_done",
                      src_is_b16=True)
        emit_update("x2", xscr, xscr, "lnS_done", "gb")

        if stop_after == "x2":
            P.stop()
        emit_stream_layer(0)
        emit_stream_layer(1)
        emit_outer_ln("ln3", ("hbm", xscr), bigA, "x2_done")
        for ti in range(NTT):
            P.cp[f"fC_hready{ti}"] = P.cp[f"ln3_ap{ti}_{NDC-1}"]

        def hookC(blk):
            if blk == 0:
                emit_stream_layer(2)
                emit_stream_layer(3)
            elif blk == 1:
                emit_stream_layer(4)
                emit_stream_layer(5)
        emit_ffn("C", hookC)

        emit_update("fout", xscr, out_h, "fC_done", "half")

        @step("sync")
        def _(e):
            e.wait_ge(s_xw[0].h, s_xw[0].n)
            e.wait_ge(s_xw[1].h, s_xw[1].n)
            return None

        with nc.Block() as blk:
            for eng in ENGS:
                def make(eng):
                    def body(e):
                        for fn in P.ops[eng]:
                            fn(e)
                    return body
                getattr(blk, eng)(make(eng))

    return nc


def _host_constants():
    c = {}
    c["c_ident"] = np.eye(128, dtype=BFnp)
    BIG = 1e9
    esb = np.zeros((128, 576), np.float32)
    s = np.arange(128)
    for p in range(4):
        for js in range(2):
            j = 2 * p + js
            col = p * 128 + js * 64
            ecol = np.where(s < 16 * j, (16.0 * j - s), BIG)
            esb[:, col:col + 64] = ecol[:, None]
    esb[:, 512:576] = (128.0 - s)[:, None]
    c["c_esb"] = esb
    etau = np.zeros((128, 64), np.float32)
    r = np.arange(128)
    for ci in range(32):
        jsub = ci // 16
        tau = float(ci % 16)
        onblk = (r // 64) == jsub
        etau[:, ci] = np.where(onblk & (jsub != 0), tau, BIG)
        etau[:, 32 + ci] = np.where(onblk, tau, BIG)
    c["c_etau"] = etau
    c["c_eglob"] = np.broadcast_to(np.arange(NT, dtype=np.float32)[None, :],
                                   (NST, NT)).copy()
    c["c_e16"] = np.broadcast_to(np.arange(16, dtype=np.float32)[None, :],
                                 (NST, 16)).copy()
    return c


def _vec_tile(v):
    return np.ascontiguousarray(v.reshape(8, 128).T)


def kernel(**inputs):
    u = np.asarray(inputs["L0_attn_u"], np.float32)
    qw = np.asarray(inputs["L0_ffn_qweights"], np.int32)
    scales = np.asarray(inputs["L0_ffn_scales"], np.float32)
    biases = np.asarray(inputs["L0_ffn_biases"], np.float32)
    lora_a = np.asarray(inputs["L0_ffn_lora_a"], np.float32)
    lora_b = np.asarray(inputs["L0_ffn_lora_b"], np.float32)
    ln_g = np.asarray(inputs["L0_ffn_ln_gamma"], np.float32)
    ln_b = np.asarray(inputs["L0_ffn_ln_beta"], np.float32)
    A_log = np.asarray(inputs["L0_attn_A_log"], np.float32)
    C = np.asarray(inputs["L0_attn_C"], np.float32)
    Dv = np.asarray(inputs["L0_attn_D"], np.float32)
    delta = np.asarray(inputs["L0_attn_delta"], np.float32)
    gamma = np.asarray(inputs["L0_attn_gamma"], np.float32)
    beta = np.asarray(inputs["L0_attn_beta"], np.float32)
    fa = np.asarray(inputs["L0_attn_freq_align"], np.float32)
    reward = np.asarray(inputs["L0_attn_reward"], np.float32)

    shared = {
        "qT": np.ascontiguousarray(qw.transpose(0, 2, 1)),
        "scT": np.ascontiguousarray(
            np.repeat(scales, 16, axis=2).transpose(0, 2, 1)),
        "la": np.ascontiguousarray(lora_a),
        "lbT": np.ascontiguousarray(lora_b.transpose(0, 2, 1)),
        "bias_t": np.concatenate([_vec_tile(biases[l]) for l in range(6)],
                                 axis=1),
        "lng_t": np.concatenate([_vec_tile(ln_g[i]) for i in range(3)],
                                axis=1),
        "lnb_t": np.concatenate([_vec_tile(ln_b[i]) for i in range(3)],
                                axis=1),
        "d_t": _vec_tile(Dv),
        "gamma_t": _vec_tile(gamma),
        "beta_t": _vec_tile(beta),
        "fa_t": _vec_tile(fa),
        "rw_t": _vec_tile(reward),
        "CT": np.ascontiguousarray(C.T),
        "A_log_t": np.ascontiguousarray(A_log.reshape(NST, 1)),
        "delta0": np.ascontiguousarray(
            np.broadcast_to(delta[0:1], (NST,)).reshape(NST, 1)),
    }
    shared.update(_host_constants())

    nc = build_nc()
    in_maps = []
    for core in range(8):
        b, seg = core // 4, core % 4
        m = dict(shared)
        m["u_fm"] = np.ascontiguousarray(u[b, seg * NT:(seg + 1) * NT, :].T)
        m["is_first"] = np.full((128, 1), 1.0 if seg == 0 else 0.0,
                                np.float32)
        pmk = np.zeros((NST, 4), np.float32)
        e2048 = np.zeros((NST, 4), np.float32)
        for p in range(3):
            if p < seg:
                pmk[:, p] = 1.0
                e2048[:, p] = 2048.0 * (seg - 1 - p)
        m["peer_mask"] = pmk
        m["c_e2048"] = e2048
        in_maps.append(m)

    res = run_bass_kernel_spmd(nc, in_maps, core_ids=list(range(8)))
    out = np.empty((2, 8192, D), np.float32)
    for core in range(8):
        b, seg = core // 4, core % 4
        out[b, seg * NT:(seg + 1) * NT, :] = \
            np.asarray(res.results[core]["out_fm"]).T
    return out


# version 05
# speedup vs baseline: 1.1576x; 1.1576x over previous
import numpy as np
import ml_dtypes
import concourse.bass as bass
import concourse.mybir as mybir
from concourse.bass_utils import run_bass_kernel_spmd

F32 = mybir.dt.float32
BF16 = mybir.dt.bfloat16
I32 = mybir.dt.int32
AF = mybir.ActivationFunctionType
OP = mybir.AluOpType
AX = mybir.AxisListType
BFnp = ml_dtypes.bfloat16

D = 1024
NT = 2048
NDC = 8
TT = 512
NTT = 4
TB = 16
TA = 128
NA = 16
NST = 64
EPS = 1e-5
CONSTS = [EPS, -1.0, 1.0 / 7.5, -0.5, 1.0 / D, 0.5, 1.0, 128.0, 0.0]

ENGS = ("sync", "gpsimd", "scalar", "vector", "tensor")


class Ctr:
    def __init__(self, h):
        self.h = h
        self.n = 0


class _CP(dict):
    stopped = False

    def __missing__(self, key):
        if self.stopped:
            return (None, 0)
        raise KeyError(key)


class Prog:
    def __init__(self):
        self.ops = {e: [] for e in ENGS}
        self.cp = _CP()
        self.stopped = False

    def stop(self):
        self.stopped = True
        self.cp.stopped = True

    def step(self, eng, waits=(), sem=None, k=1):
        if self.stopped:
            return lambda fn: None
        wl = []
        for w in waits:
            ctr, v = self.cp[w]
            if v > 0 and ctr is not None:
                wl.append((ctr.h, v))
        if sem is not None:
            sem.n += k

        def deco(fn):
            def run(e):
                for h, v in wl:
                    e.wait_ge(h, v)
                inst = fn(e)
                if sem is not None:
                    assert inst is not None
                    inst.then_inc(sem.h, k)
            self.ops[eng].append(run)
            return None
        return deco

    def mark(self, name, ctr):
        if not self.stopped:
            self.cp[name] = (ctr, ctr.n)


def build_nc(debug=(), stop_after=None):
    nc = bass.Bass("TRN2", target_bir_lowering=False)

    u_h = nc.dram_tensor("u_fm", [D, NT], F32, kind="ExternalInput")
    qT_h = nc.dram_tensor("qT", [6, D, D], I32, kind="ExternalInput")
    scT_h = nc.dram_tensor("scT", [6, D, D], F32, kind="ExternalInput")
    la_h = nc.dram_tensor("la", [6, 32, D], F32, kind="ExternalInput")
    lbT_h = nc.dram_tensor("lbT", [6, 32, D], F32, kind="ExternalInput")
    bias_h = nc.dram_tensor("bias_t", [128, 48], F32, kind="ExternalInput")
    lng_h = nc.dram_tensor("lng_t", [128, 24], F32, kind="ExternalInput")
    lnb_h = nc.dram_tensor("lnb_t", [128, 24], F32, kind="ExternalInput")
    dvec_h = nc.dram_tensor("d_t", [128, 8], F32, kind="ExternalInput")
    gam_h = nc.dram_tensor("gamma_t", [128, 8], F32, kind="ExternalInput")
    bet_h = nc.dram_tensor("beta_t", [128, 8], F32, kind="ExternalInput")
    fa_h = nc.dram_tensor("fa_t", [128, 8], F32, kind="ExternalInput")
    rw_h = nc.dram_tensor("rw_t", [128, 8], F32, kind="ExternalInput")
    ct_h = nc.dram_tensor("CT", [NST, D], F32, kind="ExternalInput")
    alog_h = nc.dram_tensor("A_log_t", [NST, 1], F32, kind="ExternalInput")
    del0_h = nc.dram_tensor("delta0", [NST, 1], F32, kind="ExternalInput")
    isf_h = nc.dram_tensor("is_first", [128, 1], F32, kind="ExternalInput")
    pm_h = nc.dram_tensor("peer_mask", [NST, 4], F32, kind="ExternalInput")
    ident_h = nc.dram_tensor("c_ident", [128, 128], BF16, kind="ExternalInput")
    esb_h = nc.dram_tensor("c_esb", [128, 576], F32, kind="ExternalInput")
    etau_h = nc.dram_tensor("c_etau", [128, 64], F32, kind="ExternalInput")
    eglob_h = nc.dram_tensor("c_eglob", [NST, NT], F32, kind="ExternalInput")
    e16_h = nc.dram_tensor("c_e16", [NST, 16], F32, kind="ExternalInput")
    e2048_h = nc.dram_tensor("c_e2048", [NST, 4], F32, kind="ExternalInput")

    out_h = nc.dram_tensor("out_fm", [D, NT], F32, kind="ExternalOutput")
    dbg = {}
    for name, shape, dt in debug:
        dbg[name] = nc.dram_tensor(name, shape, dt, kind="ExternalOutput")

    wscr = nc.dram_tensor("wscr", [6, D, D], BF16, kind="Internal")
    xscr = nc.dram_tensor("xscr", [D, NT], F32, kind="Internal")
    lamT_d = nc.dram_tensor("lamT_d", [NST, 1], F32, kind="Internal")
    cc_in = nc.dram_tensor("cc_in", [NST, D], F32, kind="Internal")
    cc_out = nc.dram_tensor("cc_out", [4 * NST, D], F32, kind="Internal")

    from contextlib import ExitStack
    ctx = ExitStack()
    P = Prog()
    step, mark = P.step, P.mark

    def SB(name, shape, dtype):
        return ctx.enter_context(nc.sbuf_tensor(name, shape, dtype))

    def PST(name, shape, dtype=F32):
        return ctx.enter_context(nc.psum_tensor(name, shape, dtype))

    def SEM(name):
        return Ctr(ctx.enter_context(nc.semaphore(name)))

    with ctx:
        bigA = SB("bigA", [128, NDC * NT], BF16)
        bigB = SB("bigB", [128, NDC * NT], BF16)
        wsb = [SB(f"wsb{i}", [128, 8192], BF16) for i in range(2)]
        ident = SB("ident", [128, 128], BF16)
        onesc = SB("onesc", [128, 1], BF16)
        onesr = SB("onesr", [1, 128], BF16)
        zrow = SB("zrow", [1, 128], BF16)
        zrow512 = SB("zrow512", [1, 512], BF16)
        cpool = SB("cpool", [128, len(CONSTS)], F32)
        for i, v in enumerate(CONSTS):
            nc.const_aps.register(F32, v, cpool[:, i:i + 1])
        biasv = SB("biasv", [128, 48], F32)
        lngv = SB("lngv", [128, 24], F32)
        lnbv = SB("lnbv", [128, 24], F32)
        dvecv = SB("dvecv", [128, 8], F32)
        gamv = SB("gamv", [128, 8], F32)
        betv = SB("betv", [128, 8], F32)
        fav = SB("fav", [128, 8], F32)
        rw2 = SB("rw2", [128, 8], F32)
        isf = SB("isf", [128, 1], F32)
        ctrep = SB("ctrep", [128, D], BF16)
        lamv = SB("lamv", [NST, 1], F32)
        lam2 = SB("lam2", [128, 1], F32)
        lam128 = SB("lam128", [NST, 1], F32)
        coefs = SB("coefs", [NST, 4], F32)
        sbL = SB("sbL", [128, 576], BF16)
        p16bd = SB("p16bd", [128, 64], BF16)
        p128L = SB("p128L", [NST, 128], BF16)
        pga = SB("pga", [NST, 128], BF16)
        eglt = SB("eglt", [NST, 128], F32)
        p16v = SB("p16v", [NST, 16], BF16)
        k16 = SB("k16", [128, NDC * 16], F32)
        tdiag = SB("tdiag", [128, 8 * 128], BF16)
        xr = [SB(f"xr{i}", [128, TT], F32) for i in range(3)]
        xr2 = [SB(f"xr2{i}", [128, TT], F32) for i in range(2)]
        xw = [SB(f"xw{i}", [128, TT], F32) for i in range(2)]
        tmp64 = SB("tmp64", [NST, 32], F32)
        sig = SB("sig", [NST, D], F32)
        sigb = SB("sigb", [NST, D], BF16)
        sinit = SB("sinit", [NST, D], F32)
        sinitb = SB("sinitb", [NST, D], BF16)
        cvqa = SB("cvqa", [NST, D], BF16)
        cv2 = SB("cv2", [128, 2 * D], BF16)
        utm = [SB(f"utm{i}", [128, D], BF16) for i in range(3)]
        pglob = SB("pglob", [NST, NT], BF16)
        rowR16 = SB("rowR16", [1, NT], BF16)
        rowA16 = SB("rowA16", [1, NT], BF16)
        rowS = SB("rowS", [1, NT], F32)[:, :]
        rowV = SB("rowV", [1, NT], F32)[:, :]
        rowQ = SB("rowQ", [1, NT], F32)[:, :]
        rowB = SB("rowB", [1, TT], F32)
        rowC = SB("rowC", [1, TT], BF16)
        aBs = [SB(f"aBs{i}", [128, TT], BF16) for i in range(2)]
        rBs = [SB(f"rBs{i}", [128, TT], BF16) for i in range(2)]
        t1s = [SB(f"t1s{i}", [128, TT], F32) for i in range(2)]
        xbt = [SB(f"xbt{i}", [128, TT], BF16) for i in range(2)]
        sqb2 = [SB(f"sqb{i}", [128, TT], BF16) for i in range(2)]
        rstdB = [SB(f"rstdB{i}", [128, TT], BF16) for i in range(2)]
        gbodd = SB("gbodd", [128, 4096], BF16)
        lab = [SB(f"lab{i}", [32, 128], BF16) for i in range(2)]
        lbb = SB("lbb", [32, D], BF16)
        wrow = SB("wrow", [128, 1], F32)
        bb16 = SB("bb16", [128, 8], BF16)
        bmean = SB("bmean", [128, 1], F32)
        lamT16 = SB("lamT16", [1, NST], F32)
        lamT16b = SB("lamT16b", [1, NST], BF16)
        lamTb = SB("lamTb", [128, NST], F32)

        def y1c(ti):
            o = 0 if ti % 2 == 0 else 8192
            return bigB[:, o:o + 4096]
        gb_sl = bigB[:, 4096:8192]
        qb_sl = [bigB[:, 12288:13312], bigB[:, 13312:14336]]
        scb_sl = [bigB[:, 14336:15360], bigB[:, 15360:16384]]

        pmm = [PST(f"pmm{i}", [128, TT], F32) for i in range(2)]
        prow = PST("prow", [1, TT], F32)
        pbc = PST("pbc", [128, TT], F32)
        pwA = PST("pwA", [128, 1024], F32)
        pwB = PST("pwB", [128, 1024], F32)

        s_in = SEM("s_in")
        s_w = SEM("s_w")
        s_xb = [SEM(f"s_xb{i}") for i in range(3)]
        s_st = SEM("s_st")
        s_qg = SEM("s_qg")
        s_qg2 = [SEM(f"s_qg2{i}") for i in range(2)]
        s_pg = SEM("s_pg")
        s_cc = SEM("s_cc")
        s_pgl = SEM("s_pgl")
        s_esb = SEM("s_esb")
        tS = SEM("tS")
        chV = SEM("chV")
        chS = SEM("chS")

        def chain(e, inst, ch=None):
            ch = ch or chV
            ch.n += 1
            inst.then_inc(ch.h, 1)
            e.wait_ge(ch.h, ch.n)
        tT = SEM("tT")
        tV = SEM("tV")
        tG = SEM("tG")

        _p0 = [(bias_h.ap(), biasv[:, 0:48]), (lng_h.ap(), lngv[:, 0:24]),
               (lnb_h.ap(), lnbv[:, 0:24]), (gam_h.ap(), gamv[:, 0:8]),
               (bet_h.ap(), betv[:, 0:8]), (fa_h.ap(), fav[:, 0:8]),
               (rw_h.ap(), rw2[:, 0:8]), (isf_h.ap(), isf[:, 0:1]),
               (dvec_h.ap(), dvecv[:, 0:8]), (ident_h.ap(), ident[:, :]),

               (alog_h.ap(), tmp64[:, 0:1]), (del0_h.ap(), tmp64[:, 1:2]),
               (pm_h.ap(), coefs[:, :]), (e16_h.ap(), tmp64[:, 4:20]),
               (e2048_h.ap(), tmp64[:, 20:24]), (etau_h.ap(), k16[:, 0:64]),
               (bass.AP(eglob_h, 0, [[NT, 64], [1, 128]]), eglt[:, :])]

        @step("sync")
        def _(e):
            for src, dst in _p0:
                e.dma_start(out=dst, in_=src).then_inc(s_in.h, 16)
            return None
        s_in.n += 16 * len(_p0)
        mark("ld_params", s_in)

        @step("gpsimd")
        def _(e):
            e.dma_start(out=ctrep[0:64, :], in_=ct_h.ap()).then_inc(s_qg.h, 16)
            e.dma_start(out=ctrep[64:128, :],
                        in_=ct_h.ap()).then_inc(s_qg.h, 16)
            return None
        s_qg.n += 32
        mark("ld_ct", s_qg)

        @step("gpsimd")
        def _(e):
            e.memset(onesc[:, :], 1.0).then_inc(tG.h, 1)
            e.memset(onesr[:, :], 1.0).then_inc(tG.h, 1)
            e.memset(zrow[:, :], 0.0).then_inc(tG.h, 1)
            e.memset(zrow512[:, :], 0.0).then_inc(tG.h, 1)
            for i, v in enumerate(CONSTS):
                e.memset(cpool[:, i:i + 1], v).then_inc(tG.h, 1)
            return None
        tG.n += 4 + len(CONSTS)
        mark("setup_g", tG)

        @step("scalar", waits=("ld_params", "setup_g"), sem=tS)
        def _(e):
            return e.activation(lamv[0:64, :], tmp64[0:64, 0:1], AF.Exp)
        mark("expA", tS)

        @step("vector", waits=("expA",), sem=tV)
        def _(e):
            chain(e, e.tensor_mul(lamv[0:64, :], lamv[0:64, :],
                                  tmp64[0:64, 1:2]))
            return e.tensor_scalar_mul(lamv[0:64, :], lamv[0:64, :], -1.0)
        mark("lamv", tV)

        @step("vector", waits=("lamv",), sem=tV)
        def _(e):
            e.tensor_copy(lam2[0:64, :], lamv[0:64, :])
            return e.tensor_copy(lam2[64:128, :], lamv[0:64, :])
        mark("lam2", tV)

        @step("scalar", waits=("lamv",), sem=tS)
        def _(e):
            e.activation(lam128[0:64, :], lamv[0:64, :], AF.Exp, scale=128.0)
            e.activation(p16v[0:64, :], tmp64[0:64, 4:20], AF.Exp,
                         scale=lamv[0:64, :])
            e.activation(p128L[0:64, :], eglt[0:64, :], AF.Exp,
                         scale=lamv[0:64, :])
            return e.activation(tmp64[0:64, 28:32], tmp64[0:64, 20:24],
                                AF.Exp, scale=lamv[0:64, :])
        mark("lams", tS)

        @step("vector", waits=("lams",), sem=tV)
        def _(e):
            e.tensor_mul(coefs[0:64, :], coefs[0:64, :], tmp64[0:64, 28:32])
            return e.tensor_scalar_mul(rw2[:, 0:8], rw2[:, 0:8], isf[:, 0:1])
        mark("coefs", tV)

        for half in range(2):
            wl = ["lams"] if half == 0 else ("pgl0",)

            @step("sync", waits=tuple(wl) if half else ("ld_params",),
                  sem=s_pgl, k=16)
            def _(e, half=half):
                return e.dma_start(
                    out=sig[0:64, :],
                    in_=eglob_h.ap()[:, half * 1024:half * 1024 + 1024])
            mark(f"pgld{half}", s_pgl)

            @step("scalar", waits=(f"pgld{half}", "lams"), sem=tS)
            def _(e, half=half):
                return e.activation(pglob[0:64, half * 1024:half * 1024 + 1024],
                                    sig[0:64, :], AF.Exp,
                                    scale=lamv[0:64, :])
            mark(f"pgl{half}", tS)
        mark("pglob_done", tS)

        s_pg_snap = s_pg.n

        @step("gpsimd", waits=("lamv",))
        def _(e):
            e.dma_start(out=lamT_d.ap(),
                        in_=lamv[0:64, :]).then_inc(s_pg.h, 16)
            e.wait_ge(s_pg.h, s_pg_snap + 16)
            e.dma_start(out=lamT16[0:1, 0:64],
                        in_=bass.AP(lamT_d, 0, [[64, 1], [1, 64]])
                        ).then_inc(s_pg.h, 16)
            return None
        s_pg.n += 32
        mark("lamT", s_pg)

        @step("vector", waits=("lamT",), sem=tV)
        def _(e):
            return e.tensor_copy(lamT16b[0:1, :], lamT16[0:1, 0:64])
        mark("lamT16b", tV)

        @step("tensor", waits=("lamT16b", "setup_g", "ld_params"), sem=tT)
        def _(e):
            return e.matmul(pbc[:, 0:64], onesr[0:1, :],
                            lamT16b[0:1, :], start=True, stop=True)
        mark("dAbc", tT)

        @step("vector", waits=("dAbc",), sem=tV)
        def _(e):
            return e.tensor_copy(lamTb[:, :], pbc[:, 0:64])
        mark("lamTb", tV)

        for c in range(2):
            cw, coff = (512, 0) if c == 0 else (64, 512)

            @step("sync", waits=(("lamTb",) if c == 0 else (f"esbx{0}",)),
                  sem=s_esb, k=16)
            def _(e, cw=cw, coff=coff):
                return e.dma_start(out=xr[2][:, 0:cw],
                                   in_=esb_h.ap()[:, coff:coff + cw])
            mark(f"esbl{c}", s_esb)

            @step("vector", waits=(f"esbl{c}", "lamTb"), sem=tV)
            def _(e, cw=cw, coff=coff):
                return e.tensor_mul(
                    xr[2][:, 0:cw], xr[2][:, 0:cw],
                    bass.AP(lamTb, coff % 64, [[64, 128], [0, 9], [1, 64]])
                    if False else
                    bass.AP(lamTb, 0, [[64, 128], [0, (cw + 63) // 64],
                                       [1, 64]]))
            mark(f"esbm{c}", tV)

            @step("scalar", waits=(f"esbm{c}",), sem=tS)
            def _(e, cw=cw, coff=coff):
                return e.activation(sbL[:, coff:coff + cw], xr[2][:, 0:cw],
                                    AF.Exp)
            mark(f"esbx{c}", tS)
        P.cp["sbL"] = P.cp["esbx1"]

        @step("scalar", waits=("lam2",), sem=tS)
        def _(e):
            return e.activation(p16bd[:, 0:64], k16[:, 0:64], AF.Exp,
                                scale=lam2[:, 0:1])
        mark("p16bd", tS)

        for dg in range(NDC):
            w = ("lams", "ld_ct", "lamTb", "p16bd") if dg == 0 else (f"k16ev{dg-1}",)

            @step("tensor", waits=w, sem=tT)
            def _(e, dg=dg):
                return e.matmul(pbc[:, 0:16],
                                ctrep[0:64, dg * 128:dg * 128 + 128],
                                p16v[0:64, :], start=True, stop=True)
            mark(f"k16mm{dg}", tT)

            @step("vector", waits=(f"k16mm{dg}",), sem=tV)
            def _(e, dg=dg):
                return e.tensor_copy(k16[:, dg * 16:dg * 16 + 16],
                                     pbc[:, 0:16])
            mark(f"k16ev{dg}", tV)

        @step("vector", waits=(f"k16ev{NDC-1}", "ld_params"), sem=tV)
        def _(e):
            return e.tensor_add(
                bass.AP(k16, 0, [[128, 128], [16, 8], [1, 1]]),
                bass.AP(k16, 0, [[128, 128], [16, 8], [1, 1]]),
                dvecv[:, 0:8])
        mark("k16d", tV)

        s_lx = [SEM(f"s_lx{i}") for i in range(3)]
        s_lx2 = [SEM(f"s_lx2{i}") for i in range(2)]
        s_xw = [SEM(f"s_xw{i}") for i in range(2)]

        def xtile(h, ti, dc):
            return bass.AP(h, (dc * 128) * NT + ti * TT, [[NT, 128], [1, TT]])

        def emit_outer_ln(tag, src, dst, src_ready, src_is_b16=False,
                          tile_ready=None):
            hbm = src[0] == "hbm"
            sh = src[1]
            xins = {}
            for ti in range(NTT):
                for dc in range(NDC):
                    idx = ti * NDC + dc
                    if hbm:
                        wl = list(tile_ready(ti, dc)) if tile_ready else []
                        if idx == 0 and src_ready:
                            wl.append(src_ready)
                        if idx >= 3:
                            wl.append(f"{tag}_cast{(idx-3)//NDC}_{(idx-3)%NDC}")

                        @step("sync", waits=tuple(wl), sem=s_lx[idx % 3],
                              k=16)
                        def _(e, ti=ti, dc=dc, idx=idx):
                            return e.dma_start(out=xr[idx % 3][:, :],
                                               in_=xtile(sh, ti, dc))
                        mark(f"{tag}_xl{ti}_{dc}", s_lx[idx % 3])
                        wl = [f"{tag}_xl{ti}_{dc}"]
                        if idx >= 2:
                            wl.append(f"{tag}_mm{(idx-2)//NDC}_{(idx-2)%NDC}")
                        xb = xbt[dc % 2]

                        @step("scalar", waits=tuple(wl), sem=tS)
                        def _(e, idx=idx, xb=xb):
                            return e.activation(xb[:, :], xr[idx % 3][:, :],
                                                AF.Copy)
                        mark(f"{tag}_cast{ti}_{dc}", tS)
                        xin = xb[:, :]
                        wv = [f"{tag}_cast{ti}_{dc}"]
                    elif not src_is_b16:
                        sl = sh[:, dc * NT + ti * TT:dc * NT + ti * TT + TT]
                        wl = [src_ready]
                        if idx >= 2:
                            wl.append(f"{tag}_mm{(idx-2)//NDC}_{(idx-2)%NDC}")
                        xb = xbt[dc % 2]

                        @step("scalar", waits=tuple(wl), sem=tS)
                        def _(e, sl=sl, xb=xb):
                            return e.activation(xb[:, :], sl, AF.Copy)
                        mark(f"{tag}_cast{ti}_{dc}", tS)
                        xin = xb[:, :]
                        wv = [f"{tag}_cast{ti}_{dc}"]
                    else:
                        xin = sh[:, dc * NT + ti * TT:dc * NT + ti * TT + TT]
                        wv = [src_ready]
                        if tile_ready:
                            wv += list(tile_ready(ti, dc))
                        if idx >= 2:
                            wv.append(f"{tag}_mm{(idx-2)//NDC}_{(idx-2)%NDC}")
                    xins[(ti, dc)] = xin
                    sq = sqb2[dc % 2]

                    @step("vector", waits=tuple(wv), sem=tV)
                    def _(e, xin=xin, sq=sq):
                        return e.tensor_mul(sq[:, :], xin, xin)
                    mark(f"{tag}_sq{ti}_{dc}", tV)

                    wl = [f"{tag}_sq{ti}_{dc}"]
                    if dc == 0 and ti > 0:
                        wl.append(f"{tag}_rows{ti-1}")

                    @step("tensor", waits=tuple(wl), sem=tT)
                    def _(e, xin=xin, sq=sq, dc=dc):
                        e.matmul(prow[0:1, :], onesc[:, 0:1], xin,
                                 start=(dc == 0), stop=(dc == NDC - 1))
                        return e.matmul(pbc[0:1, :], onesc[:, 0:1], sq[:, :],
                                        start=(dc == 0), stop=(dc == NDC - 1))
                    mark(f"{tag}_mm{ti}_{dc}", tT)

                @step("vector", waits=(f"{tag}_mm{ti}_{NDC-1}",), sem=tV)
                def _(e, ti=ti):
                    e.tensor_copy(rowS[:, ti * TT:ti * TT + TT],
                                  prow[0:1, :])
                    return e.tensor_copy(rowQ[:, ti * TT:ti * TT + TT],
                                         pbc[0:1, :])
                mark(f"{tag}_rows{ti}", tV)

            @step("scalar", waits=(f"{tag}_rows{NTT-1}",), sem=tS)
            def _(e):
                return e.activation(rowS[:, :], rowS[:, :], AF.Copy,
                                    scale=1.0 / D)
            mark(f"{tag}_fmu", tS)

            @step("vector", waits=(f"{tag}_fmu",), sem=tV)
            def _(e):
                chain(e, e.tensor_scalar(rowQ[:, :], rowQ[:, :],
                                         1.0 / D, None, OP.mult))
                chain(e, e.tensor_mul(rowV[:, :], rowS[:, :],
                                      rowS[:, :]))
                return e.tensor_sub(rowQ[:, :], rowQ[:, :],
                                    rowV[:, :])
            mark(f"{tag}_fvar", tV)

            @step("scalar", waits=(f"{tag}_fvar",), sem=tS)
            def _(e):
                return e.activation(rowQ[:, :], rowQ[:, :], AF.Ln,
                                    bias=EPS)
            mark(f"{tag}_fln", tS)

            @step("scalar", waits=(f"{tag}_fln",), sem=tS)
            def _(e):
                return e.activation(rowR16[:, :], rowQ[:, :], AF.Exp,
                                    scale=-0.5)
            mark(f"{tag}_frstd", tS)

            @step("vector", waits=(f"{tag}_frstd",), sem=tV)
            def _(e):
                chain(e, e.tensor_mul(rowS[:, :], rowS[:, :],
                                      rowR16[:, :]))
                return e.tensor_scalar(rowA16[:, :], rowS[:, :], -1.0,
                                       None, OP.mult)
            mark(f"{tag}_farow", tV)

            for ti in range(NTT):
                wl = [f"{tag}_farow"]
                if ti >= 1:
                    wl.append(f"{tag}_bcs{ti-1}")

                @step("tensor", waits=tuple(wl), sem=tT)
                def _(e, ti=ti):
                    e.matmul(pbc[:, :], onesr[0:1, :],
                             rowR16[:, ti * TT:ti * TT + TT],
                             start=True, stop=True)
                    return e.matmul(pwB[:, 0:512], onesr[0:1, :],
                                    rowA16[:, ti * TT:ti * TT + TT],
                                    start=True, stop=True)
                mark(f"{tag}_bc{ti}", tT)

                @step("vector", waits=(f"{tag}_bc{ti}",), sem=tV)
                def _(e, ti=ti):
                    e.tensor_copy(rBs[ti % 2][:, :], pbc[:, :])
                    return e.tensor_copy(aBs[ti % 2][:, :], pwB[:, 0:512])
                mark(f"{tag}_bcs{ti}", tV)

                for dc in range(NDC):
                    idx = ti * NDC + dc
                    dl = dst[:, dc * NT + ti * TT:dc * NT + ti * TT + TT]
                    t1 = t1s[dc % 2]
                    if hbm:
                        wl = []
                        if idx >= 2:
                            wl.append(f"{tag}_ap{(idx-2)//NDC}_{(idx-2)%NDC}")

                        @step("scalar", waits=tuple(wl),
                              sem=s_lx2[idx % 2], k=16)
                        def _(e, ti=ti, dc=dc, idx=idx):
                            return e.dma_start(out=xr2[idx % 2][:, :],
                                               in_=xtile(sh, ti, dc))
                        mark(f"{tag}_xl2{ti}_{dc}", s_lx2[idx % 2])

                        @step("vector", waits=(f"{tag}_bcs{ti}",
                                               f"{tag}_xl2{ti}_{dc}"),
                              sem=tV)
                        def _(e, idx=idx, dl=dl, t1=t1, ti=ti):
                            chain(e, e.tensor_mul(t1[:, :],
                                                  xr2[idx % 2][:, :],
                                                  rBs[ti % 2][:, :]))
                            return e.tensor_add(dl, t1[:, :],
                                                aBs[ti % 2][:, :])
                        mark(f"{tag}_ap{ti}_{dc}", tV)
                    else:
                        xin = xins[(ti, dc)]

                        @step("vector", waits=(f"{tag}_bcs{ti}",), sem=tV)
                        def _(e, xin=xin, dl=dl, t1=t1, ti=ti):
                            chain(e, e.tensor_mul(t1[:, :], xin,
                                                  rBs[ti % 2][:, :]))
                            return e.tensor_add(dl, t1[:, :],
                                                aBs[ti % 2][:, :])
                        mark(f"{tag}_ap{ti}_{dc}", tV)
            mark(f"{tag}_done", tV)

        def emit_update(tag, in_h, out_h2, ready, kind, tile_ready=None):
            for ti in range(NTT):
                for dc in range(NDC):
                    idx = ti * NDC + dc
                    wl = [ready] if (idx == 0 and ready) else []
                    if idx >= 3:
                        wl.append(f"{tag}_v{(idx-3)//NDC}_{(idx-3)%NDC}")

                    @step("sync", waits=tuple(wl), sem=s_lx[idx % 3], k=16)
                    def _(e, ti=ti, dc=dc, idx=idx):
                        return e.dma_start(out=xr[idx % 3][:, :],
                                           in_=xtile(in_h, ti, dc))
                    mark(f"{tag}_l{ti}_{dc}", s_lx[idx % 3])

                    wl = [f"{tag}_l{ti}_{dc}"]
                    if kind == "gb":
                        wl.append(f"lnS_ap{ti}_{dc}")
                    if tile_ready:
                        wl += list(tile_ready(ti, dc))
                    if idx >= 2:
                        wl.append(f"{tag}_s{(idx-2)//NDC}_{(idx-2)%NDC}")

                    @step("vector", waits=tuple(wl), sem=tV)
                    def _(e, ti=ti, dc=dc, idx=idx):
                        sl = bigA[:, dc * NT + ti * TT:dc * NT + ti * TT + TT]
                        o = xw[idx % 2]
                        if kind == "half":
                            return e.scalar_tensor_tensor(
                                o[:, :], sl, 0.5, xr[idx % 3][:, :],
                                OP.mult, OP.add)
                        chain(e, e.scalar_tensor_tensor(
                            o[:, :], sl, gamv[:, dc:dc + 1],
                            xr[idx % 3][:, :], OP.mult, OP.add))
                        return e.tensor_scalar(o[:, :], o[:, :],
                                               betv[:, dc:dc + 1], None,
                                               OP.add)
                    mark(f"{tag}_v{ti}_{dc}", tV)

                    @step("gpsimd", waits=(f"{tag}_v{ti}_{dc}",),
                          sem=s_xw[idx % 2], k=16)
                    def _(e, ti=ti, dc=dc, idx=idx):
                        return e.dma_start(out=xtile(out_h2, ti, dc),
                                           in_=xw[idx % 2][:, :])
                    mark(f"{tag}_s{ti}_{dc}", s_xw[idx % 2])
            @step("sync", waits=(f"{tag}_s{NTT-1}_{NDC-2}",
                                 f"{tag}_s{NTT-1}_{NDC-1}"))
            def _(e):
                return None
            P.cp[f"{tag}_done"] = P.cp[f"{tag}_s{NTT-1}_{NDC-1}"]
            P.cp[f"{tag}_done2"] = P.cp[f"{tag}_s{NTT-1}_{NDC-2}"]

        def emit_prep_layer(l):
            buf = wsb[l % 2]

            qg_prev = s_qg.n
            _lw = ["ld_params"]
            if l > 0:
                _lw.append(f"prep_lora{l-1}_{NDC-1}")

            @step("gpsimd", waits=tuple(_lw))
            def _(e, l=l, qg_prev=qg_prev):
                e.wait_ge(s_qg.h, qg_prev)
                e.dma_start(out=lbb[:, :],
                            in_=lbT_h.ap()[l]).then_inc(s_qg.h, 16)
                return None
            s_qg.n += 16
            mark(f"lora_ld{l}", s_qg)
            for dc in range(NDC):
                qbb, scbb = qb_sl[dc % 2], scb_sl[dc % 2]
                wl = []
                if dc >= 2:
                    wl.append(f"prep_asm{l}_{dc-2}")
                    wl.append(f"prep_lora{l}_{dc-2}")
                elif l > 0:
                    wl.append(f"prep_asm{l-1}_{NDC-2+dc}")
                    wl.append(f"prep_lora{l-1}_{NDC-2+dc}")

                sem2 = s_qg2[dc % 2]

                @step("gpsimd", waits=tuple(wl))
                def _(e, l=l, dc=dc, qbb=qbb, scbb=scbb, sem2=sem2):
                    e.dma_start(out=qbb,
                                in_=qT_h.ap()[l, dc * 128:dc * 128 + 128, :]
                                ).then_inc(sem2.h, 16)
                    e.dma_start(
                        out=scbb,
                        in_=scT_h.ap()[l, dc * 128:dc * 128 + 128, :]
                        ).then_inc(sem2.h, 16)
                    e.dma_start(
                        out=lab[dc % 2][:, :],
                        in_=la_h.ap()[l, :, dc * 128:dc * 128 + 128]
                        ).then_inc(sem2.h, 16)
                    return None
                sem2.n += 48
                mark(f"prep_ld{l}_{dc}", sem2)

                wl = [f"lora_ld{l}", f"prep_ld{l}_{dc}"]
                if dc >= 1:
                    wl.append(f"prep_asm{l}_{dc-1}")
                elif l > 0:
                    wl.append(f"prep_asm{l-1}_{NDC-1}")

                @step("tensor", waits=tuple(wl), sem=tT)
                def _(e, dc=dc):
                    e.matmul(pwA[:, 0:512], lab[dc % 2][:, :],
                             lbb[:, 0:512], start=True, stop=True)
                    return e.matmul(pwA[:, 512:1024],
                                    lab[dc % 2][:, :],
                                    lbb[:, 512:1024], start=True, stop=True)
                mark(f"prep_lora{l}_{dc}", tT)

                @step("scalar", waits=(f"prep_ld{l}_{dc}",), sem=tS)
                def _(e, qbb=qbb):
                    return e.activation(qbb, qbb, AF.Copy,
                                        scale=1.0 / 7.5, bias=-1.0)
                mark(f"prep_qf{l}_{dc}", tS)

                wl = [f"prep_qf{l}_{dc}", f"prep_lora{l}_{dc}"]
                if l >= 2 and dc == 0:
                    wl.append(f"ffn_wfree{l-2}")
                    wl.append(f"wscr{l-2}")

                @step("vector", waits=tuple(wl), sem=tV)
                def _(e, l=l, dc=dc, buf=buf, qbb=qbb, scbb=scbb):
                    chain(e, e.tensor_mul(qbb, qbb, scbb))
                    sl = buf[:, dc * 1024:dc * 1024 + 1024]
                    last = e.tensor_add(sl, qbb, pwA[:, :])
                    if l % 2 == 0:
                        chain(e, last)
                        chain(e, e.tensor_reduce(wrow[:, :], sl, AX.X,
                                                 OP.add))
                        chain(e, e.tensor_scalar(wrow[:, :], wrow[:, :],
                                                 1.0 / D, None, OP.mult))
                        last = e.tensor_scalar(sl, sl, wrow[:, 0:1], None,
                                               OP.subtract)
                    return last
                mark(f"prep_asm{l}_{dc}", tV)

            if l % 2 == 0:
                _bw = ["ld_params"]
                if l >= 2:
                    _bw.append(f"prep_bcb{l-2}")

                @step("scalar", waits=tuple(_bw), sem=tS)
                def _(e, l=l):
                    return e.activation(bb16[:, :], biasv[:, l * 8:l * 8 + 8],
                                        AF.Copy)
                mark(f"prep_bcc{l}", tS)

                @step("tensor", waits=(f"prep_bcc{l}", "setup_g"), sem=tT)
                def _(e):
                    return e.matmul(pwB[0:1, 512:520], onesc[:, 0:1],
                                    bb16[:, :], start=True, stop=True)
                mark(f"prep_bcm{l}", tT)

                @step("vector", waits=(f"prep_bcm{l}",), sem=tV)
                def _(e):
                    chain(e, e.tensor_reduce(tmp64[0:1, 30:31],
                                             pwB[0:1, 512:520], AX.X,
                                             OP.add))
                    chain(e, e.tensor_scalar(tmp64[0:1, 30:31],
                                             tmp64[0:1, 30:31],
                                             1.0 / D, None, OP.mult))
                    return e.tensor_copy(bb16[0:1, 0:1], tmp64[0:1, 30:31])
                mark(f"prep_bcr{l}", tV)

                @step("tensor", waits=(f"prep_bcr{l}",), sem=tT)
                def _(e):
                    return e.matmul(pwB[:, 520:521], onesr[0:1, :],
                                    bb16[0:1, 0:1], start=True, stop=True)
                mark(f"prep_bcb{l}", tT)

                @step("vector", waits=(f"prep_bcb{l}",), sem=tV)
                def _(e, l=l):
                    chain(e, e.tensor_copy(bmean[:, :], pwB[:, 520:521]))
                    return e.tensor_scalar(biasv[:, l * 8:l * 8 + 8],
                                           biasv[:, l * 8:l * 8 + 8],
                                           bmean[:, 0:1], None, OP.subtract)
                mark(f"prep_bce{l}", tV)
                P.cp[f"wready{l}_A"] = P.cp[f"prep_bce{l}"]
            else:
                P.cp[f"wready{l}_A"] = P.cp[f"prep_asm{l}_{NDC-1}"]

            sw_prev = s_w.n

            @step("sync", waits=(f"prep_asm{l}_{NDC-1}",), sem=s_w, k=16)
            def _(e, l=l, buf=buf, sw_prev=sw_prev):
                e.wait_ge(s_w.h, sw_prev)
                return e.dma_start(
                    out=bass.AP(wscr, l * D * D,
                                [[D, 128], [128 * D, 8], [1, D]]),
                    in_=buf[:, :])
            mark(f"wscr{l}", s_w)

        def emit_stream_layer(l):
            buf = wsb[l % 2]
            wl = [f"wscr{l}", f"ffn_wfree{l+4}" if l < 2 else
                  f"ffnC_wfree{l-2}"]

            sw_prev = s_w.n

            @step("sync", waits=tuple(wl), sem=s_w, k=16)
            def _(e, l=l, buf=buf, sw_prev=sw_prev):
                e.wait_ge(s_w.h, sw_prev)
                return e.dma_start(
                    out=buf[:, :],
                    in_=bass.AP(wscr, l * D * D,
                                [[D, 128], [128 * D, 8], [1, D]]))
            mark(f"wready{l}_C", s_w)

        def emit_ffn(call, hook=None):
            tag = f"f{call}"
            m2b = [pwB[:, 0:512], pwB[:, 512:1024]]

            def gb2(ti):
                if ti % 2 == 0:
                    return bigB[:, 4096:8192]
                return gbodd[:, :]

            def emit_mm1(blk, ti):
                i0 = 2 * blk
                for oc in range(NDC):
                    idx = ti * NDC + oc
                    wl = [f"wready{i0}_{call}"]
                    if blk == 0:
                        wl.append(f"{tag}_hready{ti}")
                    else:
                        wl.append(f"{tag}{blk-1}_h{ti}")
                    if oc >= 2:
                        wl.append(f"{tag}{blk}_e1_{idx-2}")
                    elif ti >= 1:
                        wl.append(f"{tag}{blk}_e1_{(ti-1)*NDC+6+oc}")
                    elif blk > 0:
                        wl.append(f"{tag}{blk-1}_e1_{3*NDC+6+oc}")

                    @step("tensor", waits=tuple(wl), sem=tT)
                    def _(e, i0=i0, ti=ti, oc=oc, idx=idx):
                        buf = wsb[i0 % 2]
                        bank = pmm[idx % 2]
                        last = None
                        for dc in range(NDC):
                            last = e.matmul(
                                bank[:, :],
                                buf[:, dc * 1024 + oc * 128:
                                    dc * 1024 + oc * 128 + 128],
                                bigA[:, dc * NT + ti * TT:
                                     dc * NT + ti * TT + TT],
                                start=(dc == 0), stop=(dc == NDC - 1))
                        return last
                    mark(f"{tag}{blk}_m1_{idx}", tT)

                    wl = [f"{tag}{blk}_m1_{idx}"]
                    if oc == 0 and ti >= 2:
                        wl.append(f"{tag}{blk}_gin{ti-2}")
                    elif oc == 0 and blk > 0 and ti < 2:
                        wl.append(f"{tag}{blk-1}_gin{ti+2}")

                    @step("scalar", waits=tuple(wl), sem=tS)
                    def _(e, i0=i0, ti=ti, oc=oc, idx=idx):
                        return e.activation(
                            y1c(ti)[:, oc * TT:oc * TT + TT],
                            pmm[idx % 2][:, :], AF.Identity,
                            bias=biasv[:, i0 * 8 + oc:i0 * 8 + oc + 1])
                    mark(f"{tag}{blk}_e1_{idx}", tS)

            def emit_stats(blk, ti):
                gb = gb2(ti)
                wl = [f"{tag}{blk}_e1_{ti*NDC+NDC-1}"]
                if ti >= 2:
                    wl.append(f"{tag}{blk}_m2_{(ti-2)*NDC+NDC-1}")
                elif blk > 0:
                    wl.append(f"{tag}{blk-1}_m2_{(ti+2)*NDC+NDC-1}")

                @step("vector", waits=tuple(wl), sem=tV)
                def _(e, ti=ti, gb=gb):
                    return e.tensor_mul(gb, y1c(ti), y1c(ti))
                mark(f"{tag}{blk}_sq{ti}", tV)

                @step("tensor", waits=(f"{tag}{blk}_sq{ti}",), sem=tT)
                def _(e, gb=gb):
                    last = None
                    for gc in range(NDC):
                        last = e.matmul(prow[0:1, :], onesc[:, 0:1],
                                        gb[:, gc * TT:gc * TT + TT],
                                        start=(gc == 0),
                                        stop=(gc == NDC - 1))
                    return last
                mark(f"{tag}{blk}_smm{ti}", tT)

                @step("scalar", waits=(f"{tag}{blk}_smm{ti}",), sem=tS)
                def _(e):
                    return e.activation(rowB[0:1, :], prow[0:1, :], AF.Ln,
                                        scale=1.0 / D, bias=EPS)
                mark(f"{tag}{blk}_lnr{ti}", tS)

                @step("scalar", waits=(f"{tag}{blk}_lnr{ti}",), sem=tS)
                def _(e):
                    return e.activation(rowC[0:1, :], rowB[0:1, :],
                                        AF.Exp, scale=-0.5)
                mark(f"{tag}{blk}_rst{ti}", tS)

                @step("tensor", waits=(f"{tag}{blk}_rst{ti}",), sem=tT)
                def _(e):
                    return e.matmul(pbc[:, :], onesr[0:1, :],
                                    rowC[0:1, :], start=True, stop=True)
                mark(f"{tag}{blk}_rbc{ti}", tT)

                @step("vector", waits=(f"{tag}{blk}_rbc{ti}",), sem=tV)
                def _(e, ti=ti):
                    return e.tensor_copy(rstdB[ti % 2][:, :], pbc[:, :])
                mark(f"{tag}{blk}_rcp{ti}", tV)

                @step("vector", waits=(f"{tag}{blk}_rcp{ti}",), sem=tV)
                def _(e, ti=ti, gb=gb):
                    return e.tensor_mul(
                        gb, y1c(ti),
                        bass.AP(rstdB[ti % 2], 0,
                                [[TT, 128], [0, NDC], [1, TT]]))
                mark(f"{tag}{blk}_gin{ti}", tV)

                for oc in range(NDC):
                    @step("scalar", waits=(f"{tag}{blk}_gin{ti}",), sem=tS)
                    def _(e, blk=blk, oc=oc, gb=gb):
                        sl = gb[:, oc * TT:oc * TT + TT]
                        return e.activation(
                            sl, sl, AF.Gelu,
                            scale=lngv[:, blk * 8 + oc:blk * 8 + oc + 1],
                            bias=lnbv[:, blk * 8 + oc:blk * 8 + oc + 1])
                    mark(f"{tag}{blk}_gel{ti}_{oc}", tS)

            def emit_mm2(blk, ti):
                i1 = 2 * blk + 1
                gb = gb2(ti)
                for oc in range(NDC):
                    idx = ti * NDC + oc
                    wl = [f"wready{i1}_{call}",
                          f"{tag}{blk}_gel{ti}_{NDC-1}"]
                    if oc >= 2:
                        wl.append(f"{tag}{blk}_e2_{idx-2}")
                    elif ti >= 1:
                        wl.append(f"{tag}{blk}_e2_{(ti-1)*NDC+6+oc}")
                    elif blk > 0:
                        wl.append(f"{tag}{blk-1}_e2_{3*NDC+6+oc}")

                    @step("tensor", waits=tuple(wl), sem=tT)
                    def _(e, i1=i1, oc=oc, gb=gb):
                        buf = wsb[i1 % 2]
                        last = None
                        for gc in range(NDC):
                            last = e.matmul(
                                m2b[oc % 2],
                                buf[:, gc * 1024 + oc * 128:
                                    gc * 1024 + oc * 128 + 128],
                                gb[:, gc * TT:gc * TT + TT],
                                start=(gc == 0), stop=(gc == NDC - 1))
                        return last
                    mark(f"{tag}{blk}_m2_{idx}", tT)

                    @step("vector", waits=(f"{tag}{blk}_m2_{idx}",), sem=tV)
                    def _(e, i1=i1, ti=ti, oc=oc):
                        hl = bigA[:, oc * NT + ti * TT:oc * NT + ti * TT + TT]
                        return e.scalar_tensor_tensor(
                            hl, m2b[oc % 2],
                            biasv[:, i1 * 8 + oc:i1 * 8 + oc + 1],
                            hl, OP.add, OP.add)
                    mark(f"{tag}{blk}_e2_{idx}", tV)
                mark(f"{tag}{blk}_h{ti}", tV)

            for blk in range(3):
                i0, i1 = 2 * blk, 2 * blk + 1
                emit_mm1(blk, 0)
                emit_mm1(blk, 1)
                emit_stats(blk, 0)
                emit_mm1(blk, 2)
                emit_stats(blk, 1)
                emit_mm1(blk, 3)
                emit_mm2(blk, 0)
                emit_stats(blk, 2)
                emit_mm2(blk, 1)
                emit_stats(blk, 3)
                emit_mm2(blk, 2)
                emit_mm2(blk, 3)
                key = "ffn_wfree" if call == "A" else "ffnC_wfree"
                P.cp[f"{key}{i0}"] = P.cp[f"{tag}{blk}_m1_{NTT*NDC-1}"]
                P.cp[f"{key}{i1}"] = P.cp[f"{tag}{blk}_m2_{NTT*NDC-1}"]
                if hook:
                    hook(blk)
                if stop_after == f"{tag}{blk}":
                    P.stop()
                    break
            mark(f"{tag}_done", tV)

        emit_prep_layer(0)
        emit_outer_ln("ln1", ("hbm", u_h), bigA, "ld_params")
        for ti in range(NTT):
            P.cp[f"fA_hready{ti}"] = P.cp[f"ln1_ap{ti}_{NDC-1}"]
        if stop_after == "ln1":
            P.stop()
        emit_prep_layer(1)

        def hookA(blk):
            if blk == 0:
                emit_prep_layer(2)
                emit_prep_layer(3)
            elif blk == 1:
                emit_prep_layer(4)
                emit_prep_layer(5)
        emit_ffn("A", hookA)

        emit_update("x1", u_h, xscr, None, "half",
                    tile_ready=lambda ti, dc: [f"fA2_e2_{ti*NDC+dc}"])
        if stop_after == "x1":
            P.stop()

        emit_outer_ln("ln2", ("hbm", xscr), bigA, None,
                      tile_ready=lambda ti, dc: [f"x1_s{ti}_{dc}"])

        if stop_after == "ln2":
            P.stop()

        @step("vector", waits=("ln2_done", "pglob_done"), sem=tV)
        def _(e):
            e.memset(sig[0:64, :], 0.0)
            return e.memset(sinit[0:64, :], 0.0)
        mark("sig0", tV)

        for a in range(NA):
            ti = (a * TA) // TT
            wl = [f"ln2_ap{ti}_{dg}" for dg in range(NDC)]
            if a >= 3:
                wl.append(f"sbmm{a-3}")

            @step("sync", waits=tuple(wl))
            def _(e, a=a):
                for dg in range(4):
                    e.dma_start(
                        out=utm[a % 3][:, dg * 128:dg * 128 + 128],
                        in_=bigA[:, dg * NT + a * TA:dg * NT + a * TA + TA],
                        transpose=True).then_inc(s_xb[a % 3].h, 16)
                return None

            @step("scalar", waits=tuple(wl))
            def _(e, a=a):
                for dg in range(4, NDC):
                    e.dma_start(
                        out=utm[a % 3][:, dg * 128:dg * 128 + 128],
                        in_=bigA[:, dg * NT + a * TA:dg * NT + a * TA + TA],
                        transpose=True).then_inc(s_xb[a % 3].h, 16)
                return None
            s_xb[a % 3].n += 16 * NDC
            mark(f"xbar{a}", s_xb[a % 3])

            @step("vector", waits=(("sig0",) if a == 0 else
                                   (f"scan{a-1}", f"pfl{a-1}")), sem=tV)
            def _(e):
                return e.tensor_copy(sigb[0:64, :], sig[0:64, :])
            mark(f"sigb{a}", tV)

            def emit_sbp(a, p):
                pw = (pwA, pwB)[p % 2]
                wl = [f"xbar{a}", "sbL"]
                if a == 0:
                    wl.append("ln2_done")
                if p >= 2:
                    wl.append(f"cvev{a}_{p-2}")
                elif a > 0:
                    wl.append(f"cvev{a-1}_{4-p}")

                @step("tensor", waits=tuple(wl), sem=tT)
                def _(e, a=a, p=p, pw=pw):
                    if p < 4:
                        lhs = sbL[:, p * 128:p * 128 + 128]
                        e.matmul(pw[:, 0:512], lhs, utm[a % 3][:, 0:512],
                                 start=True, stop=True)
                        return e.matmul(pw[:, 512:1024], lhs,
                                        utm[a % 3][:, 512:1024],
                                        start=True, stop=True)
                    lhs = sbL[:, 512:576]
                    e.matmul(pw[0:64, 0:512], lhs, utm[a % 3][:, 0:512],
                             start=True, stop=True)
                    return e.matmul(pw[0:64, 512:1024], lhs,
                                    utm[a % 3][:, 512:1024],
                                    start=True, stop=True)
                mark(f"sbp{a}_{p}", tT)

                wl = [f"sbp{a}_{p}", "ld_ct"]
                if p < 4:
                    if p >= 2:
                        wl.append(f"ptau{a}_{p-2}")
                    elif a > 0:
                        wl.append(f"ptau{a-1}_{p+2}")

                    @step("vector", waits=tuple(wl), sem=tV)
                    def _(e, p=p, pw=pw):
                        return e.tensor_mul(
                            cv2[:, (p % 2) * 1024:(p % 2) * 1024 + 1024],
                            pw[:, :], ctrep[:, :])
                    mark(f"cvev{a}_{p}", tV)
                else:
                    @step("vector", waits=tuple(wl), sem=tV)
                    def _(e, pw=pw):
                        return e.tensor_mul(cvqa[0:64, :], pw[0:64, :],
                                            ctrep[0:64, :])
                    mark(f"cvev{a}_{p}", tV)

            def emit_pfl(a):
                wl = [f"sigb{a}", "lams"]
                if a >= 1:
                    wl.append(f"yfe{a-1}")

                @step("tensor", waits=tuple(wl), sem=tT)
                def _(e):
                    e.matmul(pmm[0][:, :], p128L[0:64, :], sigb[0:64, 0:512],
                             start=True, stop=False, skip_group_check=True)
                    return e.matmul(pmm[1][:, :], p128L[0:64, :],
                                    sigb[0:64, 512:1024], start=True,
                                    stop=False, skip_group_check=True)
                mark(f"pfl{a}", tT)

            def emit_ptau(a, p):
                @step("tensor", waits=(f"cvev{a}_{p}", "p16bd"), sem=tT)
                def _(e, p=p):
                    lhs = p16bd[:, 0:32] if p == 0 else p16bd[:, 32:64]
                    cvs = cv2[:, (p % 2) * 1024:(p % 2) * 1024 + 1024]
                    e.matmul(pmm[0][32 * p:32 * p + 32, :], lhs,
                             cvs[:, 0:512], start=False, stop=True,
                             tile_position=(0, 32 * p),
                             skip_group_check=True)
                    return e.matmul(pmm[1][32 * p:32 * p + 32, :], lhs,
                                    cvs[:, 512:1024], start=False,
                                    stop=True,
                                    tile_position=(0, 32 * p),
                                    skip_group_check=True)
                mark(f"ptau{a}_{p}", tT)

            emit_sbp(a, 0)
            emit_sbp(a, 1)
            emit_pfl(a)
            emit_ptau(a, 0)
            emit_sbp(a, 2)
            emit_ptau(a, 1)
            emit_sbp(a, 3)
            emit_ptau(a, 2)
            emit_sbp(a, 4)
            emit_ptau(a, 3)
            mark(f"sbmm{a}", tT)

            @step("vector", waits=(f"cvev{a}_4", f"sigb{a}"), sem=tV)
            def _(e):
                return e.scalar_tensor_tensor(sig[0:64, :], sig[0:64, :],
                                              lam128[0:64, 0:1],
                                              cvqa[0:64, :], OP.mult, OP.add)
            mark(f"scan{a}", tV)

            @step("scalar", waits=(f"ptau{a}_3",), sem=tS)
            def _(e, a=a):
                e.activation(bigB[:, a * 1024:a * 1024 + 512], pmm[0][:, :],
                             AF.Copy)
                return e.activation(bigB[:, a * 1024 + 512:a * 1024 + 1024],
                                    pmm[1][:, :], AF.Copy)
            mark(f"yfe{a}", tS)

        if stop_after == "pass1":
            P.stop()
        @step("gpsimd", waits=(f"scan{NA-1}",), sem=s_pg, k=16)
        def _(e):
            return e.dma_start(out=cc_in.ap(), in_=sig[0:64, :])
        mark("ccin", s_pg)

        @step("gpsimd", waits=("ccin",), sem=s_cc, k=1)
        def _(e):
            return e.collective_compute(
                "AllGather", OP.bypass,
                replica_groups=[[0, 1, 2, 3], [4, 5, 6, 7]],
                ins=[cc_in.ap().opt()], outs=[cc_out.ap().opt()])
        mark("cc", s_cc)

        for m in range(3):
            @step("gpsimd", waits=(("cc",) if m == 0 else (f"sacc{m-1}",)),
                  sem=s_pg, k=16)
            def _(e, m=m):
                return e.dma_start(out=cvqa[0:64, :],
                                   in_=cc_out.ap()[m * 64:m * 64 + 64, :])
            mark(f"peer{m}", s_pg)

            @step("vector", waits=(f"peer{m}", "coefs"), sem=tV)
            def _(e, m=m):
                return e.scalar_tensor_tensor(
                    sinit[0:64, :], cvqa[0:64, :],
                    coefs[0:64, m:m + 1], sinit[0:64, :], OP.mult, OP.add)
            mark(f"sacc{m}", tV)

        @step("vector", waits=("sacc2",), sem=tV)
        def _(e):
            return e.tensor_copy(sinitb[0:64, :], sinit[0:64, :])
        mark("sinitb", tV)

        for a in range(NA):
            wl = ["lams", "sinitb", f"yfe{NA-1}"]
            if a >= 1:
                wl.append(f"gfe{a-1}")

            @step("tensor", waits=tuple(wl), sem=tT)
            def _(e, a=a):
                e.matmul(pmm[0][:, :], pglob[0:64, a * 128:a * 128 + 128],
                         sinitb[0:64, 0:512], start=True, stop=True)
                return e.matmul(pmm[1][:, :],
                                pglob[0:64, a * 128:a * 128 + 128],
                                sinitb[0:64, 512:1024], start=True, stop=True)
            mark(f"pgmm{a}", tT)

            @step("vector", waits=(f"pgmm{a}",), sem=tV)
            def _(e, a=a):
                sl = bigB[:, a * 1024:a * 1024 + 512]
                e.scalar_tensor_tensor(sl, pmm[0][:, :], 1.0,
                                       sl, OP.mult, OP.add)
                s2 = bigB[:, a * 1024 + 512:a * 1024 + 1024]
                return e.scalar_tensor_tensor(s2, pmm[1][:, :], 1.0, s2,
                                              OP.mult, OP.add)
            mark(f"gfe{a}", tV)
        mark("yfar_done", tV)

        if stop_after == "yfar":
            P.stop()
        def emit_taps(dg):
            for kb in range(2):
                wl = ["k16d", "ld_params"]
                if kb == 1:
                    wl.append(f"tapm{dg}_0")
                elif dg > 0:
                    wl.append(f"tapm{dg-1}_1")

                @step("vector", waits=tuple(wl), sem=tV)
                def _(e, dg=dg, kb=kb):
                    last = None
                    for k8 in range(8):
                        k = kb * 8 + k8
                        last = e.tensor_scalar(
                            tdiag[:, k8 * 128:k8 * 128 + 128], ident[:, :],
                            k16[:, dg * 16 + k:dg * 16 + k + 1], None,
                            OP.mult)
                    return last
                mark(f"tdiag{dg}_{kb}", tV)

                wl = [f"tdiag{dg}_{kb}"]
                if dg == 0 and kb == 0:
                    wl.append(f"cvev{NA-1}_4")
                    wl.append(f"scan{NA-1}")
                if kb == 0 and dg > 0:
                    wl.append(f"mrg{dg-1}_3")

                @step("tensor", waits=tuple(wl), sem=tT)
                def _(e, dg=dg, kb=kb):
                    last = None
                    for k8 in range(8):
                        k = kb * 8 + k8
                        lhs = tdiag[:, k8 * 128:k8 * 128 + 128]
                        for w in range(4):
                            pw = (pwA, pwB)[w // 2]
                            col0 = (w % 2) * 512
                            out = pw[:, col0 + 32 * k:col0 + 512]
                            rhs = bass.AP(
                                bigA, dg * NT + w * 512,
                                [[NDC * NT, 128], [1, TB - k], [TB, 32]])
                            last = e.matmul(out, lhs, rhs, start=(k == 0),
                                            stop=False)
                    if kb == 1:
                        for w in range(4):
                            pw = (pwA, pwB)[w // 2]
                            col0 = (w % 2) * 512
                            last = e.matmul(pw[:, col0:col0 + 512],
                                            zrow[0:1, :], zrow512[0:1, :],
                                            start=False, stop=True)
                    return last
                mark(f"tapm{dg}_{kb}", tT)
            P.cp[f"tapm{dg}"] = P.cp[f"tapm{dg}_1"]

        def emit_merge(dg):
            for w in range(4):
                wl = [f"tapm{dg}", f"gfe{NA-1}"]
                if w >= 2:
                    wl.append(f"mrg{dg}_{w-2}")
                elif dg > 0:
                    wl.append(f"mrg{dg-1}_{w+2}")

                @step("tensor", waits=tuple(wl), sem=tT)
                def _(e, dg=dg, w=w):
                    last = None
                    for j in range(4):
                        a = 4 * w + j
                        last = e.matmul(
                            pmm[w % 2][:, j * 128:j * 128 + 128],
                            bigB[:, a * 1024 + dg * 128:
                                 a * 1024 + dg * 128 + 128],
                            ident[:, :], start=True, stop=True)
                    return last
                mark(f"trans{dg}_{w}", tT)

                @step("vector", waits=(f"trans{dg}_{w}",), sem=tV)
                def _(e, dg=dg, w=w):
                    pw = (pwA, pwB)[w // 2]
                    col0 = (w % 2) * 512
                    t1 = t1s[w % 2]
                    chain(e, e.tensor_scalar(t1[:, :], pmm[w % 2][:, :],
                                             fav[:, dg:dg + 1], None,
                                             OP.mult))
                    unp = bass.AP(pw, col0, [[1024, 128], [1, 32], [32, 16]])
                    return e.scalar_tensor_tensor(
                        bigA[:, dg * NT + w * 512:dg * NT + w * 512 + 512],
                        unp, fav[:, dg:dg + 1], t1[:, :], OP.mult, OP.add)
                mark(f"mrg{dg}_{w}", tV)

        for dg in range(NDC):
            emit_taps(dg)
            emit_merge(dg)
        mark("y2_done", tV)

        @step("vector", waits=("y2_done", "coefs"), sem=tV)
        def _(e):
            last = None
            for dg in range(NDC):
                last = e.tensor_add(bigA[:, dg * NT:dg * NT + 1],
                                    bigA[:, dg * NT:dg * NT + 1],
                                    rw2[:, dg:dg + 1])
            return last
        mark("rw_done", tV)

        if stop_after == "y2":
            P.stop()
        emit_outer_ln("lnS", ("sbuf", bigA), bigA, "rw_done",
                      src_is_b16=True)
        emit_update("x2", xscr, xscr, None, "gb")

        if stop_after == "x2":
            P.stop()
        emit_stream_layer(0)
        emit_stream_layer(1)
        emit_outer_ln("ln3", ("hbm", xscr), bigA, None,
                      tile_ready=lambda ti, dc: [f"x2_s{ti}_{dc}"])
        for ti in range(NTT):
            P.cp[f"fC_hready{ti}"] = P.cp[f"ln3_ap{ti}_{NDC-1}"]

        def hookC(blk):
            if blk == 0:
                emit_stream_layer(2)
                emit_stream_layer(3)
            elif blk == 1:
                emit_stream_layer(4)
                emit_stream_layer(5)
        emit_ffn("C", hookC)

        emit_update("fout", xscr, out_h, None, "half",
                    tile_ready=lambda ti, dc: [f"fC2_e2_{ti*NDC+dc}"])

        @step("sync")
        def _(e):
            e.wait_ge(s_xw[0].h, s_xw[0].n)
            e.wait_ge(s_xw[1].h, s_xw[1].n)
            return None

        with nc.Block() as blk:
            for eng in ENGS:
                def make(eng):
                    def body(e):
                        for fn in P.ops[eng]:
                            fn(e)
                    return body
                getattr(blk, eng)(make(eng))

    return nc


def _host_constants():
    c = {}
    c["c_ident"] = np.eye(128, dtype=BFnp)
    BIG = 1e9
    esb = np.zeros((128, 576), np.float32)
    s = np.arange(128)
    for p in range(4):
        for js in range(2):
            j = 2 * p + js
            col = p * 128 + js * 64
            ecol = np.where(s < 16 * j, (16.0 * j - s), BIG)
            esb[:, col:col + 64] = ecol[:, None]
    esb[:, 512:576] = (128.0 - s)[:, None]
    c["c_esb"] = esb
    etau = np.zeros((128, 64), np.float32)
    r = np.arange(128)
    for ci in range(32):
        jsub = ci // 16
        tau = float(ci % 16)
        onblk = (r // 64) == jsub
        etau[:, ci] = np.where(onblk & (jsub != 0), tau, BIG)
        etau[:, 32 + ci] = np.where(onblk, tau, BIG)
    c["c_etau"] = etau
    c["c_eglob"] = np.broadcast_to(np.arange(NT, dtype=np.float32)[None, :],
                                   (NST, NT)).copy()
    c["c_e16"] = np.broadcast_to(np.arange(16, dtype=np.float32)[None, :],
                                 (NST, 16)).copy()
    return c


def _vec_tile(v):
    return np.ascontiguousarray(v.reshape(8, 128).T)


def kernel(**inputs):
    u = np.asarray(inputs["L0_attn_u"], np.float32)
    qw = np.asarray(inputs["L0_ffn_qweights"], np.int32)
    scales = np.asarray(inputs["L0_ffn_scales"], np.float32)
    biases = np.asarray(inputs["L0_ffn_biases"], np.float32)
    lora_a = np.asarray(inputs["L0_ffn_lora_a"], np.float32)
    lora_b = np.asarray(inputs["L0_ffn_lora_b"], np.float32)
    ln_g = np.asarray(inputs["L0_ffn_ln_gamma"], np.float32)
    ln_b = np.asarray(inputs["L0_ffn_ln_beta"], np.float32)
    A_log = np.asarray(inputs["L0_attn_A_log"], np.float32)
    C = np.asarray(inputs["L0_attn_C"], np.float32)
    Dv = np.asarray(inputs["L0_attn_D"], np.float32)
    delta = np.asarray(inputs["L0_attn_delta"], np.float32)
    gamma = np.asarray(inputs["L0_attn_gamma"], np.float32)
    beta = np.asarray(inputs["L0_attn_beta"], np.float32)
    fa = np.asarray(inputs["L0_attn_freq_align"], np.float32)
    reward = np.asarray(inputs["L0_attn_reward"], np.float32)

    shared = {
        "qT": np.ascontiguousarray(qw.transpose(0, 2, 1)),
        "scT": np.ascontiguousarray(
            np.repeat(scales, 16, axis=2).transpose(0, 2, 1)),
        "la": np.ascontiguousarray(lora_a),
        "lbT": np.ascontiguousarray(lora_b.transpose(0, 2, 1)),
        "bias_t": np.concatenate([_vec_tile(biases[l]) for l in range(6)],
                                 axis=1),
        "lng_t": np.concatenate([_vec_tile(ln_g[i]) for i in range(3)],
                                axis=1),
        "lnb_t": np.concatenate([_vec_tile(ln_b[i]) for i in range(3)],
                                axis=1),
        "d_t": _vec_tile(Dv),
        "gamma_t": _vec_tile(gamma),
        "beta_t": _vec_tile(beta),
        "fa_t": _vec_tile(fa),
        "rw_t": _vec_tile(reward),
        "CT": np.ascontiguousarray(C.T),
        "A_log_t": np.ascontiguousarray(A_log.reshape(NST, 1)),
        "delta0": np.ascontiguousarray(
            np.broadcast_to(delta[0:1], (NST,)).reshape(NST, 1)),
    }
    shared.update(_host_constants())

    nc = build_nc()
    in_maps = []
    for core in range(8):
        b, seg = core // 4, core % 4
        m = dict(shared)
        m["u_fm"] = np.ascontiguousarray(u[b, seg * NT:(seg + 1) * NT, :].T)
        m["is_first"] = np.full((128, 1), 1.0 if seg == 0 else 0.0,
                                np.float32)
        pmk = np.zeros((NST, 4), np.float32)
        e2048 = np.zeros((NST, 4), np.float32)
        for p in range(3):
            if p < seg:
                pmk[:, p] = 1.0
                e2048[:, p] = 2048.0 * (seg - 1 - p)
        m["peer_mask"] = pmk
        m["c_e2048"] = e2048
        in_maps.append(m)

    res = run_bass_kernel_spmd(nc, in_maps, core_ids=list(range(8)))
    out = np.empty((2, 8192, D), np.float32)
    for core in range(8):
        b, seg = core // 4, core % 4
        out[b, seg * NT:(seg + 1) * NT, :] = \
            np.asarray(res.results[core]["out_fm"]).T
    return out


# version 06
# speedup vs baseline: 1.2099x; 1.0451x over previous
import numpy as np
import ml_dtypes
import concourse.bass as bass
import concourse.mybir as mybir
from concourse.bass_utils import run_bass_kernel_spmd

F32 = mybir.dt.float32
BF16 = mybir.dt.bfloat16
I32 = mybir.dt.int32
AF = mybir.ActivationFunctionType
OP = mybir.AluOpType
AX = mybir.AxisListType
BFnp = ml_dtypes.bfloat16

D = 1024
NT = 2048
NDC = 8
TT = 512
NTT = 4
TB = 16
TA = 128
NA = 16
NST = 64
EPS = 1e-5
CONSTS = [EPS, -1.0, 1.0 / 7.5, -0.5, 1.0 / D, 0.5, 1.0, 128.0, 0.0]

ENGS = ("sync", "gpsimd", "scalar", "vector", "tensor")


class Ctr:
    def __init__(self, h):
        self.h = h
        self.n = 0


class _CP(dict):
    stopped = False

    def __missing__(self, key):
        if self.stopped:
            return (None, 0)
        raise KeyError(key)


class Prog:
    def __init__(self):
        self.ops = {e: [] for e in ENGS}
        self.cp = _CP()
        self.stopped = False

    def stop(self):
        self.stopped = True
        self.cp.stopped = True

    def step(self, eng, waits=(), sem=None, k=1):
        if self.stopped:
            return lambda fn: None
        wl = []
        for w in waits:
            ctr, v = self.cp[w]
            if v > 0 and ctr is not None:
                wl.append((ctr.h, v))
        if sem is not None:
            sem.n += k

        def deco(fn):
            def run(e):
                for h, v in wl:
                    e.wait_ge(h, v)
                inst = fn(e)
                if sem is not None:
                    assert inst is not None
                    inst.then_inc(sem.h, k)
            self.ops[eng].append(run)
            return None
        return deco

    def mark(self, name, ctr):
        if not self.stopped:
            self.cp[name] = (ctr, ctr.n)


def build_nc(debug=(), stop_after=None):
    nc = bass.Bass("TRN2", target_bir_lowering=False)

    u_h = nc.dram_tensor("u_fm", [D, NT], F32, kind="ExternalInput")
    qT_h = nc.dram_tensor("qT", [6, D, D], I32, kind="ExternalInput")
    scT_h = nc.dram_tensor("scT", [6, D, D], F32, kind="ExternalInput")
    la_h = nc.dram_tensor("la", [6, 32, D], F32, kind="ExternalInput")
    lbT_h = nc.dram_tensor("lbT", [6, 32, D], F32, kind="ExternalInput")
    bias_h = nc.dram_tensor("bias_t", [128, 48], F32, kind="ExternalInput")
    lng_h = nc.dram_tensor("lng_t", [128, 24], F32, kind="ExternalInput")
    lnb_h = nc.dram_tensor("lnb_t", [128, 24], F32, kind="ExternalInput")
    dvec_h = nc.dram_tensor("d_t", [128, 8], F32, kind="ExternalInput")
    gam_h = nc.dram_tensor("gamma_t", [128, 8], F32, kind="ExternalInput")
    bet_h = nc.dram_tensor("beta_t", [128, 8], F32, kind="ExternalInput")
    fa_h = nc.dram_tensor("fa_t", [128, 8], F32, kind="ExternalInput")
    rw_h = nc.dram_tensor("rw_t", [128, 8], F32, kind="ExternalInput")
    ct_h = nc.dram_tensor("CT", [NST, D], F32, kind="ExternalInput")
    alog_h = nc.dram_tensor("A_log_t", [NST, 1], F32, kind="ExternalInput")
    del0_h = nc.dram_tensor("delta0", [NST, 1], F32, kind="ExternalInput")
    isf_h = nc.dram_tensor("is_first", [128, 1], F32, kind="ExternalInput")
    pm_h = nc.dram_tensor("peer_mask", [NST, 4], F32, kind="ExternalInput")
    ident_h = nc.dram_tensor("c_ident", [128, 128], BF16, kind="ExternalInput")
    esb_h = nc.dram_tensor("c_esb", [128, 576], F32, kind="ExternalInput")
    etau_h = nc.dram_tensor("c_etau", [128, 64], F32, kind="ExternalInput")
    eglob_h = nc.dram_tensor("c_eglob", [NST, NT], F32, kind="ExternalInput")
    e16_h = nc.dram_tensor("c_e16", [NST, 16], F32, kind="ExternalInput")
    e2048_h = nc.dram_tensor("c_e2048", [NST, 4], F32, kind="ExternalInput")

    out_h = nc.dram_tensor("out_fm", [D, NT], F32, kind="ExternalOutput")
    dbg = {}
    for name, shape, dt in debug:
        dbg[name] = nc.dram_tensor(name, shape, dt, kind="ExternalOutput")

    wscr = nc.dram_tensor("wscr", [6, D, D], BF16, kind="Internal")
    xscr = nc.dram_tensor("xscr", [D, NT], F32, kind="Internal")
    lamT_d = nc.dram_tensor("lamT_d", [NST, 1], F32, kind="Internal")
    cc_in = nc.dram_tensor("cc_in", [NST, D], F32, kind="Internal")
    cc_out = nc.dram_tensor("cc_out", [4 * NST, D], F32, kind="Internal")

    from contextlib import ExitStack
    ctx = ExitStack()
    P = Prog()
    step, mark = P.step, P.mark

    def SB(name, shape, dtype):
        return ctx.enter_context(nc.sbuf_tensor(name, shape, dtype))

    def PST(name, shape, dtype=F32):
        return ctx.enter_context(nc.psum_tensor(name, shape, dtype))

    def SEM(name):
        return Ctr(ctx.enter_context(nc.semaphore(name)))

    with ctx:
        bigA = SB("bigA", [128, NDC * NT], BF16)
        bigB = SB("bigB", [128, NDC * NT], BF16)
        wsb = [SB(f"wsb{i}", [128, 8192], BF16) for i in range(2)]
        ident = SB("ident", [128, 128], BF16)
        onesc = SB("onesc", [128, 1], BF16)
        onesr = SB("onesr", [1, 128], BF16)
        zrow = SB("zrow", [1, 128], BF16)
        zrow512 = SB("zrow512", [1, 512], BF16)
        cpool = SB("cpool", [128, len(CONSTS)], F32)
        for i, v in enumerate(CONSTS):
            nc.const_aps.register(F32, v, cpool[:, i:i + 1])
        biasv = SB("biasv", [128, 48], F32)
        lngv = SB("lngv", [128, 24], F32)
        lnbv = SB("lnbv", [128, 24], F32)
        dvecv = SB("dvecv", [128, 8], F32)
        gamv = SB("gamv", [128, 8], F32)
        betv = SB("betv", [128, 8], F32)
        fav = SB("fav", [128, 8], F32)
        rw2 = SB("rw2", [128, 8], F32)
        isf = SB("isf", [128, 1], F32)
        ctrep = SB("ctrep", [128, D], BF16)
        lamv = SB("lamv", [NST, 1], F32)
        lam2 = SB("lam2", [128, 1], F32)
        lam128 = SB("lam128", [NST, 1], F32)
        coefs = SB("coefs", [NST, 4], F32)
        sbL = SB("sbL", [128, 576], BF16)
        p16bd = SB("p16bd", [128, 64], BF16)
        p128L = SB("p128L", [NST, 128], BF16)
        pga = SB("pga", [NST, 128], BF16)
        eglt = SB("eglt", [NST, 128], F32)
        p16v = SB("p16v", [NST, 16], BF16)
        k16 = SB("k16", [128, NDC * 16], F32)
        tdiag = SB("tdiag", [128, 8 * 128], BF16)
        xr = [SB(f"xr{i}", [128, TT], F32) for i in range(3)]
        xr2 = [SB(f"xr2{i}", [128, TT], F32) for i in range(2)]
        xw = [SB(f"xw{i}", [128, TT], F32) for i in range(2)]
        tmp64 = SB("tmp64", [NST, 32], F32)
        sig = SB("sig", [NST, D], F32)
        sigb = SB("sigb", [NST, D], BF16)
        sinit = SB("sinit", [NST, D], F32)
        sinitb = SB("sinitb", [NST, D], BF16)
        cvqa = SB("cvqa", [NST, D], BF16)
        cv2 = SB("cv2", [128, 2 * D], BF16)
        utm = [SB(f"utm{i}", [128, D], BF16) for i in range(3)]
        pglob = SB("pglob", [NST, NT], BF16)
        rowR16 = SB("rowR16", [1, NT], BF16)
        rowA16 = SB("rowA16", [1, NT], BF16)
        rowS = SB("rowS", [1, NT], F32)[:, :]
        rowV = SB("rowV", [1, NT], F32)[:, :]
        rowQ = SB("rowQ", [1, NT], F32)[:, :]
        rowB = SB("rowB", [1, TT], F32)
        rowC = SB("rowC", [1, TT], BF16)
        aBs = [SB(f"aBs{i}", [128, TT], BF16) for i in range(2)]
        rBs = [SB(f"rBs{i}", [128, TT], BF16) for i in range(2)]
        t1s = [SB(f"t1s{i}", [128, TT], F32) for i in range(2)]
        xbt = [SB(f"xbt{i}", [128, TT], BF16) for i in range(2)]
        sqb2 = [SB(f"sqb{i}", [128, TT], BF16) for i in range(2)]
        rstdB = [SB(f"rstdB{i}", [128, TT], BF16) for i in range(2)]
        gbodd = SB("gbodd", [128, 4096], BF16)
        lab = [SB(f"lab{i}", [32, 128], BF16) for i in range(2)]
        lbb = SB("lbb", [32, D], BF16)
        wrow = SB("wrow", [128, 1], F32)
        bb16 = SB("bb16", [128, 8], BF16)
        bmean = SB("bmean", [128, 1], F32)
        lamT16 = SB("lamT16", [1, NST], F32)
        lamT16b = SB("lamT16b", [1, NST], BF16)
        lamTb = SB("lamTb", [128, NST], F32)

        def y1c(ti):
            o = 0 if ti % 2 == 0 else 8192
            return bigB[:, o:o + 4096]
        gb_sl = bigB[:, 4096:8192]
        qb_sl = [bigB[:, 12288:13312], bigB[:, 13312:14336]]
        scb_sl = [bigB[:, 14336:15360], bigB[:, 15360:16384]]

        pmm = [PST(f"pmm{i}", [128, TT], F32) for i in range(2)]
        prow = PST("prow", [1, TT], F32)
        pbc = PST("pbc", [128, TT], F32)
        pwA = PST("pwA", [128, 1024], F32)
        pwB = PST("pwB", [128, 1024], F32)

        s_in = SEM("s_in")
        s_w = SEM("s_w")
        s_xb = [SEM(f"s_xb{i}") for i in range(3)]
        s_st = SEM("s_st")
        s_qg = SEM("s_qg")
        s_qg2 = [SEM(f"s_qg2{i}") for i in range(2)]
        s_pg = SEM("s_pg")
        s_cc = SEM("s_cc")
        s_pgl = SEM("s_pgl")
        s_esb = SEM("s_esb")
        tS = SEM("tS")
        chV = SEM("chV")
        chS = SEM("chS")

        def chain(e, inst, ch=None):
            ch = ch or chV
            ch.n += 1
            inst.then_inc(ch.h, 1)
            e.wait_ge(ch.h, ch.n)
        tT = SEM("tT")
        tV = SEM("tV")
        tG = SEM("tG")

        _p0 = [(bias_h.ap(), biasv[:, 0:48]), (lng_h.ap(), lngv[:, 0:24]),
               (lnb_h.ap(), lnbv[:, 0:24]), (gam_h.ap(), gamv[:, 0:8]),
               (bet_h.ap(), betv[:, 0:8]), (fa_h.ap(), fav[:, 0:8]),
               (rw_h.ap(), rw2[:, 0:8]), (isf_h.ap(), isf[:, 0:1]),
               (dvec_h.ap(), dvecv[:, 0:8]), (ident_h.ap(), ident[:, :]),

               (alog_h.ap(), tmp64[:, 0:1]), (del0_h.ap(), tmp64[:, 1:2]),
               (pm_h.ap(), coefs[:, :]), (e16_h.ap(), tmp64[:, 4:20]),
               (e2048_h.ap(), tmp64[:, 20:24]), (etau_h.ap(), k16[:, 0:64]),
               (bass.AP(eglob_h, 0, [[NT, 64], [1, 128]]), eglt[:, :])]

        @step("sync")
        def _(e):
            for src, dst in _p0:
                e.dma_start(out=dst, in_=src).then_inc(s_in.h, 16)
            return None
        s_in.n += 16 * len(_p0)
        mark("ld_params", s_in)

        @step("gpsimd")
        def _(e):
            e.dma_start(out=ctrep[0:64, :], in_=ct_h.ap()).then_inc(s_qg.h, 16)
            e.dma_start(out=ctrep[64:128, :],
                        in_=ct_h.ap()).then_inc(s_qg.h, 16)
            return None
        s_qg.n += 32
        mark("ld_ct", s_qg)

        @step("gpsimd")
        def _(e):
            e.memset(onesc[:, :], 1.0).then_inc(tG.h, 1)
            e.memset(onesr[:, :], 1.0).then_inc(tG.h, 1)
            e.memset(zrow[:, :], 0.0).then_inc(tG.h, 1)
            e.memset(zrow512[:, :], 0.0).then_inc(tG.h, 1)
            for i, v in enumerate(CONSTS):
                e.memset(cpool[:, i:i + 1], v).then_inc(tG.h, 1)
            return None
        tG.n += 4 + len(CONSTS)
        mark("setup_g", tG)

        @step("scalar", waits=("ld_params", "setup_g"), sem=tS)
        def _(e):
            return e.activation(lamv[0:64, :], tmp64[0:64, 0:1], AF.Exp)
        mark("expA", tS)

        @step("vector", waits=("expA",), sem=tV)
        def _(e):
            chain(e, e.tensor_mul(lamv[0:64, :], lamv[0:64, :],
                                  tmp64[0:64, 1:2]))
            return e.tensor_scalar_mul(lamv[0:64, :], lamv[0:64, :], -1.0)
        mark("lamv", tV)

        @step("vector", waits=("lamv",), sem=tV)
        def _(e):
            e.tensor_copy(lam2[0:64, :], lamv[0:64, :])
            return e.tensor_copy(lam2[64:128, :], lamv[0:64, :])
        mark("lam2", tV)

        @step("scalar", waits=("lamv",), sem=tS)
        def _(e):
            e.activation(lam128[0:64, :], lamv[0:64, :], AF.Exp, scale=128.0)
            e.activation(p16v[0:64, :], tmp64[0:64, 4:20], AF.Exp,
                         scale=lamv[0:64, :])
            e.activation(p128L[0:64, :], eglt[0:64, :], AF.Exp,
                         scale=lamv[0:64, :])
            return e.activation(tmp64[0:64, 28:32], tmp64[0:64, 20:24],
                                AF.Exp, scale=lamv[0:64, :])
        mark("lams", tS)

        @step("vector", waits=("lams",), sem=tV)
        def _(e):
            e.tensor_mul(coefs[0:64, :], coefs[0:64, :], tmp64[0:64, 28:32])
            return e.tensor_scalar_mul(rw2[:, 0:8], rw2[:, 0:8], isf[:, 0:1])
        mark("coefs", tV)

        for half in range(2):
            wl = ["lams"] if half == 0 else ("pgl0",)

            @step("sync", waits=tuple(wl) if half else ("ld_params",),
                  sem=s_pgl, k=16)
            def _(e, half=half):
                return e.dma_start(
                    out=sig[0:64, :],
                    in_=eglob_h.ap()[:, half * 1024:half * 1024 + 1024])
            mark(f"pgld{half}", s_pgl)

            @step("scalar", waits=(f"pgld{half}", "lams"), sem=tS)
            def _(e, half=half):
                return e.activation(pglob[0:64, half * 1024:half * 1024 + 1024],
                                    sig[0:64, :], AF.Exp,
                                    scale=lamv[0:64, :])
            mark(f"pgl{half}", tS)
        mark("pglob_done", tS)

        s_pg_snap = s_pg.n

        @step("gpsimd", waits=("lamv",))
        def _(e):
            e.dma_start(out=lamT_d.ap(),
                        in_=lamv[0:64, :]).then_inc(s_pg.h, 16)
            e.wait_ge(s_pg.h, s_pg_snap + 16)
            e.dma_start(out=lamT16[0:1, 0:64],
                        in_=bass.AP(lamT_d, 0, [[64, 1], [1, 64]])
                        ).then_inc(s_pg.h, 16)
            return None
        s_pg.n += 32
        mark("lamT", s_pg)

        @step("vector", waits=("lamT",), sem=tV)
        def _(e):
            return e.tensor_copy(lamT16b[0:1, :], lamT16[0:1, 0:64])
        mark("lamT16b", tV)

        @step("tensor", waits=("lamT16b", "setup_g", "ld_params"), sem=tT)
        def _(e):
            return e.matmul(pbc[:, 0:64], onesr[0:1, :],
                            lamT16b[0:1, :], start=True, stop=True)
        mark("dAbc", tT)

        @step("vector", waits=("dAbc",), sem=tV)
        def _(e):
            return e.tensor_copy(lamTb[:, :], pbc[:, 0:64])
        mark("lamTb", tV)

        for c in range(2):
            cw, coff = (512, 0) if c == 0 else (64, 512)

            @step("sync", waits=(("lamTb",) if c == 0 else (f"esbx{0}",)),
                  sem=s_esb, k=16)
            def _(e, cw=cw, coff=coff):
                return e.dma_start(out=xr[2][:, 0:cw],
                                   in_=esb_h.ap()[:, coff:coff + cw])
            mark(f"esbl{c}", s_esb)

            @step("vector", waits=(f"esbl{c}", "lamTb"), sem=tV)
            def _(e, cw=cw, coff=coff):
                return e.tensor_mul(
                    xr[2][:, 0:cw], xr[2][:, 0:cw],
                    bass.AP(lamTb, coff % 64, [[64, 128], [0, 9], [1, 64]])
                    if False else
                    bass.AP(lamTb, 0, [[64, 128], [0, (cw + 63) // 64],
                                       [1, 64]]))
            mark(f"esbm{c}", tV)

            @step("scalar", waits=(f"esbm{c}",), sem=tS)
            def _(e, cw=cw, coff=coff):
                return e.activation(sbL[:, coff:coff + cw], xr[2][:, 0:cw],
                                    AF.Exp)
            mark(f"esbx{c}", tS)
        P.cp["sbL"] = P.cp["esbx1"]

        @step("scalar", waits=("lam2",), sem=tS)
        def _(e):
            return e.activation(p16bd[:, 0:64], k16[:, 0:64], AF.Exp,
                                scale=lam2[:, 0:1])
        mark("p16bd", tS)

        for dg in range(NDC):
            w = ("lams", "ld_ct", "lamTb", "p16bd") if dg == 0 else (f"k16ev{dg-1}",)

            @step("tensor", waits=w, sem=tT)
            def _(e, dg=dg):
                return e.matmul(pbc[:, 0:16],
                                ctrep[0:64, dg * 128:dg * 128 + 128],
                                p16v[0:64, :], start=True, stop=True)
            mark(f"k16mm{dg}", tT)

            @step("vector", waits=(f"k16mm{dg}",), sem=tV)
            def _(e, dg=dg):
                return e.tensor_copy(k16[:, dg * 16:dg * 16 + 16],
                                     pbc[:, 0:16])
            mark(f"k16ev{dg}", tV)

        @step("vector", waits=(f"k16ev{NDC-1}", "ld_params"), sem=tV)
        def _(e):
            return e.tensor_add(
                bass.AP(k16, 0, [[128, 128], [16, 8], [1, 1]]),
                bass.AP(k16, 0, [[128, 128], [16, 8], [1, 1]]),
                dvecv[:, 0:8])
        mark("k16d", tV)

        s_lx = [SEM(f"s_lx{i}") for i in range(3)]
        s_lx2 = [SEM(f"s_lx2{i}") for i in range(2)]
        s_xw = [SEM(f"s_xw{i}") for i in range(2)]

        def xtile(h, ti, dc):
            return bass.AP(h, (dc * 128) * NT + ti * TT, [[NT, 128], [1, TT]])

        def emit_outer_ln(tag, src, dst, src_ready, src_is_b16=False,
                          tile_ready=None, ext_stats=False,
                          bcast_apply=True):
            hbm = src[0] == "hbm"
            sh = src[1]
            xins = {}
            for ti in (range(NTT) if not ext_stats else ()):
                for dc in range(NDC):
                    idx = ti * NDC + dc
                    if hbm:
                        wl = list(tile_ready(ti, dc)) if tile_ready else []
                        if idx == 0 and src_ready:
                            wl.append(src_ready)
                        if idx >= 3:
                            wl.append(f"{tag}_cast{(idx-3)//NDC}_{(idx-3)%NDC}")

                        @step("sync", waits=tuple(wl), sem=s_lx[idx % 3],
                              k=16)
                        def _(e, ti=ti, dc=dc, idx=idx):
                            return e.dma_start(out=xr[idx % 3][:, :],
                                               in_=xtile(sh, ti, dc))
                        mark(f"{tag}_xl{ti}_{dc}", s_lx[idx % 3])
                        wl = [f"{tag}_xl{ti}_{dc}"]
                        if idx >= 2:
                            wl.append(f"{tag}_mm{(idx-2)//NDC}_{(idx-2)%NDC}")
                        xb = xbt[dc % 2]

                        @step("scalar", waits=tuple(wl), sem=tS)
                        def _(e, idx=idx, xb=xb):
                            return e.activation(xb[:, :], xr[idx % 3][:, :],
                                                AF.Copy)
                        mark(f"{tag}_cast{ti}_{dc}", tS)
                        xin = xb[:, :]
                        wv = [f"{tag}_cast{ti}_{dc}"]
                    elif not src_is_b16:
                        sl = sh[:, dc * NT + ti * TT:dc * NT + ti * TT + TT]
                        wl = [src_ready]
                        if idx >= 2:
                            wl.append(f"{tag}_mm{(idx-2)//NDC}_{(idx-2)%NDC}")
                        xb = xbt[dc % 2]

                        @step("scalar", waits=tuple(wl), sem=tS)
                        def _(e, sl=sl, xb=xb):
                            return e.activation(xb[:, :], sl, AF.Copy)
                        mark(f"{tag}_cast{ti}_{dc}", tS)
                        xin = xb[:, :]
                        wv = [f"{tag}_cast{ti}_{dc}"]
                    else:
                        xin = sh[:, dc * NT + ti * TT:dc * NT + ti * TT + TT]
                        wv = [src_ready]
                        if tile_ready:
                            wv += list(tile_ready(ti, dc))
                        if idx >= 2:
                            wv.append(f"{tag}_mm{(idx-2)//NDC}_{(idx-2)%NDC}")
                    xins[(ti, dc)] = xin
                    sq = sqb2[dc % 2]

                    @step("vector", waits=tuple(wv), sem=tV)
                    def _(e, xin=xin, sq=sq):
                        return e.tensor_mul(sq[:, :], xin, xin)
                    mark(f"{tag}_sq{ti}_{dc}", tV)

                    wl = [f"{tag}_sq{ti}_{dc}"]
                    if dc == 0 and ti > 0:
                        wl.append(f"{tag}_rows{ti-1}")

                    @step("tensor", waits=tuple(wl), sem=tT)
                    def _(e, xin=xin, sq=sq, dc=dc):
                        e.matmul(prow[0:1, :], onesc[:, 0:1], xin,
                                 start=(dc == 0), stop=(dc == NDC - 1))
                        return e.matmul(pbc[0:1, :], onesc[:, 0:1], sq[:, :],
                                        start=(dc == 0), stop=(dc == NDC - 1))
                    mark(f"{tag}_mm{ti}_{dc}", tT)

                @step("vector", waits=(f"{tag}_mm{ti}_{NDC-1}",), sem=tV)
                def _(e, ti=ti):
                    e.tensor_copy(rowS[:, ti * TT:ti * TT + TT],
                                  prow[0:1, :])
                    return e.tensor_copy(rowQ[:, ti * TT:ti * TT + TT],
                                         pbc[0:1, :])
                mark(f"{tag}_rows{ti}", tV)

            @step("scalar", waits=(f"{tag}_rows{NTT-1}",), sem=tS)
            def _(e):
                return e.activation(rowS[:, :], rowS[:, :], AF.Copy,
                                    scale=1.0 / D)
            mark(f"{tag}_fmu", tS)

            @step("vector", waits=(f"{tag}_fmu",), sem=tV)
            def _(e):
                chain(e, e.tensor_scalar(rowQ[:, :], rowQ[:, :],
                                         1.0 / D, None, OP.mult))
                chain(e, e.tensor_mul(rowV[:, :], rowS[:, :],
                                      rowS[:, :]))
                return e.tensor_sub(rowQ[:, :], rowQ[:, :],
                                    rowV[:, :])
            mark(f"{tag}_fvar", tV)

            @step("scalar", waits=(f"{tag}_fvar",), sem=tS)
            def _(e):
                return e.activation(rowQ[:, :], rowQ[:, :], AF.Ln,
                                    bias=EPS)
            mark(f"{tag}_fln", tS)

            @step("scalar", waits=(f"{tag}_fln",), sem=tS)
            def _(e):
                return e.activation(rowR16[:, :], rowQ[:, :], AF.Exp,
                                    scale=-0.5)
            mark(f"{tag}_frstd", tS)

            @step("vector", waits=(f"{tag}_frstd",), sem=tV)
            def _(e):
                chain(e, e.tensor_mul(rowS[:, :], rowS[:, :],
                                      rowR16[:, :]))
                return e.tensor_scalar(rowA16[:, :], rowS[:, :], -1.0,
                                       None, OP.mult)
            mark(f"{tag}_farow", tV)

            if not bcast_apply:
                P.cp[f"{tag}_done"] = P.cp[f"{tag}_farow"]
                return
            for ti in range(NTT):
                wl = [f"{tag}_farow"]
                if ti >= 1:
                    wl.append(f"{tag}_bcs{ti-1}")

                @step("tensor", waits=tuple(wl), sem=tT)
                def _(e, ti=ti):
                    e.matmul(pbc[:, :], onesr[0:1, :],
                             rowR16[:, ti * TT:ti * TT + TT],
                             start=True, stop=True)
                    return e.matmul(pwB[:, 0:512], onesr[0:1, :],
                                    rowA16[:, ti * TT:ti * TT + TT],
                                    start=True, stop=True)
                mark(f"{tag}_bc{ti}", tT)

                @step("vector", waits=(f"{tag}_bc{ti}",), sem=tV)
                def _(e, ti=ti):
                    e.tensor_copy(rBs[ti % 2][:, :], pbc[:, :])
                    return e.tensor_copy(aBs[ti % 2][:, :], pwB[:, 0:512])
                mark(f"{tag}_bcs{ti}", tV)

                for dc in range(NDC):
                    idx = ti * NDC + dc
                    dl = dst[:, dc * NT + ti * TT:dc * NT + ti * TT + TT]
                    t1 = t1s[dc % 2]
                    if hbm:
                        wl = []
                        if ext_stats and tile_ready:
                            wl += list(tile_ready(ti, dc))
                        if idx >= 2:
                            wl.append(f"{tag}_ap{(idx-2)//NDC}_{(idx-2)%NDC}")

                        @step("scalar", waits=tuple(wl),
                              sem=s_lx2[idx % 2], k=16)
                        def _(e, ti=ti, dc=dc, idx=idx):
                            return e.dma_start(out=xr2[idx % 2][:, :],
                                               in_=xtile(sh, ti, dc))
                        mark(f"{tag}_xl2{ti}_{dc}", s_lx2[idx % 2])

                        @step("vector", waits=(f"{tag}_bcs{ti}",
                                               f"{tag}_xl2{ti}_{dc}"),
                              sem=tV)
                        def _(e, idx=idx, dl=dl, t1=t1, ti=ti):
                            chain(e, e.tensor_mul(t1[:, :],
                                                  xr2[idx % 2][:, :],
                                                  rBs[ti % 2][:, :]))
                            return e.tensor_add(dl, t1[:, :],
                                                aBs[ti % 2][:, :])
                        mark(f"{tag}_ap{ti}_{dc}", tV)
                    else:
                        xin = xins[(ti, dc)]

                        @step("vector", waits=(f"{tag}_bcs{ti}",), sem=tV)
                        def _(e, xin=xin, dl=dl, t1=t1, ti=ti):
                            chain(e, e.tensor_mul(t1[:, :], xin,
                                                  rBs[ti % 2][:, :]))
                            return e.tensor_add(dl, t1[:, :],
                                                aBs[ti % 2][:, :])
                        mark(f"{tag}_ap{ti}_{dc}", tV)
            mark(f"{tag}_done", tV)

        def emit_update(tag, in_h, out_h2, ready, kind, tile_ready=None,
                        stats_for=None, stats_gate=()):
            for ti in range(NTT):
                if kind == "gb2":
                    wl = ["lnS_farow"]
                    if ti >= 1:
                        wl.append(f"lnS_bcs{ti-1}")

                    @step("tensor", waits=tuple(wl), sem=tT)
                    def _(e, ti=ti):
                        e.matmul(pbc[:, :], onesr[0:1, :],
                                 rowR16[:, ti * TT:ti * TT + TT],
                                 start=True, stop=True)
                        return e.matmul(pwB[:, 0:512], onesr[0:1, :],
                                        rowA16[:, ti * TT:ti * TT + TT],
                                        start=True, stop=True)
                    mark(f"lnS_bc{ti}", tT)

                    @step("vector", waits=(f"lnS_bc{ti}",), sem=tV)
                    def _(e, ti=ti):
                        e.tensor_copy(rBs[ti % 2][:, :], pbc[:, :])
                        return e.tensor_copy(aBs[ti % 2][:, :],
                                             pwB[:, 0:512])
                    mark(f"lnS_bcs{ti}", tV)
                for dc in range(NDC):
                    idx = ti * NDC + dc
                    wl = [ready] if (idx == 0 and ready) else []
                    if idx >= 3:
                        wl.append(f"{tag}_v{(idx-3)//NDC}_{(idx-3)%NDC}")

                    @step("sync", waits=tuple(wl), sem=s_lx[idx % 3], k=16)
                    def _(e, ti=ti, dc=dc, idx=idx):
                        return e.dma_start(out=xr[idx % 3][:, :],
                                           in_=xtile(in_h, ti, dc))
                    mark(f"{tag}_l{ti}_{dc}", s_lx[idx % 3])

                    wl = [f"{tag}_l{ti}_{dc}"]
                    if kind == "gb2":
                        wl.append(f"lnS_bcs{ti}")
                    if tile_ready:
                        wl += list(tile_ready(ti, dc))
                    if idx >= 2:
                        wl.append(f"{tag}_s{(idx-2)//NDC}_{(idx-2)%NDC}")
                        if stats_for:
                            wl.append(
                                f"{stats_for}_cast"
                                f"{(idx-2)//NDC}_{(idx-2)%NDC}")

                    @step("vector", waits=tuple(wl), sem=tV)
                    def _(e, ti=ti, dc=dc, idx=idx):
                        sl = bigA[:, dc * NT + ti * TT:dc * NT + ti * TT + TT]
                        o = xw[idx % 2]
                        if kind == "half":
                            return e.scalar_tensor_tensor(
                                o[:, :], sl, 0.5, xr[idx % 3][:, :],
                                OP.mult, OP.add)
                        t1 = t1s[dc % 2]
                        chain(e, e.scalar_tensor_tensor(
                            t1[:, :], sl, gamv[:, dc:dc + 1],
                            rBs[ti % 2][:, :], OP.mult, OP.mult))
                        chain(e, e.scalar_tensor_tensor(
                            t1[:, :], aBs[ti % 2][:, :],
                            gamv[:, dc:dc + 1], t1[:, :],
                            OP.mult, OP.add))
                        return e.scalar_tensor_tensor(
                            o[:, :], t1[:, :], betv[:, dc:dc + 1],
                            xr[idx % 3][:, :], OP.add, OP.add)
                    mark(f"{tag}_v{ti}_{dc}", tV)

                    @step("gpsimd", waits=(f"{tag}_v{ti}_{dc}",),
                          sem=s_xw[idx % 2], k=16)
                    def _(e, ti=ti, dc=dc, idx=idx):
                        return e.dma_start(out=xtile(out_h2, ti, dc),
                                           in_=xw[idx % 2][:, :])
                    mark(f"{tag}_s{ti}_{dc}", s_xw[idx % 2])

                    if stats_for:
                        wl = [f"{tag}_v{ti}_{dc}"]
                        if idx >= 2:
                            wl.append(
                                f"{stats_for}_mm"
                                f"{(idx-2)//NDC}_{(idx-2)%NDC}")

                        @step("scalar", waits=tuple(wl), sem=tS)
                        def _(e, dc=dc, idx=idx):
                            return e.activation(xbt[dc % 2][:, :],
                                                xw[idx % 2][:, :], AF.Copy)
                        mark(f"{stats_for}_cast{ti}_{dc}", tS)

                        @step("vector", waits=(f"{stats_for}_cast{ti}_{dc}",),
                              sem=tV)
                        def _(e, dc=dc):
                            return e.tensor_mul(sqb2[dc % 2][:, :],
                                                xbt[dc % 2][:, :],
                                                xbt[dc % 2][:, :])
                        mark(f"{stats_for}_sq{ti}_{dc}", tV)

                        wl = [f"{stats_for}_sq{ti}_{dc}"]
                        if dc == 0 and ti > 0:
                            wl.append(f"{stats_for}_rows{ti-1}")
                        if idx == 0:
                            wl += list(stats_gate)

                        @step("tensor", waits=tuple(wl), sem=tT)
                        def _(e, dc=dc):
                            e.matmul(pmm[0][0:1, :], onesc[:, 0:1],
                                     xbt[dc % 2][:, :],
                                     start=(dc == 0), stop=(dc == NDC - 1))
                            return e.matmul(
                                pmm[1][0:1, :], onesc[:, 0:1],
                                sqb2[dc % 2][:, :],
                                start=(dc == 0), stop=(dc == NDC - 1))
                        mark(f"{stats_for}_mm{ti}_{dc}", tT)
                if stats_for:
                    @step("vector", waits=(f"{stats_for}_mm{ti}_{NDC-1}",),
                          sem=tV)
                    def _(e, ti=ti):
                        e.tensor_copy(rowS[:, ti * TT:ti * TT + TT],
                                      pmm[0][0:1, :])
                        return e.tensor_copy(rowQ[:, ti * TT:ti * TT + TT],
                                             pmm[1][0:1, :])
                    mark(f"{stats_for}_rows{ti}", tV)
            @step("sync", waits=(f"{tag}_s{NTT-1}_{NDC-2}",
                                 f"{tag}_s{NTT-1}_{NDC-1}"))
            def _(e):
                return None
            P.cp[f"{tag}_done"] = P.cp[f"{tag}_s{NTT-1}_{NDC-1}"]
            P.cp[f"{tag}_done2"] = P.cp[f"{tag}_s{NTT-1}_{NDC-2}"]

        def emit_prep_layer(l):
            buf = wsb[l % 2]

            qg_prev = s_qg.n
            _lw = ["ld_params"]
            if l > 0:
                _lw.append(f"prep_lora{l-1}_{NDC-1}")

            @step("gpsimd", waits=tuple(_lw))
            def _(e, l=l, qg_prev=qg_prev):
                e.wait_ge(s_qg.h, qg_prev)
                e.dma_start(out=lbb[:, :],
                            in_=lbT_h.ap()[l]).then_inc(s_qg.h, 16)
                return None
            s_qg.n += 16
            mark(f"lora_ld{l}", s_qg)
            for dc in range(NDC):
                qbb, scbb = qb_sl[dc % 2], scb_sl[dc % 2]
                wl = []
                if dc >= 2:
                    wl.append(f"prep_asm{l}_{dc-2}")
                    wl.append(f"prep_lora{l}_{dc-2}")
                elif l > 0:
                    wl.append(f"prep_asm{l-1}_{NDC-2+dc}")
                    wl.append(f"prep_lora{l-1}_{NDC-2+dc}")

                sem2 = s_qg2[dc % 2]

                @step("gpsimd", waits=tuple(wl))
                def _(e, l=l, dc=dc, qbb=qbb, scbb=scbb, sem2=sem2):
                    e.dma_start(out=qbb,
                                in_=qT_h.ap()[l, dc * 128:dc * 128 + 128, :]
                                ).then_inc(sem2.h, 16)
                    e.dma_start(
                        out=scbb,
                        in_=scT_h.ap()[l, dc * 128:dc * 128 + 128, :]
                        ).then_inc(sem2.h, 16)
                    e.dma_start(
                        out=lab[dc % 2][:, :],
                        in_=la_h.ap()[l, :, dc * 128:dc * 128 + 128]
                        ).then_inc(sem2.h, 16)
                    return None
                sem2.n += 48
                mark(f"prep_ld{l}_{dc}", sem2)

                wl = [f"lora_ld{l}", f"prep_ld{l}_{dc}"]
                if dc >= 1:
                    wl.append(f"prep_asm{l}_{dc-1}")
                elif l > 0:
                    wl.append(f"prep_asm{l-1}_{NDC-1}")

                @step("tensor", waits=tuple(wl), sem=tT)
                def _(e, dc=dc):
                    e.matmul(pwA[:, 0:512], lab[dc % 2][:, :],
                             lbb[:, 0:512], start=True, stop=True)
                    return e.matmul(pwA[:, 512:1024],
                                    lab[dc % 2][:, :],
                                    lbb[:, 512:1024], start=True, stop=True)
                mark(f"prep_lora{l}_{dc}", tT)

                @step("scalar", waits=(f"prep_ld{l}_{dc}",), sem=tS)
                def _(e, qbb=qbb):
                    return e.activation(qbb, qbb, AF.Copy,
                                        scale=1.0 / 7.5, bias=-1.0)
                mark(f"prep_qf{l}_{dc}", tS)

                wl = [f"prep_qf{l}_{dc}", f"prep_lora{l}_{dc}"]
                if l >= 2 and dc == 0:
                    wl.append(f"ffn_wfree{l-2}")
                    wl.append(f"wscr{l-2}")

                @step("vector", waits=tuple(wl), sem=tV)
                def _(e, l=l, dc=dc, buf=buf, qbb=qbb, scbb=scbb):
                    chain(e, e.tensor_mul(qbb, qbb, scbb))
                    sl = buf[:, dc * 1024:dc * 1024 + 1024]
                    last = e.tensor_add(sl, qbb, pwA[:, :])
                    if l % 2 == 0:
                        chain(e, last)
                        chain(e, e.tensor_reduce(wrow[:, :], sl, AX.X,
                                                 OP.add))
                        chain(e, e.tensor_scalar(wrow[:, :], wrow[:, :],
                                                 1.0 / D, None, OP.mult))
                        last = e.tensor_scalar(sl, sl, wrow[:, 0:1], None,
                                               OP.subtract)
                    return last
                mark(f"prep_asm{l}_{dc}", tV)

            if l % 2 == 0:
                _bw = ["ld_params"]
                if l >= 2:
                    _bw.append(f"prep_bcb{l-2}")

                @step("scalar", waits=tuple(_bw), sem=tS)
                def _(e, l=l):
                    return e.activation(bb16[:, :], biasv[:, l * 8:l * 8 + 8],
                                        AF.Copy)
                mark(f"prep_bcc{l}", tS)

                @step("tensor", waits=(f"prep_bcc{l}", "setup_g"), sem=tT)
                def _(e):
                    return e.matmul(pwB[0:1, 512:520], onesc[:, 0:1],
                                    bb16[:, :], start=True, stop=True)
                mark(f"prep_bcm{l}", tT)

                @step("vector", waits=(f"prep_bcm{l}",), sem=tV)
                def _(e):
                    chain(e, e.tensor_reduce(tmp64[0:1, 30:31],
                                             pwB[0:1, 512:520], AX.X,
                                             OP.add))
                    chain(e, e.tensor_scalar(tmp64[0:1, 30:31],
                                             tmp64[0:1, 30:31],
                                             1.0 / D, None, OP.mult))
                    return e.tensor_copy(bb16[0:1, 0:1], tmp64[0:1, 30:31])
                mark(f"prep_bcr{l}", tV)

                @step("tensor", waits=(f"prep_bcr{l}",), sem=tT)
                def _(e):
                    return e.matmul(pwB[:, 520:521], onesr[0:1, :],
                                    bb16[0:1, 0:1], start=True, stop=True)
                mark(f"prep_bcb{l}", tT)

                @step("vector", waits=(f"prep_bcb{l}",), sem=tV)
                def _(e, l=l):
                    chain(e, e.tensor_copy(bmean[:, :], pwB[:, 520:521]))
                    return e.tensor_scalar(biasv[:, l * 8:l * 8 + 8],
                                           biasv[:, l * 8:l * 8 + 8],
                                           bmean[:, 0:1], None, OP.subtract)
                mark(f"prep_bce{l}", tV)
                P.cp[f"wready{l}_A"] = P.cp[f"prep_bce{l}"]
            else:
                P.cp[f"wready{l}_A"] = P.cp[f"prep_asm{l}_{NDC-1}"]

            sw_prev = s_w.n

            @step("sync", waits=(f"prep_asm{l}_{NDC-1}",), sem=s_w, k=16)
            def _(e, l=l, buf=buf, sw_prev=sw_prev):
                e.wait_ge(s_w.h, sw_prev)
                return e.dma_start(
                    out=bass.AP(wscr, l * D * D,
                                [[D, 128], [128 * D, 8], [1, D]]),
                    in_=buf[:, :])
            mark(f"wscr{l}", s_w)

        def emit_stream_layer(l):
            buf = wsb[l % 2]
            wl = [f"wscr{l}", f"ffn_wfree{l+4}" if l < 2 else
                  f"ffnC_wfree{l-2}"]

            sw_prev = s_w.n

            @step("sync", waits=tuple(wl), sem=s_w, k=16)
            def _(e, l=l, buf=buf, sw_prev=sw_prev):
                e.wait_ge(s_w.h, sw_prev)
                return e.dma_start(
                    out=buf[:, :],
                    in_=bass.AP(wscr, l * D * D,
                                [[D, 128], [128 * D, 8], [1, D]]))
            mark(f"wready{l}_C", s_w)

        def emit_ffn(call, hook=None):
            tag = f"f{call}"
            m2b = [pwB[:, 0:512], pwB[:, 512:1024]]

            def gb2(ti):
                if ti % 2 == 0:
                    return bigB[:, 4096:8192]
                return gbodd[:, :]

            def emit_mm1(blk, ti):
                i0 = 2 * blk
                for oc in range(NDC):
                    idx = ti * NDC + oc
                    wl = [f"wready{i0}_{call}"]
                    if blk == 0:
                        wl.append(f"{tag}_hready{ti}")
                    else:
                        wl.append(f"{tag}{blk-1}_h{ti}")
                    if oc >= 2:
                        wl.append(f"{tag}{blk}_e1_{idx-2}")
                    elif ti >= 1:
                        wl.append(f"{tag}{blk}_e1_{(ti-1)*NDC+6+oc}")
                    elif blk > 0:
                        wl.append(f"{tag}{blk-1}_e1_{3*NDC+6+oc}")

                    @step("tensor", waits=tuple(wl), sem=tT)
                    def _(e, i0=i0, ti=ti, oc=oc, idx=idx):
                        buf = wsb[i0 % 2]
                        bank = pmm[idx % 2]
                        last = None
                        for dc in range(NDC):
                            last = e.matmul(
                                bank[:, :],
                                buf[:, dc * 1024 + oc * 128:
                                    dc * 1024 + oc * 128 + 128],
                                bigA[:, dc * NT + ti * TT:
                                     dc * NT + ti * TT + TT],
                                start=(dc == 0), stop=(dc == NDC - 1))
                        return last
                    mark(f"{tag}{blk}_m1_{idx}", tT)

                    wl = [f"{tag}{blk}_m1_{idx}"]
                    if oc == 0 and ti >= 2:
                        wl.append(f"{tag}{blk}_gin{ti-2}")
                    elif oc == 0 and blk > 0 and ti < 2:
                        wl.append(f"{tag}{blk-1}_gin{ti+2}")

                    @step("scalar", waits=tuple(wl), sem=tS)
                    def _(e, i0=i0, ti=ti, oc=oc, idx=idx):
                        return e.activation(
                            y1c(ti)[:, oc * TT:oc * TT + TT],
                            pmm[idx % 2][:, :], AF.Identity,
                            bias=biasv[:, i0 * 8 + oc:i0 * 8 + oc + 1])
                    mark(f"{tag}{blk}_e1_{idx}", tS)

            def emit_stats(blk, ti):
                gb = gb2(ti)
                wl = [f"{tag}{blk}_e1_{ti*NDC+NDC-1}"]
                if ti >= 2:
                    wl.append(f"{tag}{blk}_m2_{(ti-2)*NDC+NDC-1}")
                elif blk > 0:
                    wl.append(f"{tag}{blk-1}_m2_{(ti+2)*NDC+NDC-1}")

                @step("vector", waits=tuple(wl), sem=tV)
                def _(e, ti=ti, gb=gb):
                    return e.tensor_mul(gb, y1c(ti), y1c(ti))
                mark(f"{tag}{blk}_sq{ti}", tV)

                @step("tensor", waits=(f"{tag}{blk}_sq{ti}",), sem=tT)
                def _(e, gb=gb):
                    last = None
                    for gc in range(NDC):
                        last = e.matmul(prow[0:1, :], onesc[:, 0:1],
                                        gb[:, gc * TT:gc * TT + TT],
                                        start=(gc == 0),
                                        stop=(gc == NDC - 1))
                    return last
                mark(f"{tag}{blk}_smm{ti}", tT)

                @step("scalar", waits=(f"{tag}{blk}_smm{ti}",), sem=tS)
                def _(e):
                    return e.activation(rowB[0:1, :], prow[0:1, :], AF.Ln,
                                        scale=1.0 / D, bias=EPS)
                mark(f"{tag}{blk}_lnr{ti}", tS)

                @step("scalar", waits=(f"{tag}{blk}_lnr{ti}",), sem=tS)
                def _(e):
                    return e.activation(rowC[0:1, :], rowB[0:1, :],
                                        AF.Exp, scale=-0.5)
                mark(f"{tag}{blk}_rst{ti}", tS)

                @step("tensor", waits=(f"{tag}{blk}_rst{ti}",), sem=tT)
                def _(e):
                    return e.matmul(pbc[:, :], onesr[0:1, :],
                                    rowC[0:1, :], start=True, stop=True)
                mark(f"{tag}{blk}_rbc{ti}", tT)

                @step("vector", waits=(f"{tag}{blk}_rbc{ti}",), sem=tV)
                def _(e, ti=ti):
                    return e.tensor_copy(rstdB[ti % 2][:, :], pbc[:, :])
                mark(f"{tag}{blk}_rcp{ti}", tV)

                @step("vector", waits=(f"{tag}{blk}_rcp{ti}",), sem=tV)
                def _(e, ti=ti, gb=gb):
                    return e.tensor_mul(
                        gb, y1c(ti),
                        bass.AP(rstdB[ti % 2], 0,
                                [[TT, 128], [0, NDC], [1, TT]]))
                mark(f"{tag}{blk}_gin{ti}", tV)

                for oc in range(NDC):
                    @step("scalar", waits=(f"{tag}{blk}_gin{ti}",), sem=tS)
                    def _(e, blk=blk, oc=oc, gb=gb):
                        sl = gb[:, oc * TT:oc * TT + TT]
                        return e.activation(
                            sl, sl, AF.Gelu,
                            scale=lngv[:, blk * 8 + oc:blk * 8 + oc + 1],
                            bias=lnbv[:, blk * 8 + oc:blk * 8 + oc + 1])
                    mark(f"{tag}{blk}_gel{ti}_{oc}", tS)

            def emit_mm2(blk, ti):
                i1 = 2 * blk + 1
                gb = gb2(ti)
                for oc in range(NDC):
                    idx = ti * NDC + oc
                    wl = [f"wready{i1}_{call}",
                          f"{tag}{blk}_gel{ti}_{NDC-1}"]
                    if oc >= 2:
                        wl.append(f"{tag}{blk}_e2_{idx-2}")
                    elif ti >= 1:
                        wl.append(f"{tag}{blk}_e2_{(ti-1)*NDC+6+oc}")
                    elif blk > 0:
                        wl.append(f"{tag}{blk-1}_e2_{3*NDC+6+oc}")

                    @step("tensor", waits=tuple(wl), sem=tT)
                    def _(e, i1=i1, oc=oc, gb=gb):
                        buf = wsb[i1 % 2]
                        last = None
                        for gc in range(NDC):
                            last = e.matmul(
                                m2b[oc % 2],
                                buf[:, gc * 1024 + oc * 128:
                                    gc * 1024 + oc * 128 + 128],
                                gb[:, gc * TT:gc * TT + TT],
                                start=(gc == 0), stop=(gc == NDC - 1))
                        return last
                    mark(f"{tag}{blk}_m2_{idx}", tT)

                    @step("vector", waits=(f"{tag}{blk}_m2_{idx}",), sem=tV)
                    def _(e, i1=i1, ti=ti, oc=oc):
                        hl = bigA[:, oc * NT + ti * TT:oc * NT + ti * TT + TT]
                        return e.scalar_tensor_tensor(
                            hl, m2b[oc % 2],
                            biasv[:, i1 * 8 + oc:i1 * 8 + oc + 1],
                            hl, OP.add, OP.add)
                    mark(f"{tag}{blk}_e2_{idx}", tV)
                mark(f"{tag}{blk}_h{ti}", tV)

            for blk in range(3):
                i0, i1 = 2 * blk, 2 * blk + 1
                emit_mm1(blk, 0)
                emit_mm1(blk, 1)
                emit_stats(blk, 0)
                emit_mm1(blk, 2)
                emit_stats(blk, 1)
                emit_mm1(blk, 3)
                emit_mm2(blk, 0)
                emit_stats(blk, 2)
                emit_mm2(blk, 1)
                emit_stats(blk, 3)
                emit_mm2(blk, 2)
                emit_mm2(blk, 3)
                key = "ffn_wfree" if call == "A" else "ffnC_wfree"
                P.cp[f"{key}{i0}"] = P.cp[f"{tag}{blk}_m1_{NTT*NDC-1}"]
                P.cp[f"{key}{i1}"] = P.cp[f"{tag}{blk}_m2_{NTT*NDC-1}"]
                if hook:
                    hook(blk)
                if stop_after == f"{tag}{blk}":
                    P.stop()
                    break
            mark(f"{tag}_done", tV)

        emit_prep_layer(0)
        emit_outer_ln("ln1", ("hbm", u_h), bigA, "ld_params")
        for ti in range(NTT):
            P.cp[f"fA_hready{ti}"] = P.cp[f"ln1_ap{ti}_{NDC-1}"]
        if stop_after == "ln1":
            P.stop()
        emit_prep_layer(1)

        def hookA(blk):
            if blk == 0:
                emit_prep_layer(2)
                emit_prep_layer(3)
            elif blk == 1:
                emit_prep_layer(4)
                emit_prep_layer(5)
        emit_ffn("A", hookA)

        emit_update("x1", u_h, xscr, None, "half",
                    tile_ready=lambda ti, dc: [f"fA2_e2_{ti*NDC+dc}"],
                    stats_for="ln2", stats_gate=("fA2_e1_31",))
        if stop_after == "x1":
            P.stop()

        emit_outer_ln("ln2", ("hbm", xscr), bigA, None,
                      tile_ready=lambda ti, dc: [f"x1_s{ti}_{dc}"],
                      ext_stats=True)

        if stop_after == "ln2":
            P.stop()

        @step("vector", waits=("ln2_done", "pglob_done"), sem=tV)
        def _(e):
            e.memset(sig[0:64, :], 0.0)
            return e.memset(sinit[0:64, :], 0.0)
        mark("sig0", tV)

        for a in range(NA):
            ti = (a * TA) // TT
            wl = [f"ln2_ap{ti}_{dg}" for dg in range(NDC)]
            if a >= 3:
                wl.append(f"sbmm{a-3}")

            @step("sync", waits=tuple(wl))
            def _(e, a=a):
                for dg in range(4):
                    e.dma_start(
                        out=utm[a % 3][:, dg * 128:dg * 128 + 128],
                        in_=bigA[:, dg * NT + a * TA:dg * NT + a * TA + TA],
                        transpose=True).then_inc(s_xb[a % 3].h, 16)
                return None

            @step("scalar", waits=tuple(wl))
            def _(e, a=a):
                for dg in range(4, NDC):
                    e.dma_start(
                        out=utm[a % 3][:, dg * 128:dg * 128 + 128],
                        in_=bigA[:, dg * NT + a * TA:dg * NT + a * TA + TA],
                        transpose=True).then_inc(s_xb[a % 3].h, 16)
                return None
            s_xb[a % 3].n += 16 * NDC
            mark(f"xbar{a}", s_xb[a % 3])

            @step("vector", waits=(("sig0",) if a == 0 else
                                   (f"scan{a-1}", f"pfl{a-1}")), sem=tV)
            def _(e):
                return e.tensor_copy(sigb[0:64, :], sig[0:64, :])
            mark(f"sigb{a}", tV)

            def emit_sbp(a, p):
                pw = (pwA, pwB)[p % 2]
                wl = [f"xbar{a}", "sbL"]
                if a == 0:
                    wl.append("ln2_done")
                if p >= 2:
                    wl.append(f"cvev{a}_{p-2}")
                elif a > 0:
                    wl.append(f"cvev{a-1}_{4-p}")

                @step("tensor", waits=tuple(wl), sem=tT)
                def _(e, a=a, p=p, pw=pw):
                    if p < 4:
                        lhs = sbL[:, p * 128:p * 128 + 128]
                        e.matmul(pw[:, 0:512], lhs, utm[a % 3][:, 0:512],
                                 start=True, stop=True)
                        return e.matmul(pw[:, 512:1024], lhs,
                                        utm[a % 3][:, 512:1024],
                                        start=True, stop=True)
                    lhs = sbL[:, 512:576]
                    e.matmul(pw[0:64, 0:512], lhs, utm[a % 3][:, 0:512],
                             start=True, stop=True)
                    return e.matmul(pw[0:64, 512:1024], lhs,
                                    utm[a % 3][:, 512:1024],
                                    start=True, stop=True)
                mark(f"sbp{a}_{p}", tT)

                wl = [f"sbp{a}_{p}", "ld_ct"]
                if p < 4:
                    if p >= 2:
                        wl.append(f"ptau{a}_{p-2}")
                    elif a > 0:
                        wl.append(f"ptau{a-1}_{p+2}")

                    @step("vector", waits=tuple(wl), sem=tV)
                    def _(e, p=p, pw=pw):
                        return e.tensor_mul(
                            cv2[:, (p % 2) * 1024:(p % 2) * 1024 + 1024],
                            pw[:, :], ctrep[:, :])
                    mark(f"cvev{a}_{p}", tV)
                else:
                    @step("vector", waits=tuple(wl), sem=tV)
                    def _(e, pw=pw):
                        return e.tensor_mul(cvqa[0:64, :], pw[0:64, :],
                                            ctrep[0:64, :])
                    mark(f"cvev{a}_{p}", tV)

            def emit_pfl(a):
                wl = [f"sigb{a}", "lams"]
                if a >= 1:
                    wl.append(f"yfe{a-1}")

                @step("tensor", waits=tuple(wl), sem=tT)
                def _(e):
                    e.matmul(pmm[0][:, :], p128L[0:64, :], sigb[0:64, 0:512],
                             start=True, stop=False, skip_group_check=True)
                    return e.matmul(pmm[1][:, :], p128L[0:64, :],
                                    sigb[0:64, 512:1024], start=True,
                                    stop=False, skip_group_check=True)
                mark(f"pfl{a}", tT)

            def emit_ptau(a, p):
                @step("tensor", waits=(f"cvev{a}_{p}", "p16bd"), sem=tT)
                def _(e, p=p):
                    lhs = p16bd[:, 0:32] if p == 0 else p16bd[:, 32:64]
                    cvs = cv2[:, (p % 2) * 1024:(p % 2) * 1024 + 1024]
                    e.matmul(pmm[0][32 * p:32 * p + 32, :], lhs,
                             cvs[:, 0:512], start=False, stop=True,
                             tile_position=(0, 32 * p),
                             skip_group_check=True)
                    return e.matmul(pmm[1][32 * p:32 * p + 32, :], lhs,
                                    cvs[:, 512:1024], start=False,
                                    stop=True,
                                    tile_position=(0, 32 * p),
                                    skip_group_check=True)
                mark(f"ptau{a}_{p}", tT)

            emit_sbp(a, 0)
            emit_sbp(a, 1)
            emit_pfl(a)
            emit_ptau(a, 0)
            emit_sbp(a, 2)
            emit_ptau(a, 1)
            emit_sbp(a, 3)
            emit_ptau(a, 2)
            emit_sbp(a, 4)
            emit_ptau(a, 3)
            mark(f"sbmm{a}", tT)

            @step("vector", waits=(f"cvev{a}_4", f"sigb{a}"), sem=tV)
            def _(e):
                return e.scalar_tensor_tensor(sig[0:64, :], sig[0:64, :],
                                              lam128[0:64, 0:1],
                                              cvqa[0:64, :], OP.mult, OP.add)
            mark(f"scan{a}", tV)

            @step("scalar", waits=(f"ptau{a}_3",), sem=tS)
            def _(e, a=a):
                e.activation(bigB[:, a * 1024:a * 1024 + 512], pmm[0][:, :],
                             AF.Copy)
                return e.activation(bigB[:, a * 1024 + 512:a * 1024 + 1024],
                                    pmm[1][:, :], AF.Copy)
            mark(f"yfe{a}", tS)

        if stop_after == "pass1":
            P.stop()
        @step("gpsimd", waits=(f"scan{NA-1}",), sem=s_pg, k=16)
        def _(e):
            return e.dma_start(out=cc_in.ap(), in_=sig[0:64, :])
        mark("ccin", s_pg)

        @step("gpsimd", waits=("ccin",), sem=s_cc, k=1)
        def _(e):
            return e.collective_compute(
                "AllGather", OP.bypass,
                replica_groups=[[0, 1, 2, 3], [4, 5, 6, 7]],
                ins=[cc_in.ap().opt()], outs=[cc_out.ap().opt()])
        mark("cc", s_cc)

        for m in range(3):
            @step("gpsimd", waits=(("cc",) if m == 0 else (f"sacc{m-1}",)),
                  sem=s_pg, k=16)
            def _(e, m=m):
                return e.dma_start(out=cvqa[0:64, :],
                                   in_=cc_out.ap()[m * 64:m * 64 + 64, :])
            mark(f"peer{m}", s_pg)

            @step("vector", waits=(f"peer{m}", "coefs"), sem=tV)
            def _(e, m=m):
                return e.scalar_tensor_tensor(
                    sinit[0:64, :], cvqa[0:64, :],
                    coefs[0:64, m:m + 1], sinit[0:64, :], OP.mult, OP.add)
            mark(f"sacc{m}", tV)

        @step("vector", waits=("sacc2",), sem=tV)
        def _(e):
            return e.tensor_copy(sinitb[0:64, :], sinit[0:64, :])
        mark("sinitb", tV)

        def emit_global():
          for a in range(NA):
              wl = ["lams", "sinitb", f"yfe{NA-1}"]
              if a >= 1:
                  wl.append(f"gfe{a-1}")

              @step("tensor", waits=tuple(wl), sem=tT)
              def _(e, a=a):
                  e.matmul(pmm[0][:, :], pglob[0:64, a * 128:a * 128 + 128],
                           sinitb[0:64, 0:512], start=True, stop=True)
                  return e.matmul(pmm[1][:, :],
                                  pglob[0:64, a * 128:a * 128 + 128],
                                  sinitb[0:64, 512:1024], start=True, stop=True)
              mark(f"pgmm{a}", tT)

              @step("vector", waits=(f"pgmm{a}",), sem=tV)
              def _(e, a=a):
                  sl = bigB[:, a * 1024:a * 1024 + 512]
                  e.scalar_tensor_tensor(sl, pmm[0][:, :], 1.0,
                                         sl, OP.mult, OP.add)
                  s2 = bigB[:, a * 1024 + 512:a * 1024 + 1024]
                  return e.scalar_tensor_tensor(s2, pmm[1][:, :], 1.0, s2,
                                                OP.mult, OP.add)
              mark(f"gfe{a}", tV)

        def emit_taps(dg):
            for kb in range(2):
                wl = ["k16d", "ld_params"]
                if kb == 1:
                    wl.append(f"tapm{dg}_0")
                elif dg > 0:
                    wl.append(f"tapm{dg-1}_1")

                @step("vector", waits=tuple(wl), sem=tV)
                def _(e, dg=dg, kb=kb):
                    last = None
                    for k8 in range(8):
                        k = kb * 8 + k8
                        last = e.tensor_scalar(
                            tdiag[:, k8 * 128:k8 * 128 + 128], ident[:, :],
                            k16[:, dg * 16 + k:dg * 16 + k + 1], None,
                            OP.mult)
                    return last
                mark(f"tdiag{dg}_{kb}", tV)

                wl = [f"tdiag{dg}_{kb}"]
                if dg == 0 and kb == 0:
                    wl.append(f"cvev{NA-1}_4")
                    wl.append(f"scan{NA-1}")
                if kb == 0 and dg > 0:
                    wl.append(f"mrg{dg-1}_3")

                @step("tensor", waits=tuple(wl), sem=tT)
                def _(e, dg=dg, kb=kb):
                    last = None
                    for k8 in range(8):
                        k = kb * 8 + k8
                        lhs = tdiag[:, k8 * 128:k8 * 128 + 128]
                        for w in range(4):
                            pw = (pwA, pwB)[w // 2]
                            col0 = (w % 2) * 512
                            out = pw[:, col0 + 32 * k:col0 + 512]
                            rhs = bass.AP(
                                bigA, dg * NT + w * 512,
                                [[NDC * NT, 128], [1, TB - k], [TB, 32]])
                            last = e.matmul(out, lhs, rhs, start=(k == 0),
                                            stop=False)
                    if kb == 1:
                        for w in range(4):
                            pw = (pwA, pwB)[w // 2]
                            col0 = (w % 2) * 512
                            last = e.matmul(pw[:, col0:col0 + 512],
                                            zrow[0:1, :], zrow512[0:1, :],
                                            start=False, stop=True)
                    return last
                mark(f"tapm{dg}_{kb}", tT)
            P.cp[f"tapm{dg}"] = P.cp[f"tapm{dg}_1"]

        def emit_merge(dg):
            for w in range(4):
                wl = [f"tapm{dg}", f"gfe{NA-1}"]
                if w >= 2:
                    wl.append(f"mrg{dg}_{w-2}")
                elif dg > 0:
                    wl.append(f"mrg{dg-1}_{w+2}")

                @step("tensor", waits=tuple(wl), sem=tT)
                def _(e, dg=dg, w=w):
                    last = None
                    for j in range(4):
                        a = 4 * w + j
                        last = e.matmul(
                            pmm[w % 2][:, j * 128:j * 128 + 128],
                            bigB[:, a * 1024 + dg * 128:
                                 a * 1024 + dg * 128 + 128],
                            ident[:, :], start=True, stop=True)
                    return last
                mark(f"trans{dg}_{w}", tT)

                @step("vector", waits=(f"trans{dg}_{w}",), sem=tV)
                def _(e, dg=dg, w=w):
                    pw = (pwA, pwB)[w // 2]
                    col0 = (w % 2) * 512
                    t1 = t1s[w % 2]
                    chain(e, e.tensor_scalar(t1[:, :], pmm[w % 2][:, :],
                                             fav[:, dg:dg + 1], None,
                                             OP.mult))
                    unp = bass.AP(pw, col0, [[1024, 128], [1, 32], [32, 16]])
                    return e.scalar_tensor_tensor(
                        bigA[:, dg * NT + w * 512:dg * NT + w * 512 + 512],
                        unp, fav[:, dg:dg + 1], t1[:, :], OP.mult, OP.add)
                mark(f"mrg{dg}_{w}", tV)

        emit_taps(0)
        emit_global()
        mark("yfar_done", tV)
        if stop_after == "yfar":
            P.stop()
        emit_merge(0)
        for dg in range(1, NDC):
            emit_taps(dg)
            emit_merge(dg)
        mark("y2_done", tV)

        @step("vector", waits=("y2_done", "coefs"), sem=tV)
        def _(e):
            last = None
            for dg in range(NDC):
                last = e.tensor_add(bigA[:, dg * NT:dg * NT + 1],
                                    bigA[:, dg * NT:dg * NT + 1],
                                    rw2[:, dg:dg + 1])
            return last
        mark("rw_done", tV)

        if stop_after == "y2":
            P.stop()
        emit_outer_ln("lnS", ("sbuf", bigA), bigA, "rw_done",
                      src_is_b16=True, bcast_apply=False)
        emit_update("x2", xscr, xscr, None, "gb2",
                    stats_for="ln3", stats_gate=(f"lnS_rows{NTT-1}",))

        if stop_after == "x2":
            P.stop()
        emit_stream_layer(0)
        emit_stream_layer(1)
        emit_outer_ln("ln3", ("hbm", xscr), bigA, None,
                      tile_ready=lambda ti, dc: [f"x2_s{ti}_{dc}"],
                      ext_stats=True)
        for ti in range(NTT):
            P.cp[f"fC_hready{ti}"] = P.cp[f"ln3_ap{ti}_{NDC-1}"]

        def hookC(blk):
            if blk == 0:
                emit_stream_layer(2)
                emit_stream_layer(3)
            elif blk == 1:
                emit_stream_layer(4)
                emit_stream_layer(5)
        emit_ffn("C", hookC)

        emit_update("fout", xscr, out_h, None, "half",
                    tile_ready=lambda ti, dc: [f"fC2_e2_{ti*NDC+dc}"])

        @step("sync")
        def _(e):
            e.wait_ge(s_xw[0].h, s_xw[0].n)
            e.wait_ge(s_xw[1].h, s_xw[1].n)
            return None

        with nc.Block() as blk:
            for eng in ENGS:
                def make(eng):
                    def body(e):
                        for fn in P.ops[eng]:
                            fn(e)
                    return body
                getattr(blk, eng)(make(eng))

    return nc


def _host_constants():
    c = {}
    c["c_ident"] = np.eye(128, dtype=BFnp)
    BIG = 1e9
    esb = np.zeros((128, 576), np.float32)
    s = np.arange(128)
    for p in range(4):
        for js in range(2):
            j = 2 * p + js
            col = p * 128 + js * 64
            ecol = np.where(s < 16 * j, (16.0 * j - s), BIG)
            esb[:, col:col + 64] = ecol[:, None]
    esb[:, 512:576] = (128.0 - s)[:, None]
    c["c_esb"] = esb
    etau = np.zeros((128, 64), np.float32)
    r = np.arange(128)
    for ci in range(32):
        jsub = ci // 16
        tau = float(ci % 16)
        onblk = (r // 64) == jsub
        etau[:, ci] = np.where(onblk & (jsub != 0), tau, BIG)
        etau[:, 32 + ci] = np.where(onblk, tau, BIG)
    c["c_etau"] = etau
    c["c_eglob"] = np.broadcast_to(np.arange(NT, dtype=np.float32)[None, :],
                                   (NST, NT)).copy()
    c["c_e16"] = np.broadcast_to(np.arange(16, dtype=np.float32)[None, :],
                                 (NST, 16)).copy()
    return c


def _vec_tile(v):
    return np.ascontiguousarray(v.reshape(8, 128).T)


def kernel(**inputs):
    u = np.asarray(inputs["L0_attn_u"], np.float32)
    qw = np.asarray(inputs["L0_ffn_qweights"], np.int32)
    scales = np.asarray(inputs["L0_ffn_scales"], np.float32)
    biases = np.asarray(inputs["L0_ffn_biases"], np.float32)
    lora_a = np.asarray(inputs["L0_ffn_lora_a"], np.float32)
    lora_b = np.asarray(inputs["L0_ffn_lora_b"], np.float32)
    ln_g = np.asarray(inputs["L0_ffn_ln_gamma"], np.float32)
    ln_b = np.asarray(inputs["L0_ffn_ln_beta"], np.float32)
    A_log = np.asarray(inputs["L0_attn_A_log"], np.float32)
    C = np.asarray(inputs["L0_attn_C"], np.float32)
    Dv = np.asarray(inputs["L0_attn_D"], np.float32)
    delta = np.asarray(inputs["L0_attn_delta"], np.float32)
    gamma = np.asarray(inputs["L0_attn_gamma"], np.float32)
    beta = np.asarray(inputs["L0_attn_beta"], np.float32)
    fa = np.asarray(inputs["L0_attn_freq_align"], np.float32)
    reward = np.asarray(inputs["L0_attn_reward"], np.float32)

    shared = {
        "qT": np.ascontiguousarray(qw.transpose(0, 2, 1)),
        "scT": np.ascontiguousarray(
            np.repeat(scales, 16, axis=2).transpose(0, 2, 1)),
        "la": np.ascontiguousarray(lora_a),
        "lbT": np.ascontiguousarray(lora_b.transpose(0, 2, 1)),
        "bias_t": np.concatenate([_vec_tile(biases[l]) for l in range(6)],
                                 axis=1),
        "lng_t": np.concatenate([_vec_tile(ln_g[i]) for i in range(3)],
                                axis=1),
        "lnb_t": np.concatenate([_vec_tile(ln_b[i]) for i in range(3)],
                                axis=1),
        "d_t": _vec_tile(Dv),
        "gamma_t": _vec_tile(gamma),
        "beta_t": _vec_tile(beta),
        "fa_t": _vec_tile(fa),
        "rw_t": _vec_tile(reward),
        "CT": np.ascontiguousarray(C.T),
        "A_log_t": np.ascontiguousarray(A_log.reshape(NST, 1)),
        "delta0": np.ascontiguousarray(
            np.broadcast_to(delta[0:1], (NST,)).reshape(NST, 1)),
    }
    shared.update(_host_constants())

    nc = build_nc()
    in_maps = []
    for core in range(8):
        b, seg = core // 4, core % 4
        m = dict(shared)
        m["u_fm"] = np.ascontiguousarray(u[b, seg * NT:(seg + 1) * NT, :].T)
        m["is_first"] = np.full((128, 1), 1.0 if seg == 0 else 0.0,
                                np.float32)
        pmk = np.zeros((NST, 4), np.float32)
        e2048 = np.zeros((NST, 4), np.float32)
        for p in range(3):
            if p < seg:
                pmk[:, p] = 1.0
                e2048[:, p] = 2048.0 * (seg - 1 - p)
        m["peer_mask"] = pmk
        m["c_e2048"] = e2048
        in_maps.append(m)

    res = run_bass_kernel_spmd(nc, in_maps, core_ids=list(range(8)))
    out = np.empty((2, 8192, D), np.float32)
    for core in range(8):
        b, seg = core // 4, core % 4
        out[b, seg * NT:(seg + 1) * NT, :] = \
            np.asarray(res.results[core]["out_fm"]).T
    return out


# version 13
# speedup vs baseline: 1.3204x; 1.0914x over previous
import numpy as np
import ml_dtypes
import concourse.bass as bass
from concourse.bass_isa import ReduceOp
import concourse.mybir as mybir
from concourse.bass_utils import run_bass_kernel_spmd

F32 = mybir.dt.float32
BF16 = mybir.dt.bfloat16
I32 = mybir.dt.int32
I8 = mybir.dt.int8
AF = mybir.ActivationFunctionType
OP = mybir.AluOpType
AX = mybir.AxisListType
BFnp = ml_dtypes.bfloat16

D = 1024
NT = 2048
NDC = 8
TT = 512
NTT = 4
TB = 16
TA = 128
NA = 16
NST = 64
EPS = 1e-5
CONSTS = [EPS, -1.0, 1.0 / 7.5, -0.5, 1.0 / D, 0.5, 1.0, 128.0, 0.0, -1.0 / D]

ENGS = ("sync", "gpsimd", "scalar", "vector", "tensor")


class Ctr:
    def __init__(self, h):
        self.h = h
        self.n = 0


class _CP(dict):
    stopped = False

    def __missing__(self, key):
        if self.stopped:
            return (None, 0)
        raise KeyError(key)


class Prog:
    def __init__(self):
        self.ops = {e: [] for e in ENGS}
        self.cp = _CP()
        self.stopped = False

    def stop(self):
        self.stopped = True
        self.cp.stopped = True

    def step(self, eng, waits=(), sem=None, k=1):
        if self.stopped:
            return lambda fn: None
        wl = []
        for w in waits:
            ctr, v = self.cp[w]
            if v > 0 and ctr is not None:
                wl.append((ctr.h, v))
        if sem is not None:
            sem.n += k

        def deco(fn):
            def run(e):
                for h, v in wl:
                    e.wait_ge(h, v)
                inst = fn(e)
                if sem is not None:
                    assert inst is not None
                    inst.then_inc(sem.h, k)
            self.ops[eng].append(run)
            return None
        return deco

    def mark(self, name, ctr):
        if not self.stopped:
            self.cp[name] = (ctr, ctr.n)


def build_nc(debug=(), stop_after=None):
    nc = bass.Bass("TRN2", target_bir_lowering=False)

    u_h = nc.dram_tensor("u_fm", [D, NT], F32, kind="ExternalInput")
    qT_h = nc.dram_tensor("qT", [6, D, D], I8, kind="ExternalInput")
    scT_h = nc.dram_tensor("scT", [6, 64, D], F32, kind="ExternalInput")
    la_h = nc.dram_tensor("la", [6, 32, D], F32, kind="ExternalInput")
    lbT_h = nc.dram_tensor("lbT", [6, 32, D], F32, kind="ExternalInput")
    bias_h = nc.dram_tensor("bias_t", [128, 48], F32, kind="ExternalInput")
    lng_h = nc.dram_tensor("lng_t", [128, 24], F32, kind="ExternalInput")
    lnb_h = nc.dram_tensor("lnb_t", [128, 24], F32, kind="ExternalInput")
    dvec_h = nc.dram_tensor("d_t", [128, 8], F32, kind="ExternalInput")
    gam_h = nc.dram_tensor("gamma_t", [128, 8], F32, kind="ExternalInput")
    bet_h = nc.dram_tensor("beta_t", [128, 8], F32, kind="ExternalInput")
    fa_h = nc.dram_tensor("fa_t", [128, 8], F32, kind="ExternalInput")
    rw_h = nc.dram_tensor("rw_t", [128, 8], F32, kind="ExternalInput")
    ct_h = nc.dram_tensor("CT", [NST, D], F32, kind="ExternalInput")
    alog_h = nc.dram_tensor("A_log_t", [NST, 1], F32, kind="ExternalInput")
    del0_h = nc.dram_tensor("delta0", [NST, 1], F32, kind="ExternalInput")
    isf_h = nc.dram_tensor("is_first", [128, 1], F32, kind="ExternalInput")
    pm_h = nc.dram_tensor("peer_mask", [NST, 4], F32, kind="ExternalInput")
    ident_h = nc.dram_tensor("c_ident", [128, 128], BF16, kind="ExternalInput")
    esb_h = nc.dram_tensor("c_esb", [128, 576], F32, kind="ExternalInput")
    etau_h = nc.dram_tensor("c_etau", [128, 64], F32, kind="ExternalInput")
    eglob_h = nc.dram_tensor("c_eglob", [NST, NT], F32, kind="ExternalInput")
    e16_h = nc.dram_tensor("c_e16", [NST, 16], F32, kind="ExternalInput")
    e2048_h = nc.dram_tensor("c_e2048", [NST, 4], F32, kind="ExternalInput")

    out_h = nc.dram_tensor("out_fm", [D, NT], F32, kind="ExternalOutput")
    dbg = {}
    for name, shape, dt in debug:
        dbg[name] = nc.dram_tensor(name, shape, dt, kind="ExternalOutput")

    wscr = nc.dram_tensor("wscr", [6, D, D], BF16, kind="Internal")
    xscr = nc.dram_tensor("xscr", [D, NT], F32, kind="Internal")
    lamT_d = nc.dram_tensor("lamT_d", [NST, 1], F32, kind="Internal")
    cc_in = nc.dram_tensor("cc_in", [NST, D], F32, kind="Internal")
    cc_out = nc.dram_tensor("cc_out", [4 * NST, D], F32, kind="Internal")

    from contextlib import ExitStack
    ctx = ExitStack()
    P = Prog()
    step, mark = P.step, P.mark

    def SB(name, shape, dtype):
        return ctx.enter_context(nc.sbuf_tensor(name, shape, dtype))

    def PST(name, shape, dtype=F32):
        return ctx.enter_context(nc.psum_tensor(name, shape, dtype))

    def SEM(name):
        return Ctr(ctx.enter_context(nc.semaphore(name)))

    with ctx:
        bigA = SB("bigA", [128, NDC * NT], BF16)
        bigB = SB("bigB", [128, NDC * NT], BF16)
        wsb = [SB(f"wsb{i}", [128, 8192], BF16) for i in range(2)]
        ident = SB("ident", [128, 128], BF16)
        onesc = SB("onesc", [128, 1], BF16)
        onesr = SB("onesr", [1, 128], BF16)
        zrow = SB("zrow", [1, 128], BF16)
        zrow512 = SB("zrow512", [1, 512], BF16)
        cpool = SB("cpool", [128, len(CONSTS)], F32)
        for i, v in enumerate(CONSTS):
            nc.const_aps.register(F32, v, cpool[:, i:i + 1])
        biasv = SB("biasv", [128, 48], F32)
        lngv = SB("lngv", [128, 24], F32)
        lnbv = SB("lnbv", [128, 24], F32)
        dvecv = SB("dvecv", [128, 8], F32)
        gamv = SB("gamv", [128, 8], F32)
        betv = SB("betv", [128, 8], F32)
        fav = SB("fav", [128, 8], F32)
        rw2 = SB("rw2", [128, 8], F32)
        isf = SB("isf", [128, 1], F32)
        ctrep = SB("ctrep", [128, D], BF16)
        lamv = SB("lamv", [NST, 1], F32)
        lam2 = SB("lam2", [128, 1], F32)
        lam128 = SB("lam128", [NST, 1], F32)
        coefs = SB("coefs", [NST, 4], F32)
        sbL = SB("sbL", [128, 576], BF16)
        p16bd = SB("p16bd", [128, 64], BF16)
        p128L = SB("p128L", [NST, 128], BF16)
        pga = SB("pga", [NST, 128], BF16)
        eglt = SB("eglt", [NST, 128], F32)
        p16v = SB("p16v", [NST, 16], BF16)
        k16 = SB("k16", [128, NDC * 16], F32)
        tdiag = SB("tdiag", [128, 8 * 128], BF16)
        xr = [SB(f"xr{i}", [128, TT], F32) for i in range(3)]
        xr2 = [SB(f"xr2{i}", [128, TT], F32) for i in range(2)]
        xw = [SB(f"xw{i}", [128, TT], F32) for i in range(2)]
        tmp64 = SB("tmp64", [NST, 32], F32)
        sig = SB("sig", [NST, D], F32)
        sigb = SB("sigb", [NST, D], BF16)
        sinit = SB("sinit", [NST, D], F32)
        sinitb = SB("sinitb", [NST, D], BF16)
        cvqa = SB("cvqa", [NST, D], BF16)
        cv2 = SB("cv2", [128, 2 * D], BF16)
        utm = [SB(f"utm{i}", [128, D], BF16) for i in range(3)]
        pglob = SB("pglob", [NST, NT], BF16)
        rowR16 = SB("rowR16", [1, NT], BF16)
        rowA16 = SB("rowA16", [1, NT], BF16)
        rowS = SB("rowS", [1, NT], F32)[:, :]
        rowV = SB("rowV", [1, NT], F32)[:, :]
        rowQ = SB("rowQ", [1, NT], F32)[:, :]
        rowB = SB("rowB", [1, TT], F32)
        rowC = SB("rowC", [1, TT], BF16)
        aBs = [SB(f"aBs{i}", [128, TT], BF16) for i in range(2)]
        rBs = [SB(f"rBs{i}", [128, TT], BF16) for i in range(2)]
        t1s = [SB(f"t1s{i}", [128, TT], F32) for i in range(2)]
        xbt = [SB(f"xbt{i}", [128, TT], BF16) for i in range(2)]
        sqb2 = [SB(f"sqb{i}", [128, TT], BF16) for i in range(2)]
        rstdB = [SB(f"rstdB{i}", [128, TT], BF16) for i in range(2)]
        gbodd = SB("gbodd", [128, 4096], BF16)
        lab = [SB(f"lab{i}", [32, 128], BF16) for i in range(2)]
        lbb = SB("lbb", [32, D], BF16)
        wrow = SB("wrow", [128, 1], F32)
        wrow2 = SB("wrow2", [128, 1], F32)
        bsum8 = SB("bsum8", [128, 8], F32)
        bmean2 = SB("bmean2", [128, 1], F32)
        bb16 = SB("bb16", [128, 8], BF16)
        bmean = SB("bmean", [128, 1], F32)
        lamT16 = SB("lamT16", [1, NST], F32)
        lamT16b = SB("lamT16b", [1, NST], BF16)
        lamTb = SB("lamTb", [128, NST], F32)

        def y1c(ti):
            o = 0 if ti % 2 == 0 else 8192
            return bigB[:, o:o + 4096]
        gb_sl = bigB[:, 4096:8192]
        qb_sl = [bigB[:, 12288:13312], bigB[:, 13312:14336]]
        scb_sl = [bigB[:, 14336:15360], bigB[:, 15360:16384]]

        pmm = [PST(f"pmm{i}", [128, TT], F32) for i in range(2)]
        prow = PST("prow", [1, TT], F32)
        pbc = PST("pbc", [128, TT], F32)
        pwA = PST("pwA", [128, 1024], F32)
        pwB = PST("pwB", [128, 1024], F32)

        s_in = SEM("s_in")
        s_w = SEM("s_w")
        s_wg = SEM("s_wg")
        s_ws = SEM("s_ws")
        s_xb = [SEM(f"s_xb{i}") for i in range(3)]
        s_st = SEM("s_st")
        s_qg = SEM("s_qg")
        s_qg2 = [SEM(f"s_qg2{i}") for i in range(2)]
        s_pp = SEM("s_pp")
        s_pg = SEM("s_pg")
        s_cc = SEM("s_cc")
        s_pgl = SEM("s_pgl")
        s_esb = SEM("s_esb")
        tS = SEM("tS")
        chV = SEM("chV")
        chS = SEM("chS")

        def chain(e, inst, ch=None):
            ch = ch or chV
            ch.n += 1
            inst.then_inc(ch.h, 1)
            e.wait_ge(ch.h, ch.n)
        tT = SEM("tT")
        tV = SEM("tV")
        tG = SEM("tG")

        _p0 = [(bias_h.ap(), biasv[:, 0:48]), (lng_h.ap(), lngv[:, 0:24]),
               (lnb_h.ap(), lnbv[:, 0:24]), (gam_h.ap(), gamv[:, 0:8]),
               (bet_h.ap(), betv[:, 0:8]), (fa_h.ap(), fav[:, 0:8]),
               (rw_h.ap(), rw2[:, 0:8]), (isf_h.ap(), isf[:, 0:1]),
               (dvec_h.ap(), dvecv[:, 0:8]), (ident_h.ap(), ident[:, :]),

               (alog_h.ap(), tmp64[:, 0:1]), (del0_h.ap(), tmp64[:, 1:2]),
               (pm_h.ap(), coefs[:, :]), (e16_h.ap(), tmp64[:, 4:20]),
               (e2048_h.ap(), tmp64[:, 20:24]), (etau_h.ap(), k16[:, 0:64]),
               (bass.AP(eglob_h, 0, [[NT, 64], [1, 128]]), eglt[:, :])]

        @step("sync")
        def _(e):
            for src, dst in _p0:
                e.dma_start(out=dst, in_=src).then_inc(s_in.h, 16)
            return None
        s_in.n += 16 * len(_p0)
        mark("ld_params", s_in)

        @step("gpsimd")
        def _(e):
            e.dma_start(out=ctrep[0:64, :], in_=ct_h.ap()).then_inc(s_qg.h, 16)
            e.dma_start(out=ctrep[64:128, :],
                        in_=ct_h.ap()).then_inc(s_qg.h, 16)
            return None
        s_qg.n += 32
        mark("ld_ct", s_qg)

        @step("gpsimd")
        def _(e):
            e.memset(onesc[:, :], 1.0).then_inc(tG.h, 1)
            e.memset(onesr[:, :], 1.0).then_inc(tG.h, 1)
            e.memset(zrow[:, :], 0.0).then_inc(tG.h, 1)
            e.memset(zrow512[:, :], 0.0).then_inc(tG.h, 1)
            for i, v in enumerate(CONSTS):
                e.memset(cpool[:, i:i + 1], v).then_inc(tG.h, 1)
            return None
        tG.n += 4 + len(CONSTS)
        mark("setup_g", tG)

        @step("scalar", waits=("ld_params", "setup_g"), sem=tS)
        def _(e):
            return e.activation(lamv[0:64, :], tmp64[0:64, 0:1], AF.Exp)
        mark("expA", tS)

        @step("vector", waits=("expA",), sem=tV)
        def _(e):
            chain(e, e.tensor_mul(lamv[0:64, :], lamv[0:64, :],
                                  tmp64[0:64, 1:2]))
            return e.tensor_scalar_mul(lamv[0:64, :], lamv[0:64, :], -1.0)
        mark("lamv", tV)

        @step("vector", waits=("lamv",), sem=tV)
        def _(e):
            e.tensor_copy(lam2[0:64, :], lamv[0:64, :])
            return e.tensor_copy(lam2[64:128, :], lamv[0:64, :])
        mark("lam2", tV)

        @step("scalar", waits=("lamv",), sem=tS)
        def _(e):
            e.activation(lam128[0:64, :], lamv[0:64, :], AF.Exp, scale=128.0)
            e.activation(p16v[0:64, :], tmp64[0:64, 4:20], AF.Exp,
                         scale=lamv[0:64, :])
            e.activation(p128L[0:64, :], eglt[0:64, :], AF.Exp,
                         scale=lamv[0:64, :])
            return e.activation(tmp64[0:64, 28:32], tmp64[0:64, 20:24],
                                AF.Exp, scale=lamv[0:64, :])
        mark("lams", tS)

        @step("vector", waits=("lams",), sem=tV)
        def _(e):
            e.tensor_mul(coefs[0:64, :], coefs[0:64, :], tmp64[0:64, 28:32])
            return e.tensor_scalar_mul(rw2[:, 0:8], rw2[:, 0:8], isf[:, 0:1])
        mark("coefs", tV)

        for half in range(2):
            wl = ["lams"] if half == 0 else ("pgl0",)

            @step("sync", waits=tuple(wl) if half else ("ld_params",),
                  sem=s_pgl, k=16)
            def _(e, half=half):
                return e.dma_start(
                    out=sig[0:64, :],
                    in_=eglob_h.ap()[:, half * 1024:half * 1024 + 1024])
            mark(f"pgld{half}", s_pgl)

            @step("scalar", waits=(f"pgld{half}", "lams"), sem=tS)
            def _(e, half=half):
                return e.activation(pglob[0:64, half * 1024:half * 1024 + 1024],
                                    sig[0:64, :], AF.Exp,
                                    scale=lamv[0:64, :])
            mark(f"pgl{half}", tS)
        mark("pglob_done", tS)

        s_pg_snap = s_pg.n

        @step("gpsimd", waits=("lamv",))
        def _(e):
            e.dma_start(out=lamT_d.ap(),
                        in_=lamv[0:64, :]).then_inc(s_pg.h, 16)
            e.wait_ge(s_pg.h, s_pg_snap + 16)
            e.dma_start(out=lamT16[0:1, 0:64],
                        in_=bass.AP(lamT_d, 0, [[64, 1], [1, 64]])
                        ).then_inc(s_pg.h, 16)
            return None
        s_pg.n += 32
        mark("lamT", s_pg)

        @step("vector", waits=("lamT",), sem=tV)
        def _(e):
            return e.tensor_copy(lamT16b[0:1, :], lamT16[0:1, 0:64])
        mark("lamT16b", tV)

        @step("tensor", waits=("lamT16b", "setup_g", "ld_params"), sem=tT)
        def _(e):
            return e.matmul(pbc[:, 0:64], onesr[0:1, :],
                            lamT16b[0:1, :], start=True, stop=True)
        mark("dAbc", tT)

        @step("vector", waits=("dAbc",), sem=tV)
        def _(e):
            return e.tensor_copy(lamTb[:, :], pbc[:, 0:64])
        mark("lamTb", tV)

        for c in range(2):
            cw, coff = (512, 0) if c == 0 else (64, 512)

            @step("sync", waits=(("lamTb",) if c == 0 else (f"esbx{0}",)),
                  sem=s_esb, k=16)
            def _(e, cw=cw, coff=coff):
                return e.dma_start(out=xr[2][:, 0:cw],
                                   in_=esb_h.ap()[:, coff:coff + cw])
            mark(f"esbl{c}", s_esb)

            @step("vector", waits=(f"esbl{c}", "lamTb"), sem=tV)
            def _(e, cw=cw, coff=coff):
                return e.tensor_mul(
                    xr[2][:, 0:cw], xr[2][:, 0:cw],
                    bass.AP(lamTb, coff % 64, [[64, 128], [0, 9], [1, 64]])
                    if False else
                    bass.AP(lamTb, 0, [[64, 128], [0, (cw + 63) // 64],
                                       [1, 64]]))
            mark(f"esbm{c}", tV)

            @step("scalar", waits=(f"esbm{c}",), sem=tS)
            def _(e, cw=cw, coff=coff):
                return e.activation(sbL[:, coff:coff + cw], xr[2][:, 0:cw],
                                    AF.Exp)
            mark(f"esbx{c}", tS)
        P.cp["sbL"] = P.cp["esbx1"]

        @step("scalar", waits=("lam2",), sem=tS)
        def _(e):
            return e.activation(p16bd[:, 0:64], k16[:, 0:64], AF.Exp,
                                scale=lam2[:, 0:1])
        mark("p16bd", tS)

        for dg in range(NDC):
            w = ("lams", "ld_ct", "lamTb", "p16bd") if dg == 0 else (f"k16ev{dg-1}",)

            @step("tensor", waits=w, sem=tT)
            def _(e, dg=dg):
                return e.matmul(pbc[:, 0:16],
                                ctrep[0:64, dg * 128:dg * 128 + 128],
                                p16v[0:64, :], start=True, stop=True)
            mark(f"k16mm{dg}", tT)

            @step("vector", waits=(f"k16mm{dg}",), sem=tV)
            def _(e, dg=dg):
                return e.tensor_copy(k16[:, dg * 16:dg * 16 + 16],
                                     pbc[:, 0:16])
            mark(f"k16ev{dg}", tV)

        @step("vector", waits=(f"k16ev{NDC-1}", "ld_params"), sem=tV)
        def _(e):
            return e.tensor_add(
                bass.AP(k16, 0, [[128, 128], [16, 8], [1, 1]]),
                bass.AP(k16, 0, [[128, 128], [16, 8], [1, 1]]),
                dvecv[:, 0:8])
        mark("k16d", tV)

        s_lx = [SEM(f"s_lx{i}") for i in range(3)]
        s_lx2 = [SEM(f"s_lx2{i}") for i in range(2)]
        s_xw = [SEM(f"s_xw{i}") for i in range(2)]

        def xtile(h, ti, dc):
            return bass.AP(h, (dc * 128) * NT + ti * TT, [[NT, 128], [1, TT]])

        def emit_outer_ln(tag, src, dst, src_ready, src_is_b16=False,
                          tile_ready=None, ext_stats=False,
                          bcast_apply=True, sbuf_apply=False):
            hbm = src[0] == "hbm"
            sh = src[1]
            xins = {}
            for ti in (range(NTT) if not ext_stats else ()):
                for dc in range(NDC):
                    idx = ti * NDC + dc
                    if hbm:
                        wl = list(tile_ready(ti, dc)) if tile_ready else []
                        if idx == 0 and src_ready:
                            wl.append(src_ready)
                        if idx >= 3:
                            wl.append(f"{tag}_cast{(idx-3)//NDC}_{(idx-3)%NDC}")

                        @step("sync" if idx % 2 == 0 else "scalar",
                              waits=tuple(wl), sem=s_lx[idx % 3],
                              k=16)
                        def _(e, ti=ti, dc=dc, idx=idx):
                            return e.dma_start(out=xr[idx % 3][:, :],
                                               in_=xtile(sh, ti, dc))
                        mark(f"{tag}_xl{ti}_{dc}", s_lx[idx % 3])
                        wl = [f"{tag}_xl{ti}_{dc}"]
                        if idx >= 2:
                            wl.append(f"{tag}_mm{(idx-2)//NDC}_{(idx-2)%NDC}")
                        if sbuf_apply:
                            xb = dst[:, dc * NT + ti * TT:
                                     dc * NT + ti * TT + TT]
                        else:
                            xb = xbt[dc % 2][:, :]

                        @step("scalar", waits=tuple(wl), sem=tS)
                        def _(e, idx=idx, xb=xb):
                            return e.activation(xb, xr[idx % 3][:, :],
                                                AF.Copy)
                        mark(f"{tag}_cast{ti}_{dc}", tS)
                        xin = xb
                        wv = [f"{tag}_cast{ti}_{dc}"]
                    elif not src_is_b16:
                        sl = sh[:, dc * NT + ti * TT:dc * NT + ti * TT + TT]
                        wl = [src_ready]
                        if idx >= 2:
                            wl.append(f"{tag}_mm{(idx-2)//NDC}_{(idx-2)%NDC}")
                        xb = xbt[dc % 2]

                        @step("scalar", waits=tuple(wl), sem=tS)
                        def _(e, sl=sl, xb=xb):
                            return e.activation(xb[:, :], sl, AF.Copy)
                        mark(f"{tag}_cast{ti}_{dc}", tS)
                        xin = xb[:, :]
                        wv = [f"{tag}_cast{ti}_{dc}"]
                    else:
                        xin = sh[:, dc * NT + ti * TT:dc * NT + ti * TT + TT]
                        wv = [src_ready]
                        if tile_ready:
                            wv += list(tile_ready(ti, dc))
                        if idx >= 2:
                            wv.append(f"{tag}_mm{(idx-2)//NDC}_{(idx-2)%NDC}")
                    xins[(ti, dc)] = xin
                    sq = sqb2[dc % 2]

                    @step("scalar", waits=tuple(wv), sem=tS)
                    def _(e, xin=xin, sq=sq):
                        return e.activation(sq[:, :], xin, AF.Square)
                    mark(f"{tag}_sq{ti}_{dc}", tS)

                    wl = [f"{tag}_sq{ti}_{dc}"]
                    if dc == 0 and ti > 0:
                        wl.append(f"{tag}_rows{ti-1}")

                    @step("tensor", waits=tuple(wl), sem=tT)
                    def _(e, xin=xin, sq=sq, dc=dc):
                        e.matmul(prow[0:1, :], onesc[:, 0:1], xin,
                                 start=(dc == 0), stop=(dc == NDC - 1))
                        return e.matmul(pbc[0:1, :], onesc[:, 0:1], sq[:, :],
                                        start=(dc == 0), stop=(dc == NDC - 1))
                    mark(f"{tag}_mm{ti}_{dc}", tT)

                @step("vector", waits=(f"{tag}_mm{ti}_{NDC-1}",), sem=tV)
                def _(e, ti=ti):
                    e.tensor_copy(rowS[:, ti * TT:ti * TT + TT],
                                  prow[0:1, :])
                    return e.tensor_copy(rowQ[:, ti * TT:ti * TT + TT],
                                         pbc[0:1, :])
                mark(f"{tag}_rows{ti}", tV)

            @step("scalar", waits=(f"{tag}_rows{NTT-1}",), sem=tS)
            def _(e):
                return e.activation(rowS[:, :], rowS[:, :], AF.Copy,
                                    scale=1.0 / D)
            mark(f"{tag}_fmu", tS)

            @step("vector", waits=(f"{tag}_fmu",), sem=tV)
            def _(e):
                chain(e, e.tensor_scalar(rowQ[:, :], rowQ[:, :],
                                         1.0 / D, None, OP.mult))
                chain(e, e.tensor_mul(rowV[:, :], rowS[:, :],
                                      rowS[:, :]))
                return e.tensor_sub(rowQ[:, :], rowQ[:, :],
                                    rowV[:, :])
            mark(f"{tag}_fvar", tV)

            @step("scalar", waits=(f"{tag}_fvar",), sem=tS)
            def _(e):
                return e.activation(rowQ[:, :], rowQ[:, :], AF.Ln,
                                    bias=EPS)
            mark(f"{tag}_fln", tS)

            @step("scalar", waits=(f"{tag}_fln",), sem=tS)
            def _(e):
                return e.activation(rowR16[:, :], rowQ[:, :], AF.Exp,
                                    scale=-0.5)
            mark(f"{tag}_frstd", tS)

            @step("vector", waits=(f"{tag}_frstd",), sem=tV)
            def _(e):
                chain(e, e.tensor_mul(rowS[:, :], rowS[:, :],
                                      rowR16[:, :]))
                return e.tensor_scalar(rowA16[:, :], rowS[:, :], -1.0,
                                       None, OP.mult)
            mark(f"{tag}_farow", tV)

            if not bcast_apply:
                P.cp[f"{tag}_done"] = P.cp[f"{tag}_farow"]
                return
            for ti in range(NTT):
                wl = [f"{tag}_farow"]
                if ti >= 1:
                    wl.append(f"{tag}_bcs{ti-1}")

                @step("tensor", waits=tuple(wl), sem=tT)
                def _(e, ti=ti):
                    e.matmul(pbc[:, :], onesr[0:1, :],
                             rowR16[:, ti * TT:ti * TT + TT],
                             start=True, stop=True)
                    return e.matmul(pwB[:, 0:512], onesr[0:1, :],
                                    rowA16[:, ti * TT:ti * TT + TT],
                                    start=True, stop=True)
                mark(f"{tag}_bc{ti}", tT)

                @step("vector", waits=(f"{tag}_bc{ti}",), sem=tV)
                def _(e, ti=ti):
                    e.tensor_copy(rBs[ti % 2][:, :], pbc[:, :])
                    return e.tensor_copy(aBs[ti % 2][:, :], pwB[:, 0:512])
                mark(f"{tag}_bcs{ti}", tV)

                for dc in range(NDC):
                    idx = ti * NDC + dc
                    dl = dst[:, dc * NT + ti * TT:dc * NT + ti * TT + TT]
                    t1 = t1s[dc % 2]
                    if hbm and not sbuf_apply:
                        wl = []
                        if ext_stats and tile_ready:
                            wl += list(tile_ready(ti, dc))
                        if idx >= 2:
                            wl.append(f"{tag}_ap{(idx-2)//NDC}_{(idx-2)%NDC}")

                        @step("scalar", waits=tuple(wl),
                              sem=s_lx2[idx % 2], k=16)
                        def _(e, ti=ti, dc=dc, idx=idx):
                            return e.dma_start(out=xr2[idx % 2][:, :],
                                               in_=xtile(sh, ti, dc))
                        mark(f"{tag}_xl2{ti}_{dc}", s_lx2[idx % 2])

                        @step("vector", waits=(f"{tag}_bcs{ti}",
                                               f"{tag}_xl2{ti}_{dc}"),
                              sem=tV)
                        def _(e, idx=idx, dl=dl, t1=t1, ti=ti):
                            chain(e, e.tensor_mul(t1[:, :],
                                                  xr2[idx % 2][:, :],
                                                  rBs[ti % 2][:, :]))
                            return e.tensor_add(dl, t1[:, :],
                                                aBs[ti % 2][:, :])
                        mark(f"{tag}_ap{ti}_{dc}", tV)
                    else:
                        xin = xins.get((ti, dc))
                        if xin is None:
                            xin = dl
                        t1b = xbt[dc % 2][:, :]

                        @step("vector", waits=(f"{tag}_bcs{ti}",), sem=tV)
                        def _(e, xin=xin, dl=dl, t1b=t1b, ti=ti):
                            chain(e, e.tensor_mul(t1b, xin,
                                                  rBs[ti % 2][:, :]))
                            return e.tensor_add(dl, t1b,
                                                aBs[ti % 2][:, :])
                        mark(f"{tag}_ap{ti}_{dc}", tV)
            mark(f"{tag}_done", tV)

        def emit_update(tag, in_h, out_h2, ready, kind, tile_ready=None,
                        stats_for=None, stats_gate=(), load_ready=None):
            for ti in range(NTT):
                if kind == "gb2":
                    wl = ["lnS_farow"]
                    if ti >= 1:
                        wl.append(f"lnS_bcs{ti-1}")

                    @step("tensor", waits=tuple(wl), sem=tT)
                    def _(e, ti=ti):
                        e.matmul(pbc[:, :], onesr[0:1, :],
                                 rowR16[:, ti * TT:ti * TT + TT],
                                 start=True, stop=True)
                        return e.matmul(pwB[:, 0:512], onesr[0:1, :],
                                        rowA16[:, ti * TT:ti * TT + TT],
                                        start=True, stop=True)
                    mark(f"lnS_bc{ti}", tT)

                    @step("vector", waits=(f"lnS_bc{ti}",), sem=tV)
                    def _(e, ti=ti):
                        e.tensor_copy(rBs[ti % 2][:, :], pbc[:, :])
                        return e.tensor_copy(aBs[ti % 2][:, :],
                                             pwB[:, 0:512])
                    mark(f"lnS_bcs{ti}", tV)
                for dc in range(NDC):
                    idx = ti * NDC + dc
                    wl = [ready] if (idx == 0 and ready) else []
                    if load_ready:
                        wl += list(load_ready(ti, dc))
                    if idx >= 3:
                        wl.append(f"{tag}_v{(idx-3)//NDC}_{(idx-3)%NDC}")

                    @step("sync" if idx % 2 == 0 else "scalar",
                          waits=tuple(wl), sem=s_lx[idx % 3], k=16)
                    def _(e, ti=ti, dc=dc, idx=idx):
                        return e.dma_start(out=xr[idx % 3][:, :],
                                           in_=xtile(in_h, ti, dc))
                    mark(f"{tag}_l{ti}_{dc}", s_lx[idx % 3])

                    wl = [f"{tag}_l{ti}_{dc}"]
                    if kind == "gb2":
                        wl.append(f"lnS_bcs{ti}")
                    if tile_ready:
                        wl += list(tile_ready(ti, dc))
                    if idx >= 2:
                        wl.append(f"{tag}_s{(idx-2)//NDC}_{(idx-2)%NDC}")
                        if stats_for:
                            wl.append(
                                f"{stats_for}_cast"
                                f"{(idx-2)//NDC}_{(idx-2)%NDC}")

                    @step("vector", waits=tuple(wl), sem=tV)
                    def _(e, ti=ti, dc=dc, idx=idx):
                        sl = bigA[:, dc * NT + ti * TT:dc * NT + ti * TT + TT]
                        o = xw[idx % 2]
                        if kind == "half":
                            return e.scalar_tensor_tensor(
                                o[:, :], sl, 0.5, xr[idx % 3][:, :],
                                OP.mult, OP.add)
                        t1 = t1s[dc % 2]
                        chain(e, e.scalar_tensor_tensor(
                            t1[:, :], sl, gamv[:, dc:dc + 1],
                            rBs[ti % 2][:, :], OP.mult, OP.mult))
                        chain(e, e.scalar_tensor_tensor(
                            t1[:, :], aBs[ti % 2][:, :],
                            gamv[:, dc:dc + 1], t1[:, :],
                            OP.mult, OP.add))
                        return e.scalar_tensor_tensor(
                            o[:, :], t1[:, :], betv[:, dc:dc + 1],
                            xr[idx % 3][:, :], OP.add, OP.add)
                    mark(f"{tag}_v{ti}_{dc}", tV)

                    @step("gpsimd", waits=(f"{tag}_v{ti}_{dc}",),
                          sem=s_xw[idx % 2], k=16)
                    def _(e, ti=ti, dc=dc, idx=idx):
                        return e.dma_start(out=xtile(out_h2, ti, dc),
                                           in_=xw[idx % 2][:, :])
                    mark(f"{tag}_s{ti}_{dc}", s_xw[idx % 2])

                    if stats_for:
                        wl = [f"{tag}_v{ti}_{dc}"]
                        if idx >= 2:
                            wl.append(
                                f"{stats_for}_mm"
                                f"{(idx-2)//NDC}_{(idx-2)%NDC}")

                        csl = bigA[:, dc * NT + ti * TT:
                                   dc * NT + ti * TT + TT]

                        @step("scalar", waits=tuple(wl), sem=tS)
                        def _(e, idx=idx, csl=csl):
                            return e.activation(csl, xw[idx % 2][:, :],
                                                AF.Copy)
                        mark(f"{stats_for}_cast{ti}_{dc}", tS)

                        @step("scalar",
                              waits=(f"{stats_for}_cast{ti}_{dc}",),
                              sem=tS)
                        def _(e, dc=dc, csl=csl):
                            return e.activation(sqb2[dc % 2][:, :], csl,
                                                AF.Square)
                        mark(f"{stats_for}_sq{ti}_{dc}", tS)

                        wl = [f"{stats_for}_sq{ti}_{dc}"]
                        if dc == 0 and ti > 0:
                            wl.append(f"{stats_for}_rows{ti-1}")
                        if idx == 0:
                            wl += list(stats_gate)

                        @step("tensor", waits=tuple(wl), sem=tT)
                        def _(e, dc=dc, csl=csl):
                            e.matmul(pmm[0][0:1, :], onesc[:, 0:1],
                                     csl,
                                     start=(dc == 0), stop=(dc == NDC - 1))
                            return e.matmul(
                                pmm[1][0:1, :], onesc[:, 0:1],
                                sqb2[dc % 2][:, :],
                                start=(dc == 0), stop=(dc == NDC - 1))
                        mark(f"{stats_for}_mm{ti}_{dc}", tT)
                if stats_for:
                    @step("vector", waits=(f"{stats_for}_mm{ti}_{NDC-1}",),
                          sem=tV)
                    def _(e, ti=ti):
                        e.tensor_copy(rowS[:, ti * TT:ti * TT + TT],
                                      pmm[0][0:1, :])
                        return e.tensor_copy(rowQ[:, ti * TT:ti * TT + TT],
                                             pmm[1][0:1, :])
                    mark(f"{stats_for}_rows{ti}", tV)
            @step("sync", waits=(f"{tag}_s{NTT-1}_{NDC-2}",
                                 f"{tag}_s{NTT-1}_{NDC-1}"))
            def _(e):
                return None
            P.cp[f"{tag}_done"] = P.cp[f"{tag}_s{NTT-1}_{NDC-1}"]
            P.cp[f"{tag}_done2"] = P.cp[f"{tag}_s{NTT-1}_{NDC-2}"]

        def emit_prep_layer(l):
            buf = wsb[l % 2]

            qg_prev = s_qg.n
            _lw = ["ld_params"]
            if l > 0:
                _lw.append(f"prep_lora{l-1}_{NDC-1}")

            @step("gpsimd", waits=tuple(_lw))
            def _(e, l=l, qg_prev=qg_prev):
                e.wait_ge(s_qg.h, qg_prev)
                e.dma_start(out=lbb[:, :],
                            in_=lbT_h.ap()[l]).then_inc(s_qg.h, 16)
                return None
            s_qg.n += 16
            mark(f"lora_ld{l}", s_qg)
            for dc in range(NDC):
                qbb, scbb = qb_sl[dc % 2], scb_sl[dc % 2]
                wl = []
                if dc >= 2:
                    wl.append(f"prep_asm{l}_{dc-2}")
                    wl.append(f"prep_lora{l}_{dc-2}")
                elif l > 0:
                    wl.append(f"prep_asm{l-1}_{NDC-2+dc}")
                    wl.append(f"prep_lora{l-1}_{NDC-2+dc}")

                sem2 = s_qg2[dc % 2]

                @step("gpsimd", waits=tuple(wl))
                def _(e, l=l, dc=dc, qbb=qbb, scbb=scbb, sem2=sem2):
                    e.dma_start(out=qbb,
                                in_=qT_h.ap()[l, dc * 128:dc * 128 + 128, :]
                                ).then_inc(sem2.h, 16)
                    e.dma_start(
                        out=scbb,
                        in_=bass.AP(scT_h, l * 64 * D + dc * 8 * D,
                                    [[D, 8], [0, 16], [1, D]])
                        ).then_inc(sem2.h, 16)
                    e.dma_start(
                        out=lab[dc % 2][:, :],
                        in_=la_h.ap()[l, :, dc * 128:dc * 128 + 128]
                        ).then_inc(sem2.h, 16)
                    return None
                sem2.n += 48
                mark(f"prep_ld{l}_{dc}", sem2)

                wl = [f"lora_ld{l}", f"prep_ld{l}_{dc}"]
                if dc >= 1:
                    wl.append(f"prep_asm{l}_{dc-1}")
                elif l > 0:
                    wl.append(f"prep_asm{l-1}_{NDC-1}")

                @step("tensor", waits=tuple(wl), sem=tT)
                def _(e, dc=dc):
                    e.matmul(pwA[:, 0:512], lab[dc % 2][:, :],
                             lbb[:, 0:512], start=True, stop=True)
                    return e.matmul(pwA[:, 512:1024],
                                    lab[dc % 2][:, :],
                                    lbb[:, 512:1024], start=True, stop=True)
                mark(f"prep_lora{l}_{dc}", tT)

                @step("scalar", waits=(f"prep_ld{l}_{dc}",), sem=tS)
                def _(e, qbb=qbb):
                    return e.activation(qbb, qbb, AF.Copy,
                                        scale=1.0 / 7.5, bias=-1.0)
                mark(f"prep_qf{l}_{dc}", tS)

                wl = [f"prep_qf{l}_{dc}", f"prep_lora{l}_{dc}"]
                if l >= 2 and dc == 0:
                    wl.append(f"ffn_wfree{l-2}")
                    wl.append(f"wscr{l-2}")

                @step("vector", waits=tuple(wl), sem=tV)
                def _(e, l=l, dc=dc, buf=buf, qbb=qbb, scbb=scbb):
                    chain(e, e.tensor_mul(qbb, qbb, scbb))
                    sl = buf[:, dc * 1024:dc * 1024 + 1024]
                    last = e.tensor_add(sl, qbb, pwA[:, :])
                    if l % 2 == 0:
                        chain(e, last)
                        chain(e, e.tensor_reduce(wrow[:, :], sl, AX.X,
                                                 OP.add))
                        chain(e, e.tensor_scalar(wrow[:, :], wrow[:, :],
                                                 1.0 / D, None, OP.mult))
                        last = e.tensor_scalar(sl, sl, wrow[:, 0:1],
                                               None, OP.subtract)
                    return last
                mark(f"prep_asm{l}_{dc}", tV)

            P.cp[f"wready{l}_A"] = P.cp[f"prep_asm{l}_{NDC-1}"]

            sw_prev = s_ws.n

            @step("sync", waits=(f"prep_asm{l}_{NDC-1}",), sem=s_ws, k=16)
            def _(e, l=l, buf=buf, sw_prev=sw_prev):
                e.wait_ge(s_ws.h, sw_prev)
                return e.dma_start(
                    out=bass.AP(wscr, l * D * D,
                                [[D, 128], [128 * D, 8], [1, D]]),
                    in_=buf[:, :])
            mark(f"wscr{l}", s_ws)

        def emit_stream_layer(l):
            buf = wsb[l % 2]
            wl = [f"wscr{l}", "wscr5", f"ffn_wfree{l+4}" if l < 2 else
                  f"ffnC_wfree{l-2}"]

            sw_prev = s_w.n

            @step("sync", waits=tuple(wl), sem=s_w, k=16)
            def _(e, l=l, buf=buf, sw_prev=sw_prev):
                e.wait_ge(s_w.h, sw_prev)
                return e.dma_start(
                    out=buf[:, 0:4096],
                    in_=bass.AP(wscr, l * D * D,
                                [[D, 128], [128 * D, 4], [1, D]]))
            mark(f"wready{l}_C", s_w)

            swg_prev = s_wg.n

            @step("gpsimd", waits=tuple(wl))
            def _(e, l=l, buf=buf, swg_prev=swg_prev):
                e.wait_ge(s_wg.h, swg_prev)
                e.dma_start(
                    out=buf[:, 4096:8192],
                    in_=bass.AP(wscr, l * D * D + 4 * 128 * D,
                                [[D, 128], [128 * D, 4], [1, D]])
                    ).then_inc(s_wg.h, 16)
                return None
            s_wg.n += 16
            mark(f"wready{l}_Cg", s_wg)

        def emit_ffn(call, hook=None):
            tag = f"f{call}"
            m2b = [pwB[:, 0:512], pwB[:, 512:1024]]

            def gb2(ti):
                if ti % 2 == 0:
                    return bigB[:, 4096:8192]
                return gbodd[:, :]

            def emit_mm1(blk, ti):
                i0 = 2 * blk
                for oc in range(NDC):
                    idx = ti * NDC + oc
                    wl = [f"wready{i0}_{call}"]
                    if call == "C":
                        wl.append(f"wready{i0}_Cg")
                    elif idx == 0:
                        wl.append(f"prep_bce{i0}")
                    if blk == 0:
                        wl.append(f"{tag}_hready{ti}")
                    else:
                        wl.append(f"{tag}{blk-1}_h{ti}")
                    if oc >= 2:
                        wl.append(f"{tag}{blk}_e1_{idx-2}")
                    elif ti >= 1:
                        wl.append(f"{tag}{blk}_e1_{(ti-1)*NDC+6+oc}")
                    elif blk > 0:
                        wl.append(f"{tag}{blk-1}_e1_{3*NDC+6+oc}")

                    @step("tensor", waits=tuple(wl), sem=tT)
                    def _(e, i0=i0, ti=ti, oc=oc, idx=idx):
                        buf = wsb[i0 % 2]
                        bank = pmm[idx % 2]
                        last = None
                        for dc in range(NDC):
                            last = e.matmul(
                                bank[:, :],
                                buf[:, dc * 1024 + oc * 128:
                                    dc * 1024 + oc * 128 + 128],
                                bigA[:, dc * NT + ti * TT:
                                     dc * NT + ti * TT + TT],
                                start=(dc == 0), stop=(dc == NDC - 1))
                        return last
                    mark(f"{tag}{blk}_m1_{idx}", tT)

                    wl = [f"{tag}{blk}_m1_{idx}"]
                    if oc == 0 and ti >= 2:
                        wl.append(f"{tag}{blk}_gin{ti-2}")
                    elif oc == 0 and blk > 0 and ti < 2:
                        wl.append(f"{tag}{blk-1}_gin{ti+2}")

                    @step("scalar", waits=tuple(wl), sem=tS)
                    def _(e, i0=i0, ti=ti, oc=oc, idx=idx):
                        return e.activation(
                            y1c(ti)[:, oc * TT:oc * TT + TT],
                            pmm[idx % 2][:, :], AF.Identity,
                            bias=biasv[:, i0 * 8 + oc:i0 * 8 + oc + 1])
                    mark(f"{tag}{blk}_e1_{idx}", tS)

            def emit_stats(blk, ti):
                gb = gb2(ti)
                wl = [f"{tag}{blk}_e1_{ti*NDC+NDC-1}"]
                if ti >= 2:
                    wl.append(f"{tag}{blk}_m2_{(ti-2)*NDC+NDC-1}")
                elif blk > 0:
                    wl.append(f"{tag}{blk-1}_m2_{(ti+2)*NDC+NDC-1}")

                @step("vector", waits=tuple(wl), sem=tV)
                def _(e, ti=ti, gb=gb):
                    return e.tensor_mul(gb, y1c(ti), y1c(ti))
                mark(f"{tag}{blk}_sq{ti}", tV)

                @step("tensor", waits=(f"{tag}{blk}_sq{ti}",), sem=tT)
                def _(e, gb=gb):
                    last = None
                    for gc in range(NDC):
                        last = e.matmul(prow[0:1, :], onesc[:, 0:1],
                                        gb[:, gc * TT:gc * TT + TT],
                                        start=(gc == 0),
                                        stop=(gc == NDC - 1))
                    return last
                mark(f"{tag}{blk}_smm{ti}", tT)

                @step("scalar", waits=(f"{tag}{blk}_smm{ti}",), sem=tS)
                def _(e):
                    return e.activation(rowB[0:1, :], prow[0:1, :], AF.Ln,
                                        scale=1.0 / D, bias=EPS)
                mark(f"{tag}{blk}_lnr{ti}", tS)

                @step("scalar", waits=(f"{tag}{blk}_lnr{ti}",), sem=tS)
                def _(e):
                    return e.activation(rowC[0:1, :], rowB[0:1, :],
                                        AF.Exp, scale=-0.5)
                mark(f"{tag}{blk}_rst{ti}", tS)

                @step("tensor", waits=(f"{tag}{blk}_rst{ti}",), sem=tT)
                def _(e):
                    return e.matmul(pbc[:, :], onesr[0:1, :],
                                    rowC[0:1, :], start=True, stop=True)
                mark(f"{tag}{blk}_rbc{ti}", tT)

                @step("vector", waits=(f"{tag}{blk}_rbc{ti}",), sem=tV)
                def _(e, ti=ti):
                    return e.tensor_copy(rstdB[ti % 2][:, :], pbc[:, :])
                mark(f"{tag}{blk}_rcp{ti}", tV)

                @step("vector", waits=(f"{tag}{blk}_rcp{ti}",), sem=tV)
                def _(e, ti=ti, gb=gb):
                    return e.tensor_mul(
                        gb, y1c(ti),
                        bass.AP(rstdB[ti % 2], 0,
                                [[TT, 128], [0, NDC], [1, TT]]))
                mark(f"{tag}{blk}_gin{ti}", tV)

                for oc in range(NDC):
                    @step("scalar", waits=(f"{tag}{blk}_gin{ti}",), sem=tS)
                    def _(e, blk=blk, oc=oc, gb=gb):
                        sl = gb[:, oc * TT:oc * TT + TT]
                        return e.activation(
                            sl, sl, AF.Gelu,
                            scale=lngv[:, blk * 8 + oc:blk * 8 + oc + 1],
                            bias=lnbv[:, blk * 8 + oc:blk * 8 + oc + 1])
                    mark(f"{tag}{blk}_gel{ti}_{oc}", tS)

            def emit_mm2(blk, ti):
                i1 = 2 * blk + 1
                gb = gb2(ti)
                for oc in range(NDC):
                    idx = ti * NDC + oc
                    wl = ([f"wready{i1}_Cg"] if call == "C" else []) + [
                        f"wready{i1}_{call}",
                          f"{tag}{blk}_gel{ti}_{NDC-1}"]
                    if oc >= 2:
                        wl.append(f"{tag}{blk}_e2_{idx-2}")
                    elif ti >= 1:
                        wl.append(f"{tag}{blk}_e2_{(ti-1)*NDC+6+oc}")
                    elif blk > 0:
                        wl.append(f"{tag}{blk-1}_e2_{3*NDC+6+oc}")

                    @step("tensor", waits=tuple(wl), sem=tT)
                    def _(e, i1=i1, oc=oc, gb=gb):
                        buf = wsb[i1 % 2]
                        last = None
                        for gc in range(NDC):
                            last = e.matmul(
                                m2b[oc % 2],
                                buf[:, gc * 1024 + oc * 128:
                                    gc * 1024 + oc * 128 + 128],
                                gb[:, gc * TT:gc * TT + TT],
                                start=(gc == 0), stop=(gc == NDC - 1))
                        return last
                    mark(f"{tag}{blk}_m2_{idx}", tT)

                    @step("vector", waits=(f"{tag}{blk}_m2_{idx}",), sem=tV)
                    def _(e, i1=i1, ti=ti, oc=oc):
                        hl = bigA[:, oc * NT + ti * TT:oc * NT + ti * TT + TT]
                        return e.scalar_tensor_tensor(
                            hl, m2b[oc % 2],
                            biasv[:, i1 * 8 + oc:i1 * 8 + oc + 1],
                            hl, OP.add, OP.add)
                    mark(f"{tag}{blk}_e2_{idx}", tV)
                mark(f"{tag}{blk}_h{ti}", tV)

            for blk in range(3):
                i0, i1 = 2 * blk, 2 * blk + 1
                emit_mm1(blk, 0)
                emit_mm1(blk, 1)
                emit_stats(blk, 0)
                emit_mm1(blk, 2)
                emit_stats(blk, 1)
                emit_mm1(blk, 3)
                emit_mm2(blk, 0)
                emit_stats(blk, 2)
                emit_mm2(blk, 1)
                emit_stats(blk, 3)
                emit_mm2(blk, 2)
                emit_mm2(blk, 3)
                key = "ffn_wfree" if call == "A" else "ffnC_wfree"
                P.cp[f"{key}{i0}"] = P.cp[f"{tag}{blk}_m1_{NTT*NDC-1}"]
                P.cp[f"{key}{i1}"] = P.cp[f"{tag}{blk}_m2_{NTT*NDC-1}"]
                if hook:
                    hook(blk)
                if stop_after == f"{tag}{blk}":
                    P.stop()
                    break
            mark(f"{tag}_done", tV)


        def emit_bias_center(l):
            _bw = ["ld_params"]
            if l >= 2:
                _bw.append(f"prep_bcb{l-2}")

            @step("scalar", waits=tuple(_bw), sem=tS)
            def _(e, l=l):
                return e.activation(bb16[:, :],
                                    biasv[:, l * 8:l * 8 + 8],
                                    AF.Copy)
            mark(f"prep_bcc{l}", tS)

            @step("tensor", waits=(f"prep_bcc{l}", "setup_g"), sem=tT)
            def _(e):
                return e.matmul(pwB[0:1, 512:520], onesc[:, 0:1],
                                bb16[:, :], start=True, stop=True)
            mark(f"prep_bcm{l}", tT)

            @step("vector", waits=(f"prep_bcm{l}",), sem=tV)
            def _(e):
                chain(e, e.tensor_reduce(tmp64[0:1, 30:31],
                                         pwB[0:1, 512:520], AX.X,
                                         OP.add))
                chain(e, e.tensor_scalar(tmp64[0:1, 30:31],
                                         tmp64[0:1, 30:31],
                                         1.0 / D, None, OP.mult))
                return e.tensor_copy(bb16[0:1, 0:1], tmp64[0:1, 30:31])
            mark(f"prep_bcr{l}", tV)

            @step("tensor", waits=(f"prep_bcr{l}",), sem=tT)
            def _(e):
                return e.matmul(pwB[:, 520:521], onesr[0:1, :],
                                bb16[0:1, 0:1], start=True, stop=True)
            mark(f"prep_bcb{l}", tT)

            @step("vector", waits=(f"prep_bcb{l}",), sem=tV)
            def _(e, l=l):
                chain(e, e.tensor_copy(bmean[:, :], pwB[:, 520:521]))
                return e.tensor_scalar(biasv[:, l * 8:l * 8 + 8],
                                       biasv[:, l * 8:l * 8 + 8],
                                       bmean[:, 0:1], None, OP.subtract)
            mark(f"prep_bce{l}", tV)

        for _l in (0, 2, 4):
            emit_bias_center(_l)

        emit_prep_layer(0)
        emit_outer_ln("ln1", ("hbm", u_h), bigA, "ld_params",
                      sbuf_apply=True)
        for ti in range(NTT):
            P.cp[f"fA_hready{ti}"] = P.cp[f"ln1_ap{ti}_{NDC-1}"]
        if stop_after == "ln1":
            P.stop()
        emit_prep_layer(1)

        def hookA(blk):
            if blk == 0:
                emit_prep_layer(2)
                emit_prep_layer(3)
            elif blk == 1:
                emit_prep_layer(4)
                emit_prep_layer(5)
        emit_ffn("A", hookA)

        emit_update("x1", u_h, xscr, None, "half",
                    tile_ready=lambda ti, dc: [f"fA2_e2_{ti*NDC+dc}"],
                    stats_for="ln2", stats_gate=("fA2_e1_31",))
        if stop_after == "x1":
            P.stop()

        emit_outer_ln("ln2", ("hbm", xscr), bigA, None,
                      ext_stats=True, sbuf_apply=True)

        if stop_after == "ln2":
            P.stop()

        @step("vector", waits=("ln2_done", "pglob_done"), sem=tV)
        def _(e):
            e.memset(sig[0:64, :], 0.0)
            return e.memset(sinit[0:64, :], 0.0)
        mark("sig0", tV)

        for a in range(NA):
            ti = (a * TA) // TT
            wl = [f"ln2_ap{ti}_{dg}" for dg in range(NDC)]
            if a >= 3:
                wl.append(f"sbmm{a-3}")

            @step("sync", waits=tuple(wl))
            def _(e, a=a):
                for dg in range(4):
                    e.dma_start(
                        out=utm[a % 3][:, dg * 128:dg * 128 + 128],
                        in_=bigA[:, dg * NT + a * TA:dg * NT + a * TA + TA],
                        transpose=True).then_inc(s_xb[a % 3].h, 16)
                return None

            @step("scalar", waits=tuple(wl))
            def _(e, a=a):
                for dg in range(4, NDC):
                    e.dma_start(
                        out=utm[a % 3][:, dg * 128:dg * 128 + 128],
                        in_=bigA[:, dg * NT + a * TA:dg * NT + a * TA + TA],
                        transpose=True).then_inc(s_xb[a % 3].h, 16)
                return None
            s_xb[a % 3].n += 16 * NDC
            mark(f"xbar{a}", s_xb[a % 3])

            @step("vector", waits=(("sig0",) if a == 0 else
                                   (f"scan{a-1}", f"pfl{a-1}")), sem=tV)
            def _(e):
                return e.tensor_copy(sigb[0:64, :], sig[0:64, :])
            mark(f"sigb{a}", tV)

            def emit_sbp(a, p):
                pw = (pwA, pwB)[p % 2]
                wl = [f"xbar{a}", "sbL"]
                if a == 0:
                    wl.append("ln2_done")
                if p >= 2:
                    wl.append(f"cvev{a}_{p-2}")
                elif a > 0:
                    wl.append(f"cvev{a-1}_{4-p}")

                @step("tensor", waits=tuple(wl), sem=tT)
                def _(e, a=a, p=p, pw=pw):
                    if p < 4:
                        lhs = sbL[:, p * 128:p * 128 + 128]
                        e.matmul(pw[:, 0:512], lhs, utm[a % 3][:, 0:512],
                                 start=True, stop=True)
                        return e.matmul(pw[:, 512:1024], lhs,
                                        utm[a % 3][:, 512:1024],
                                        start=True, stop=True)
                    lhs = sbL[:, 512:576]
                    e.matmul(pw[0:64, 0:512], lhs, utm[a % 3][:, 0:512],
                             start=True, stop=True)
                    return e.matmul(pw[0:64, 512:1024], lhs,
                                    utm[a % 3][:, 512:1024],
                                    start=True, stop=True)
                mark(f"sbp{a}_{p}", tT)

                wl = [f"sbp{a}_{p}", "ld_ct"]
                if p < 4:
                    if p >= 2:
                        wl.append(f"ptau{a}_{p-2}")
                    elif a > 0:
                        wl.append(f"ptau{a-1}_{p+2}")

                    @step("vector", waits=tuple(wl), sem=tV)
                    def _(e, p=p, pw=pw):
                        return e.tensor_mul(
                            cv2[:, (p % 2) * 1024:(p % 2) * 1024 + 1024],
                            pw[:, :], ctrep[:, :])
                    mark(f"cvev{a}_{p}", tV)
                else:
                    @step("vector", waits=tuple(wl), sem=tV)
                    def _(e, pw=pw):
                        return e.tensor_mul(cvqa[0:64, :], pw[0:64, :],
                                            ctrep[0:64, :])
                    mark(f"cvev{a}_{p}", tV)

            def emit_pfl(a):
                wl = [f"sigb{a}", "lams"]
                if a >= 1:
                    wl.append(f"yfe{a-1}")

                @step("tensor", waits=tuple(wl), sem=tT)
                def _(e):
                    e.matmul(pmm[0][:, :], p128L[0:64, :], sigb[0:64, 0:512],
                             start=True, stop=False, skip_group_check=True)
                    return e.matmul(pmm[1][:, :], p128L[0:64, :],
                                    sigb[0:64, 512:1024], start=True,
                                    stop=False, skip_group_check=True)
                mark(f"pfl{a}", tT)

            def emit_ptau(a, p):
                @step("tensor", waits=(f"cvev{a}_{p}", "p16bd"), sem=tT)
                def _(e, p=p):
                    lhs = p16bd[:, 0:32] if p == 0 else p16bd[:, 32:64]
                    cvs = cv2[:, (p % 2) * 1024:(p % 2) * 1024 + 1024]
                    e.matmul(pmm[0][32 * p:32 * p + 32, :], lhs,
                             cvs[:, 0:512], start=False, stop=True,
                             tile_position=(0, 32 * p),
                             skip_group_check=True)
                    return e.matmul(pmm[1][32 * p:32 * p + 32, :], lhs,
                                    cvs[:, 512:1024], start=False,
                                    stop=True,
                                    tile_position=(0, 32 * p),
                                    skip_group_check=True)
                mark(f"ptau{a}_{p}", tT)

            emit_sbp(a, 0)
            emit_sbp(a, 1)
            emit_pfl(a)
            emit_ptau(a, 0)
            emit_sbp(a, 2)
            emit_ptau(a, 1)
            emit_sbp(a, 3)
            emit_ptau(a, 2)
            emit_sbp(a, 4)
            emit_ptau(a, 3)
            mark(f"sbmm{a}", tT)

            @step("vector", waits=(f"cvev{a}_4", f"sigb{a}"), sem=tV)
            def _(e):
                return e.scalar_tensor_tensor(sig[0:64, :], sig[0:64, :],
                                              lam128[0:64, 0:1],
                                              cvqa[0:64, :], OP.mult, OP.add)
            mark(f"scan{a}", tV)

            @step("scalar", waits=(f"ptau{a}_3",), sem=tS)
            def _(e, a=a):
                e.activation(bigB[:, a * 1024:a * 1024 + 512], pmm[0][:, :],
                             AF.Copy)
                return e.activation(bigB[:, a * 1024 + 512:a * 1024 + 1024],
                                    pmm[1][:, :], AF.Copy)
            mark(f"yfe{a}", tS)

        if stop_after == "pass1":
            P.stop()
        def emit_taps(dg):
            pbufs = [xbt[0], xbt[1], sqb2[0], sqb2[1]]
            wl = ["ln2_done"] if dg == 0 else [f"tapm{dg-1}_1"]

            @step("vector", waits=tuple(wl), sem=tV)
            def _(e, dg=dg):
                last = None
                for w in range(4):
                    last = e.tensor_copy(
                        pbufs[w][:, :],
                        bass.AP(bigA, dg * NT + w * 512,
                                [[NDC * NT, 128], [1, 16], [16, 32]]))
                return last
            mark(f"perm{dg}", tV)
            for kb in range(2):
                wl = ["k16d", "ld_params"]
                if kb == 1:
                    wl.append(f"tapm{dg}_0")
                elif dg > 0:
                    wl.append(f"tapm{dg-1}_1")

                @step("vector", waits=tuple(wl), sem=tV)
                def _(e, dg=dg, kb=kb):
                    last = None
                    for k8 in range(8):
                        k = kb * 8 + k8
                        last = e.tensor_scalar(
                            tdiag[:, k8 * 128:k8 * 128 + 128], ident[:, :],
                            k16[:, dg * 16 + k:dg * 16 + k + 1], None,
                            OP.mult)
                    return last
                mark(f"tdiag{dg}_{kb}", tV)

                wl = [f"tdiag{dg}_{kb}", f"perm{dg}"]
                if dg == 0 and kb == 0:
                    wl.append(f"cvev{NA-1}_4")
                    wl.append(f"scan{NA-1}")
                if kb == 0 and dg > 0:
                    wl.append(f"mrg{dg-1}_3")

                @step("tensor", waits=tuple(wl), sem=tT)
                def _(e, dg=dg, kb=kb):
                    last = None
                    for k8 in range(8):
                        k = kb * 8 + k8
                        lhs = tdiag[:, k8 * 128:k8 * 128 + 128]
                        for w in range(4):
                            pw = (pwA, pwB)[w // 2]
                            col0 = (w % 2) * 512
                            out = pw[:, col0 + 32 * k:col0 + 512]
                            rhs = pbufs[w][:, 0:512 - 32 * k]
                            last = e.matmul(out, lhs, rhs, start=(k == 0),
                                            stop=False)
                    if kb == 1:
                        for w in range(4):
                            pw = (pwA, pwB)[w // 2]
                            col0 = (w % 2) * 512
                            last = e.matmul(pw[:, col0:col0 + 512],
                                            zrow[0:1, :], zrow512[0:1, :],
                                            start=False, stop=True)
                    return last
                mark(f"tapm{dg}_{kb}", tT)
            P.cp[f"tapm{dg}"] = P.cp[f"tapm{dg}_1"]

        def emit_merge(dg):
            for w in range(4):
                wl = [f"tapm{dg}", f"gfe{NA-1}"]
                if w >= 2:
                    wl.append(f"mrg{dg}_{w-2}")
                elif dg > 0:
                    wl.append(f"mrg{dg-1}_{w+2}")

                @step("tensor", waits=tuple(wl), sem=tT)
                def _(e, dg=dg, w=w):
                    last = None
                    for j in range(4):
                        a = 4 * w + j
                        last = e.matmul(
                            pmm[w % 2][:, j * 128:j * 128 + 128],
                            bigB[:, a * 1024 + dg * 128:
                                 a * 1024 + dg * 128 + 128],
                            ident[:, :], start=True, stop=True)
                    return last
                mark(f"trans{dg}_{w}", tT)

                @step("vector", waits=(f"trans{dg}_{w}",), sem=tV)
                def _(e, dg=dg, w=w):
                    pw = (pwA, pwB)[w // 2]
                    col0 = (w % 2) * 512
                    t1 = t1s[w % 2]
                    chain(e, e.tensor_scalar(t1[:, :], pmm[w % 2][:, :],
                                             fav[:, dg:dg + 1], None,
                                             OP.mult))
                    unp = bass.AP(pw, col0, [[1024, 128], [1, 32], [32, 16]])
                    return e.scalar_tensor_tensor(
                        bigA[:, dg * NT + w * 512:dg * NT + w * 512 + 512],
                        unp, fav[:, dg:dg + 1], t1[:, :], OP.mult, OP.add)
                mark(f"mrg{dg}_{w}", tV)

        emit_taps(0)

        @step("gpsimd", waits=(f"scan{NA-1}",), sem=s_pg, k=16)
        def _(e):
            return e.dma_start(out=cc_in.ap(), in_=sig[0:64, :])
        mark("ccin", s_pg)

        @step("gpsimd", waits=("ccin",), sem=s_cc, k=1)
        def _(e):
            return e.collective_compute(
                "AllGather", OP.bypass,
                replica_groups=[[0, 1, 2, 3], [4, 5, 6, 7]],
                ins=[cc_in.ap().opt()], outs=[cc_out.ap().opt()])
        mark("cc", s_cc)

        for m in range(3):
            @step("gpsimd", waits=(("cc",) if m == 0 else (f"sacc{m-1}",)),
                  sem=s_pg, k=16)
            def _(e, m=m):
                return e.dma_start(out=cvqa[0:64, :],
                                   in_=cc_out.ap()[m * 64:m * 64 + 64, :])
            mark(f"peer{m}", s_pg)

            @step("vector", waits=(f"peer{m}", "coefs"), sem=tV)
            def _(e, m=m):
                return e.scalar_tensor_tensor(
                    sinit[0:64, :], cvqa[0:64, :],
                    coefs[0:64, m:m + 1], sinit[0:64, :], OP.mult, OP.add)
            mark(f"sacc{m}", tV)

        @step("vector", waits=("sacc2",), sem=tV)
        def _(e):
            return e.tensor_copy(sinitb[0:64, :], sinit[0:64, :])
        mark("sinitb", tV)

        def emit_global():
          for a in range(NA):
            wl = ["lams", "sinitb", f"yfe{NA-1}"]
            if a >= 1:
                wl.append(f"gfeA{a-1}")

            @step("tensor", waits=tuple(wl), sem=tT)
            def _(e, a=a):
                return e.matmul(pmm[0][:, :],
                                pglob[0:64, a * 128:a * 128 + 128],
                                sinitb[0:64, 0:512], start=True, stop=True)
            mark(f"pgA{a}", tT)

            wl = ["lams", "sinitb", f"yfe{NA-1}"]
            if a >= 1:
                wl.append(f"gfe{a-1}")

            @step("tensor", waits=tuple(wl), sem=tT)
            def _(e, a=a):
                return e.matmul(pmm[1][:, :],
                                pglob[0:64, a * 128:a * 128 + 128],
                                sinitb[0:64, 512:1024], start=True,
                                stop=True)
            mark(f"pgB{a}", tT)

            @step("vector", waits=(f"pgA{a}",), sem=tV)
            def _(e, a=a):
                sl = bigB[:, a * 1024:a * 1024 + 512]
                return e.scalar_tensor_tensor(sl, pmm[0][:, :], 1.0,
                                              sl, OP.mult, OP.add)
            mark(f"gfeA{a}", tV)

            @step("vector", waits=(f"pgB{a}",), sem=tV)
            def _(e, a=a):
                s2 = bigB[:, a * 1024 + 512:a * 1024 + 1024]
                return e.scalar_tensor_tensor(s2, pmm[1][:, :], 1.0, s2,
                                              OP.mult, OP.add)
            mark(f"gfe{a}", tV)

        emit_global()
        mark("yfar_done", tV)
        if stop_after == "yfar":
            P.stop()
        emit_merge(0)
        for dg in range(1, NDC):
            emit_taps(dg)
            emit_merge(dg)
        mark("y2_done", tV)

        @step("vector", waits=("y2_done", "coefs"), sem=tV)
        def _(e):
            last = None
            for dg in range(NDC):
                last = e.tensor_add(bigA[:, dg * NT:dg * NT + 1],
                                    bigA[:, dg * NT:dg * NT + 1],
                                    rw2[:, dg:dg + 1])
            return last
        mark("rw_done", tV)

        if stop_after == "y2":
            P.stop()
        emit_outer_ln("lnS", ("sbuf", bigA), bigA, "rw_done",
                      src_is_b16=True, bcast_apply=False)
        emit_update("x2", xscr, xscr, None, "gb2",
                    stats_for="ln3", stats_gate=(f"lnS_rows{NTT-1}",))

        if stop_after == "x2":
            P.stop()
        emit_stream_layer(0)
        emit_stream_layer(1)
        emit_outer_ln("ln3", ("hbm", xscr), bigA, None,
                      ext_stats=True, sbuf_apply=True)
        for ti in range(NTT):
            P.cp[f"fC_hready{ti}"] = P.cp[f"ln3_ap{ti}_{NDC-1}"]

        def hookC(blk):
            if blk == 0:
                emit_stream_layer(2)
                emit_stream_layer(3)
            elif blk == 1:
                emit_stream_layer(4)
                emit_stream_layer(5)
        emit_ffn("C", hookC)

        emit_update("fout", xscr, out_h, None, "half",
                    tile_ready=lambda ti, dc: [f"fC2_e2_{ti*NDC+dc}"],
                    load_ready=lambda ti, dc: [f"x2_s{ti}_{dc}"])

        @step("sync")
        def _(e):
            e.wait_ge(s_xw[0].h, s_xw[0].n)
            e.wait_ge(s_xw[1].h, s_xw[1].n)
            return None

        with nc.Block() as blk:
            for eng in ENGS:
                def make(eng):
                    def body(e):
                        for fn in P.ops[eng]:
                            fn(e)
                    return body
                getattr(blk, eng)(make(eng))

    return nc


def _host_constants():
    c = {}
    c["c_ident"] = np.eye(128, dtype=BFnp)
    BIG = 1e9
    esb = np.zeros((128, 576), np.float32)
    s = np.arange(128)
    for p in range(4):
        for js in range(2):
            j = 2 * p + js
            col = p * 128 + js * 64
            ecol = np.where(s < 16 * j, (16.0 * j - s), BIG)
            esb[:, col:col + 64] = ecol[:, None]
    esb[:, 512:576] = (128.0 - s)[:, None]
    c["c_esb"] = esb
    etau = np.zeros((128, 64), np.float32)
    r = np.arange(128)
    for ci in range(32):
        jsub = ci // 16
        tau = float(ci % 16)
        onblk = (r // 64) == jsub
        etau[:, ci] = np.where(onblk & (jsub != 0), tau, BIG)
        etau[:, 32 + ci] = np.where(onblk, tau, BIG)
    c["c_etau"] = etau
    c["c_eglob"] = np.broadcast_to(np.arange(NT, dtype=np.float32)[None, :],
                                   (NST, NT)).copy()
    c["c_e16"] = np.broadcast_to(np.arange(16, dtype=np.float32)[None, :],
                                 (NST, 16)).copy()
    return c


def _vec_tile(v):
    return np.ascontiguousarray(v.reshape(8, 128).T)


def kernel(**inputs):
    u = np.asarray(inputs["L0_attn_u"], np.float32)
    qw = np.asarray(inputs["L0_ffn_qweights"], np.int32)
    scales = np.asarray(inputs["L0_ffn_scales"], np.float32)
    biases = np.asarray(inputs["L0_ffn_biases"], np.float32)
    lora_a = np.asarray(inputs["L0_ffn_lora_a"], np.float32)
    lora_b = np.asarray(inputs["L0_ffn_lora_b"], np.float32)
    ln_g = np.asarray(inputs["L0_ffn_ln_gamma"], np.float32)
    ln_b = np.asarray(inputs["L0_ffn_ln_beta"], np.float32)
    A_log = np.asarray(inputs["L0_attn_A_log"], np.float32)
    C = np.asarray(inputs["L0_attn_C"], np.float32)
    Dv = np.asarray(inputs["L0_attn_D"], np.float32)
    delta = np.asarray(inputs["L0_attn_delta"], np.float32)
    gamma = np.asarray(inputs["L0_attn_gamma"], np.float32)
    beta = np.asarray(inputs["L0_attn_beta"], np.float32)
    fa = np.asarray(inputs["L0_attn_freq_align"], np.float32)
    reward = np.asarray(inputs["L0_attn_reward"], np.float32)

    shared = {
        "qT": np.ascontiguousarray(qw.transpose(0, 2, 1).astype(np.int8)),
        "scT": np.ascontiguousarray(scales.transpose(0, 2, 1)),
        "la": np.ascontiguousarray(lora_a),
        "lbT": np.ascontiguousarray(lora_b.transpose(0, 2, 1)),
        "bias_t": np.concatenate([_vec_tile(biases[l]) for l in range(6)],
                                 axis=1),
        "lng_t": np.concatenate([_vec_tile(ln_g[i]) for i in range(3)],
                                axis=1),
        "lnb_t": np.concatenate([_vec_tile(ln_b[i]) for i in range(3)],
                                axis=1),
        "d_t": _vec_tile(Dv),
        "gamma_t": _vec_tile(gamma),
        "beta_t": _vec_tile(beta),
        "fa_t": _vec_tile(fa),
        "rw_t": _vec_tile(reward),
        "CT": np.ascontiguousarray(C.T),
        "A_log_t": np.ascontiguousarray(A_log.reshape(NST, 1)),
        "delta0": np.ascontiguousarray(
            np.broadcast_to(delta[0:1], (NST,)).reshape(NST, 1)),
    }
    shared.update(_host_constants())

    nc = build_nc()
    in_maps = []
    for core in range(8):
        b, seg = core // 4, core % 4
        m = dict(shared)
        m["u_fm"] = np.ascontiguousarray(u[b, seg * NT:(seg + 1) * NT, :].T)
        m["is_first"] = np.full((128, 1), 1.0 if seg == 0 else 0.0,
                                np.float32)
        pmk = np.zeros((NST, 4), np.float32)
        e2048 = np.zeros((NST, 4), np.float32)
        for p in range(3):
            if p < seg:
                pmk[:, p] = 1.0
                e2048[:, p] = 2048.0 * (seg - 1 - p)
        m["peer_mask"] = pmk
        m["c_e2048"] = e2048
        in_maps.append(m)

    res = run_bass_kernel_spmd(nc, in_maps, core_ids=list(range(8)))
    out = np.empty((2, 8192, D), np.float32)
    for core in range(8):
        b, seg = core // 4, core % 4
        out[b, seg * NT:(seg + 1) * NT, :] = \
            np.asarray(res.results[core]["out_fm"]).T
    return out


# version 14
# speedup vs baseline: 1.3728x; 1.0396x over previous
import numpy as np
import ml_dtypes
import concourse.bass as bass
from concourse.bass_isa import ReduceOp
import concourse.mybir as mybir
from concourse.bass_utils import run_bass_kernel_spmd

F32 = mybir.dt.float32
BF16 = mybir.dt.bfloat16
I32 = mybir.dt.int32
I8 = mybir.dt.int8
AF = mybir.ActivationFunctionType
OP = mybir.AluOpType
AX = mybir.AxisListType
BFnp = ml_dtypes.bfloat16

D = 1024
NT = 2048
NDC = 8
TT = 512
NTT = 4
TB = 16
TA = 128
NA = 16
NST = 64
EPS = 1e-5
CONSTS = [EPS, -1.0, 1.0 / 7.5, -0.5, 1.0 / D, 0.5, 1.0, 128.0, 0.0, -1.0 / D]

ENGS = ("sync", "gpsimd", "scalar", "vector", "tensor")


class Ctr:
    def __init__(self, h):
        self.h = h
        self.n = 0


class _CP(dict):
    stopped = False

    def __missing__(self, key):
        if self.stopped:
            return (None, 0)
        raise KeyError(key)


class Prog:
    def __init__(self):
        self.ops = {e: [] for e in ENGS}
        self.cp = _CP()
        self.stopped = False

    def stop(self):
        self.stopped = True
        self.cp.stopped = True

    def step(self, eng, waits=(), sem=None, k=1):
        if self.stopped:
            return lambda fn: None
        wl = []
        for w in waits:
            ctr, v = self.cp[w]
            if v > 0 and ctr is not None:
                wl.append((ctr.h, v))
        if sem is not None:
            sem.n += k

        def deco(fn):
            def run(e):
                for h, v in wl:
                    e.wait_ge(h, v)
                inst = fn(e)
                if sem is not None:
                    assert inst is not None
                    inst.then_inc(sem.h, k)
            self.ops[eng].append(run)
            return None
        return deco

    def mark(self, name, ctr):
        if not self.stopped:
            self.cp[name] = (ctr, ctr.n)


def build_nc(debug=(), stop_after=None):
    nc = bass.Bass("TRN2", target_bir_lowering=False)

    u_h = nc.dram_tensor("u_fm", [D, NT], F32, kind="ExternalInput")
    qT_h = nc.dram_tensor("qT", [6, D, D], I8, kind="ExternalInput")
    scT_h = nc.dram_tensor("scT", [6, 64, D], F32, kind="ExternalInput")
    la_h = nc.dram_tensor("la", [6, 32, D], F32, kind="ExternalInput")
    lbT_h = nc.dram_tensor("lbT", [6, 32, D], F32, kind="ExternalInput")
    bias_h = nc.dram_tensor("bias_t", [128, 48], F32, kind="ExternalInput")
    lng_h = nc.dram_tensor("lng_t", [128, 24], F32, kind="ExternalInput")
    lnb_h = nc.dram_tensor("lnb_t", [128, 24], F32, kind="ExternalInput")
    dvec_h = nc.dram_tensor("d_t", [128, 8], F32, kind="ExternalInput")
    gam_h = nc.dram_tensor("gamma_t", [128, 8], F32, kind="ExternalInput")
    bet_h = nc.dram_tensor("beta_t", [128, 8], F32, kind="ExternalInput")
    fa_h = nc.dram_tensor("fa_t", [128, 8], F32, kind="ExternalInput")
    rw_h = nc.dram_tensor("rw_t", [128, 8], F32, kind="ExternalInput")
    ct_h = nc.dram_tensor("CT", [NST, D], F32, kind="ExternalInput")
    alog_h = nc.dram_tensor("A_log_t", [NST, 1], F32, kind="ExternalInput")
    del0_h = nc.dram_tensor("delta0", [NST, 1], F32, kind="ExternalInput")
    isf_h = nc.dram_tensor("is_first", [128, 1], F32, kind="ExternalInput")
    pm_h = nc.dram_tensor("peer_mask", [NST, 4], F32, kind="ExternalInput")
    ident_h = nc.dram_tensor("c_ident", [128, 128], BF16, kind="ExternalInput")
    esb_h = nc.dram_tensor("c_esb", [128, 576], F32, kind="ExternalInput")
    etau_h = nc.dram_tensor("c_etau", [128, 64], F32, kind="ExternalInput")
    eglob_h = nc.dram_tensor("c_eglob", [NST, NT], F32, kind="ExternalInput")
    e16_h = nc.dram_tensor("c_e16", [NST, 16], F32, kind="ExternalInput")
    e2048_h = nc.dram_tensor("c_e2048", [NST, 4], F32, kind="ExternalInput")

    out_h = nc.dram_tensor("out_fm", [D, NT], F32, kind="ExternalOutput")
    dbg = {}
    for name, shape, dt in debug:
        dbg[name] = nc.dram_tensor(name, shape, dt, kind="ExternalOutput")

    wscr = nc.dram_tensor("wscr", [6, D, D], BF16, kind="Internal")
    xscr = nc.dram_tensor("xscr", [D, NT], F32, kind="Internal")
    lamT_d = nc.dram_tensor("lamT_d", [NST, 1], F32, kind="Internal")
    cc_in = nc.dram_tensor("cc_in", [NST, D], F32, kind="Internal")
    cc_out = nc.dram_tensor("cc_out", [4 * NST, D], F32, kind="Internal")

    from contextlib import ExitStack
    ctx = ExitStack()
    P = Prog()
    step, mark = P.step, P.mark

    def SB(name, shape, dtype):
        return ctx.enter_context(nc.sbuf_tensor(name, shape, dtype))

    def PST(name, shape, dtype=F32):
        return ctx.enter_context(nc.psum_tensor(name, shape, dtype))

    def SEM(name):
        return Ctr(ctx.enter_context(nc.semaphore(name)))

    with ctx:
        bigA = SB("bigA", [128, NDC * NT], BF16)
        bigB = SB("bigB", [128, NDC * NT], BF16)
        wsb = [SB(f"wsb{i}", [128, 8192], BF16) for i in range(2)]
        ident = SB("ident", [128, 128], BF16)
        onesc = SB("onesc", [128, 1], BF16)
        onesr = SB("onesr", [1, 128], BF16)
        zrow = SB("zrow", [1, 128], BF16)
        zrow512 = SB("zrow512", [1, 512], BF16)
        cpool = SB("cpool", [128, len(CONSTS)], F32)
        for i, v in enumerate(CONSTS):
            nc.const_aps.register(F32, v, cpool[:, i:i + 1])
        biasv = SB("biasv", [128, 48], F32)
        lngv = SB("lngv", [128, 24], F32)
        lnbv = SB("lnbv", [128, 24], F32)
        dvecv = SB("dvecv", [128, 8], F32)
        gamv = SB("gamv", [128, 8], F32)
        betv = SB("betv", [128, 8], F32)
        fav = SB("fav", [128, 8], F32)
        rw2 = SB("rw2", [128, 8], F32)
        isf = SB("isf", [128, 1], F32)
        ctrep = SB("ctrep", [128, D], BF16)
        lamv = SB("lamv", [NST, 1], F32)
        lam2 = SB("lam2", [128, 1], F32)
        lam128 = SB("lam128", [NST, 1], F32)
        coefs = SB("coefs", [NST, 4], F32)
        sbL = SB("sbL", [128, 576], BF16)
        p16bd = SB("p16bd", [128, 64], BF16)
        p128L = SB("p128L", [NST, 128], BF16)
        pga = SB("pga", [NST, 128], BF16)
        eglt = SB("eglt", [NST, 128], F32)
        p16v = SB("p16v", [NST, 16], BF16)
        k16 = SB("k16", [128, NDC * 16], F32)
        tdiag = SB("tdiag", [128, 8 * 128], BF16)
        xr = [SB(f"xr{i}", [128, TT], F32) for i in range(3)]
        xr2 = [SB(f"xr2{i}", [128, TT], F32) for i in range(2)]
        xw = [SB(f"xw{i}", [128, TT], F32) for i in range(2)]
        tmp64 = SB("tmp64", [NST, 32], F32)
        sig = SB("sig", [NST, D], F32)
        sigb = SB("sigb", [NST, D], BF16)
        sinit = SB("sinit", [NST, D], F32)
        sinitb = SB("sinitb", [NST, D], BF16)
        cvqa = SB("cvqa", [NST, D], BF16)
        cv2 = SB("cv2", [128, 2 * D], BF16)
        utm = [SB(f"utm{i}", [128, D], BF16) for i in range(3)]
        pglob = SB("pglob", [NST, NT], BF16)
        rowR16 = SB("rowR16", [1, NT], BF16)
        rowA16 = SB("rowA16", [1, NT], BF16)
        rowS = SB("rowS", [1, NT], F32)[:, :]
        rowV = SB("rowV", [1, NT], F32)[:, :]
        rowQ = SB("rowQ", [1, NT], F32)[:, :]
        rowB = SB("rowB", [1, TT], F32)
        rowC = SB("rowC", [1, TT], BF16)
        aBs = [SB(f"aBs{i}", [128, TT], BF16) for i in range(2)]
        rBs = [SB(f"rBs{i}", [128, TT], BF16) for i in range(2)]
        t1s = [SB(f"t1s{i}", [128, TT], F32) for i in range(2)]
        xbt = [SB(f"xbt{i}", [128, TT], BF16) for i in range(2)]
        sqb2 = [SB(f"sqb{i}", [128, TT], BF16) for i in range(2)]
        rstdB = [SB(f"rstdB{i}", [128, TT], BF16) for i in range(2)]
        gbodd = SB("gbodd", [128, 4096], BF16)
        lab = [SB(f"lab{i}", [32, 128], BF16) for i in range(2)]
        lbb = SB("lbb", [32, D], BF16)
        wrow = SB("wrow", [128, 1], F32)
        wrow2 = SB("wrow2", [128, 1], F32)
        bsum8 = SB("bsum8", [128, 8], F32)
        bmean2 = SB("bmean2", [128, 1], F32)
        bb16 = SB("bb16", [128, 8], BF16)
        bmean = SB("bmean", [128, 1], F32)
        lamT16 = SB("lamT16", [1, NST], F32)
        lamT16b = SB("lamT16b", [1, NST], BF16)
        lamTb = SB("lamTb", [128, NST], F32)

        def y1c(ti):
            o = 0 if ti % 2 == 0 else 8192
            return bigB[:, o:o + 4096]
        gb_sl = bigB[:, 4096:8192]
        qb_sl = [bigB[:, 12288:13312], bigB[:, 13312:14336]]
        scb_sl = [bigB[:, 14336:15360], bigB[:, 15360:16384]]

        pmm = [PST(f"pmm{i}", [128, TT], F32) for i in range(2)]
        prow = PST("prow", [1, TT], F32)
        pbc = PST("pbc", [128, TT], F32)
        pwA = PST("pwA", [128, 1024], F32)
        pwB = PST("pwB", [128, 1024], F32)

        s_in = SEM("s_in")
        s_w = SEM("s_w")
        s_wg = SEM("s_wg")
        s_ws = SEM("s_ws")
        s_xb = [SEM(f"s_xb{i}") for i in range(3)]
        s_st = SEM("s_st")
        s_qg = SEM("s_qg")
        s_qg2 = [SEM(f"s_qg2{i}") for i in range(2)]
        s_pp = SEM("s_pp")
        s_pg = SEM("s_pg")
        s_cc = SEM("s_cc")
        s_pgl = SEM("s_pgl")
        s_esb = SEM("s_esb")
        tS = SEM("tS")
        chV = SEM("chV")
        chS = SEM("chS")

        def chain(e, inst, ch=None):
            ch = ch or chV
            ch.n += 1
            inst.then_inc(ch.h, 1)
            e.wait_ge(ch.h, ch.n)
        tT = SEM("tT")
        tV = SEM("tV")
        tG = SEM("tG")

        _p0 = [(bias_h.ap(), biasv[:, 0:48]), (lng_h.ap(), lngv[:, 0:24]),
               (lnb_h.ap(), lnbv[:, 0:24]), (gam_h.ap(), gamv[:, 0:8]),
               (bet_h.ap(), betv[:, 0:8]), (fa_h.ap(), fav[:, 0:8]),
               (rw_h.ap(), rw2[:, 0:8]), (isf_h.ap(), isf[:, 0:1]),
               (dvec_h.ap(), dvecv[:, 0:8]), (ident_h.ap(), ident[:, :]),

               (alog_h.ap(), tmp64[:, 0:1]), (del0_h.ap(), tmp64[:, 1:2]),
               (pm_h.ap(), coefs[:, :]), (e16_h.ap(), tmp64[:, 4:20]),
               (e2048_h.ap(), tmp64[:, 20:24]), (etau_h.ap(), k16[:, 0:64]),
               (bass.AP(eglob_h, 0, [[NT, 64], [1, 128]]), eglt[:, :])]

        @step("sync")
        def _(e):
            for src, dst in _p0:
                e.dma_start(out=dst, in_=src).then_inc(s_in.h, 16)
            return None
        s_in.n += 16 * len(_p0)
        mark("ld_params", s_in)

        @step("gpsimd")
        def _(e):
            e.dma_start(out=ctrep[0:64, :], in_=ct_h.ap()).then_inc(s_qg.h, 16)
            e.dma_start(out=ctrep[64:128, :],
                        in_=ct_h.ap()).then_inc(s_qg.h, 16)
            return None
        s_qg.n += 32
        mark("ld_ct", s_qg)

        @step("gpsimd")
        def _(e):
            e.memset(onesc[:, :], 1.0).then_inc(tG.h, 1)
            e.memset(onesr[:, :], 1.0).then_inc(tG.h, 1)
            e.memset(zrow[:, :], 0.0).then_inc(tG.h, 1)
            e.memset(zrow512[:, :], 0.0).then_inc(tG.h, 1)
            for i, v in enumerate(CONSTS):
                e.memset(cpool[:, i:i + 1], v).then_inc(tG.h, 1)
            return None
        tG.n += 4 + len(CONSTS)
        mark("setup_g", tG)

        @step("scalar", waits=("ld_params", "setup_g"), sem=tS)
        def _(e):
            return e.activation(lamv[0:64, :], tmp64[0:64, 0:1], AF.Exp)
        mark("expA", tS)

        @step("vector", waits=("expA",), sem=tV)
        def _(e):
            chain(e, e.tensor_mul(lamv[0:64, :], lamv[0:64, :],
                                  tmp64[0:64, 1:2]))
            return e.tensor_scalar_mul(lamv[0:64, :], lamv[0:64, :], -1.0)
        mark("lamv", tV)

        @step("vector", waits=("lamv",), sem=tV)
        def _(e):
            e.tensor_copy(lam2[0:64, :], lamv[0:64, :])
            return e.tensor_copy(lam2[64:128, :], lamv[0:64, :])
        mark("lam2", tV)

        @step("scalar", waits=("lamv",), sem=tS)
        def _(e):
            e.activation(lam128[0:64, :], lamv[0:64, :], AF.Exp, scale=128.0)
            e.activation(p16v[0:64, :], tmp64[0:64, 4:20], AF.Exp,
                         scale=lamv[0:64, :])
            e.activation(p128L[0:64, :], eglt[0:64, :], AF.Exp,
                         scale=lamv[0:64, :])
            return e.activation(tmp64[0:64, 28:32], tmp64[0:64, 20:24],
                                AF.Exp, scale=lamv[0:64, :])
        mark("lams", tS)

        @step("vector", waits=("lams",), sem=tV)
        def _(e):
            e.tensor_mul(coefs[0:64, :], coefs[0:64, :], tmp64[0:64, 28:32])
            return e.tensor_scalar_mul(rw2[:, 0:8], rw2[:, 0:8], isf[:, 0:1])
        mark("coefs", tV)

        for half in range(2):
            wl = ["lams"] if half == 0 else ("pgl0",)

            @step("sync", waits=tuple(wl) if half else ("ld_params",),
                  sem=s_pgl, k=16)
            def _(e, half=half):
                return e.dma_start(
                    out=sig[0:64, :],
                    in_=eglob_h.ap()[:, half * 1024:half * 1024 + 1024])
            mark(f"pgld{half}", s_pgl)

            @step("scalar", waits=(f"pgld{half}", "lams"), sem=tS)
            def _(e, half=half):
                return e.activation(pglob[0:64, half * 1024:half * 1024 + 1024],
                                    sig[0:64, :], AF.Exp,
                                    scale=lamv[0:64, :])
            mark(f"pgl{half}", tS)
        mark("pglob_done", tS)

        s_pg_snap = s_pg.n

        @step("gpsimd", waits=("lamv",))
        def _(e):
            e.dma_start(out=lamT_d.ap(),
                        in_=lamv[0:64, :]).then_inc(s_pg.h, 16)
            e.wait_ge(s_pg.h, s_pg_snap + 16)
            e.dma_start(out=lamT16[0:1, 0:64],
                        in_=bass.AP(lamT_d, 0, [[64, 1], [1, 64]])
                        ).then_inc(s_pg.h, 16)
            return None
        s_pg.n += 32
        mark("lamT", s_pg)

        @step("vector", waits=("lamT",), sem=tV)
        def _(e):
            return e.tensor_copy(lamT16b[0:1, :], lamT16[0:1, 0:64])
        mark("lamT16b", tV)

        @step("tensor", waits=("lamT16b", "setup_g", "ld_params"), sem=tT)
        def _(e):
            return e.matmul(pbc[:, 0:64], onesr[0:1, :],
                            lamT16b[0:1, :], start=True, stop=True)
        mark("dAbc", tT)

        @step("vector", waits=("dAbc",), sem=tV)
        def _(e):
            return e.tensor_copy(lamTb[:, :], pbc[:, 0:64])
        mark("lamTb", tV)

        for c in range(2):
            cw, coff = (512, 0) if c == 0 else (64, 512)

            @step("sync", waits=(("lamTb",) if c == 0 else (f"esbx{0}",)),
                  sem=s_esb, k=16)
            def _(e, cw=cw, coff=coff):
                return e.dma_start(out=xr[2][:, 0:cw],
                                   in_=esb_h.ap()[:, coff:coff + cw])
            mark(f"esbl{c}", s_esb)

            @step("vector", waits=(f"esbl{c}", "lamTb"), sem=tV)
            def _(e, cw=cw, coff=coff):
                return e.tensor_mul(
                    xr[2][:, 0:cw], xr[2][:, 0:cw],
                    bass.AP(lamTb, coff % 64, [[64, 128], [0, 9], [1, 64]])
                    if False else
                    bass.AP(lamTb, 0, [[64, 128], [0, (cw + 63) // 64],
                                       [1, 64]]))
            mark(f"esbm{c}", tV)

            @step("scalar", waits=(f"esbm{c}",), sem=tS)
            def _(e, cw=cw, coff=coff):
                return e.activation(sbL[:, coff:coff + cw], xr[2][:, 0:cw],
                                    AF.Exp)
            mark(f"esbx{c}", tS)
        P.cp["sbL"] = P.cp["esbx1"]

        @step("scalar", waits=("lam2",), sem=tS)
        def _(e):
            return e.activation(p16bd[:, 0:64], k16[:, 0:64], AF.Exp,
                                scale=lam2[:, 0:1])
        mark("p16bd", tS)

        for dg in range(NDC):
            w = ("lams", "ld_ct", "lamTb", "p16bd") if dg == 0 else (f"k16ev{dg-1}",)

            @step("tensor", waits=w, sem=tT)
            def _(e, dg=dg):
                return e.matmul(pbc[:, 0:16],
                                ctrep[0:64, dg * 128:dg * 128 + 128],
                                p16v[0:64, :], start=True, stop=True)
            mark(f"k16mm{dg}", tT)

            @step("vector", waits=(f"k16mm{dg}",), sem=tV)
            def _(e, dg=dg):
                return e.tensor_copy(k16[:, dg * 16:dg * 16 + 16],
                                     pbc[:, 0:16])
            mark(f"k16ev{dg}", tV)

        @step("vector", waits=(f"k16ev{NDC-1}", "ld_params"), sem=tV)
        def _(e):
            return e.tensor_add(
                bass.AP(k16, 0, [[128, 128], [16, 8], [1, 1]]),
                bass.AP(k16, 0, [[128, 128], [16, 8], [1, 1]]),
                dvecv[:, 0:8])
        mark("k16d", tV)

        s_lx = [SEM(f"s_lx{i}") for i in range(3)]
        s_lx2 = [SEM(f"s_lx2{i}") for i in range(2)]
        s_xw = [SEM(f"s_xw{i}") for i in range(2)]

        def xtile(h, ti, dc):
            return bass.AP(h, (dc * 128) * NT + ti * TT, [[NT, 128], [1, TT]])

        def emit_outer_ln(tag, src, dst, src_ready, src_is_b16=False,
                          tile_ready=None, ext_stats=False,
                          bcast_apply=True, sbuf_apply=False):
            hbm = src[0] == "hbm"
            sh = src[1]
            xins = {}
            for ti in (range(NTT) if not ext_stats else ()):
                for dc in range(NDC):
                    idx = ti * NDC + dc
                    if hbm:
                        wl = list(tile_ready(ti, dc)) if tile_ready else []
                        if idx == 0 and src_ready:
                            wl.append(src_ready)
                        if idx >= 3:
                            wl.append(f"{tag}_cast{(idx-3)//NDC}_{(idx-3)%NDC}")

                        @step("sync", waits=tuple(wl), sem=s_lx[idx % 3],
                              k=16)
                        def _(e, ti=ti, dc=dc, idx=idx):
                            return e.dma_start(out=xr[idx % 3][:, :],
                                               in_=xtile(sh, ti, dc))
                        mark(f"{tag}_xl{ti}_{dc}", s_lx[idx % 3])
                        wl = [f"{tag}_xl{ti}_{dc}"]
                        if idx >= 2:
                            wl.append(f"{tag}_mm{(idx-2)//NDC}_{(idx-2)%NDC}")
                        if sbuf_apply:
                            xb = dst[:, dc * NT + ti * TT:
                                     dc * NT + ti * TT + TT]
                        else:
                            xb = xbt[dc % 2][:, :]

                        @step("scalar", waits=tuple(wl), sem=tS)
                        def _(e, idx=idx, xb=xb):
                            return e.activation(xb, xr[idx % 3][:, :],
                                                AF.Copy)
                        mark(f"{tag}_cast{ti}_{dc}", tS)
                        xin = xb
                        wv = [f"{tag}_cast{ti}_{dc}"]
                    elif not src_is_b16:
                        sl = sh[:, dc * NT + ti * TT:dc * NT + ti * TT + TT]
                        wl = [src_ready]
                        if idx >= 2:
                            wl.append(f"{tag}_mm{(idx-2)//NDC}_{(idx-2)%NDC}")
                        xb = xbt[dc % 2]

                        @step("scalar", waits=tuple(wl), sem=tS)
                        def _(e, sl=sl, xb=xb):
                            return e.activation(xb[:, :], sl, AF.Copy)
                        mark(f"{tag}_cast{ti}_{dc}", tS)
                        xin = xb[:, :]
                        wv = [f"{tag}_cast{ti}_{dc}"]
                    else:
                        xin = sh[:, dc * NT + ti * TT:dc * NT + ti * TT + TT]
                        wv = [src_ready]
                        if tile_ready:
                            wv += list(tile_ready(ti, dc))
                        if idx >= 2:
                            wv.append(f"{tag}_mm{(idx-2)//NDC}_{(idx-2)%NDC}")
                    xins[(ti, dc)] = xin
                    sq = sqb2[dc % 2]

                    @step("scalar", waits=tuple(wv), sem=tS)
                    def _(e, xin=xin, sq=sq):
                        return e.activation(sq[:, :], xin, AF.Square)
                    mark(f"{tag}_sq{ti}_{dc}", tS)

                    wl = [f"{tag}_sq{ti}_{dc}"]
                    if dc == 0 and ti > 0:
                        wl.append(f"{tag}_rows{ti-1}")

                    @step("tensor", waits=tuple(wl), sem=tT)
                    def _(e, xin=xin, sq=sq, dc=dc):
                        e.matmul(prow[0:1, :], onesc[:, 0:1], xin,
                                 start=(dc == 0), stop=(dc == NDC - 1))
                        return e.matmul(pbc[0:1, :], onesc[:, 0:1], sq[:, :],
                                        start=(dc == 0), stop=(dc == NDC - 1))
                    mark(f"{tag}_mm{ti}_{dc}", tT)

                @step("vector", waits=(f"{tag}_mm{ti}_{NDC-1}",), sem=tV)
                def _(e, ti=ti):
                    e.tensor_copy(rowS[:, ti * TT:ti * TT + TT],
                                  prow[0:1, :])
                    return e.tensor_copy(rowQ[:, ti * TT:ti * TT + TT],
                                         pbc[0:1, :])
                mark(f"{tag}_rows{ti}", tV)

            @step("scalar", waits=(f"{tag}_rows{NTT-1}",), sem=tS)
            def _(e):
                return e.activation(rowS[:, :], rowS[:, :], AF.Copy,
                                    scale=1.0 / D)
            mark(f"{tag}_fmu", tS)

            @step("vector", waits=(f"{tag}_fmu",), sem=tV)
            def _(e):
                chain(e, e.tensor_scalar(rowQ[:, :], rowQ[:, :],
                                         1.0 / D, None, OP.mult))
                chain(e, e.tensor_mul(rowV[:, :], rowS[:, :],
                                      rowS[:, :]))
                return e.tensor_sub(rowQ[:, :], rowQ[:, :],
                                    rowV[:, :])
            mark(f"{tag}_fvar", tV)

            @step("scalar", waits=(f"{tag}_fvar",), sem=tS)
            def _(e):
                return e.activation(rowQ[:, :], rowQ[:, :], AF.Ln,
                                    bias=EPS)
            mark(f"{tag}_fln", tS)

            @step("scalar", waits=(f"{tag}_fln",), sem=tS)
            def _(e):
                return e.activation(rowR16[:, :], rowQ[:, :], AF.Exp,
                                    scale=-0.5)
            mark(f"{tag}_frstd", tS)

            @step("vector", waits=(f"{tag}_frstd",), sem=tV)
            def _(e):
                chain(e, e.tensor_mul(rowS[:, :], rowS[:, :],
                                      rowR16[:, :]))
                return e.tensor_scalar(rowA16[:, :], rowS[:, :], -1.0,
                                       None, OP.mult)
            mark(f"{tag}_farow", tV)

            if not bcast_apply:
                P.cp[f"{tag}_done"] = P.cp[f"{tag}_farow"]
                return
            for ti in range(NTT):
                wl = [f"{tag}_farow"]
                if ti >= 1:
                    wl.append(f"{tag}_bcs{ti-1}")

                @step("tensor", waits=tuple(wl), sem=tT)
                def _(e, ti=ti):
                    e.matmul(pbc[:, :], onesr[0:1, :],
                             rowR16[:, ti * TT:ti * TT + TT],
                             start=True, stop=True)
                    return e.matmul(pwB[:, 0:512], onesr[0:1, :],
                                    rowA16[:, ti * TT:ti * TT + TT],
                                    start=True, stop=True)
                mark(f"{tag}_bc{ti}", tT)

                @step("vector", waits=(f"{tag}_bc{ti}",), sem=tV)
                def _(e, ti=ti):
                    e.tensor_copy(rBs[ti % 2][:, :], pbc[:, :])
                    return e.tensor_copy(aBs[ti % 2][:, :], pwB[:, 0:512])
                mark(f"{tag}_bcs{ti}", tV)

                for dc in range(NDC):
                    idx = ti * NDC + dc
                    dl = dst[:, dc * NT + ti * TT:dc * NT + ti * TT + TT]
                    t1 = t1s[dc % 2]
                    if hbm and not sbuf_apply:
                        wl = []
                        if ext_stats and tile_ready:
                            wl += list(tile_ready(ti, dc))
                        if idx >= 2:
                            wl.append(f"{tag}_ap{(idx-2)//NDC}_{(idx-2)%NDC}")

                        @step("scalar", waits=tuple(wl),
                              sem=s_lx2[idx % 2], k=16)
                        def _(e, ti=ti, dc=dc, idx=idx):
                            return e.dma_start(out=xr2[idx % 2][:, :],
                                               in_=xtile(sh, ti, dc))
                        mark(f"{tag}_xl2{ti}_{dc}", s_lx2[idx % 2])

                        @step("vector", waits=(f"{tag}_bcs{ti}",
                                               f"{tag}_xl2{ti}_{dc}"),
                              sem=tV)
                        def _(e, idx=idx, dl=dl, t1=t1, ti=ti):
                            chain(e, e.tensor_mul(t1[:, :],
                                                  xr2[idx % 2][:, :],
                                                  rBs[ti % 2][:, :]))
                            return e.tensor_add(dl, t1[:, :],
                                                aBs[ti % 2][:, :])
                        mark(f"{tag}_ap{ti}_{dc}", tV)
                    else:
                        xin = xins.get((ti, dc))
                        if xin is None:
                            xin = dl
                        t1b = xbt[dc % 2][:, :]

                        @step("vector", waits=(f"{tag}_bcs{ti}",), sem=tV)
                        def _(e, xin=xin, dl=dl, t1b=t1b, ti=ti):
                            chain(e, e.tensor_mul(t1b, xin,
                                                  rBs[ti % 2][:, :]))
                            return e.tensor_add(dl, t1b,
                                                aBs[ti % 2][:, :])
                        mark(f"{tag}_ap{ti}_{dc}", tV)
            mark(f"{tag}_done", tV)

        def emit_update(tag, in_h, out_h2, ready, kind, tile_ready=None,
                        stats_for=None, stats_gate=(), load_ready=None):
            for ti in range(NTT):
                if kind == "gb2":
                    wl = ["lnS_farow"]
                    if ti >= 1:
                        wl.append(f"lnS_bcs{ti-1}")

                    @step("tensor", waits=tuple(wl), sem=tT)
                    def _(e, ti=ti):
                        e.matmul(pbc[:, :], onesr[0:1, :],
                                 rowR16[:, ti * TT:ti * TT + TT],
                                 start=True, stop=True)
                        return e.matmul(pwB[:, 0:512], onesr[0:1, :],
                                        rowA16[:, ti * TT:ti * TT + TT],
                                        start=True, stop=True)
                    mark(f"lnS_bc{ti}", tT)

                    @step("vector", waits=(f"lnS_bc{ti}",), sem=tV)
                    def _(e, ti=ti):
                        e.tensor_copy(rBs[ti % 2][:, :], pbc[:, :])
                        return e.tensor_copy(aBs[ti % 2][:, :],
                                             pwB[:, 0:512])
                    mark(f"lnS_bcs{ti}", tV)
                for dc in range(NDC):
                    idx = ti * NDC + dc
                    wl = [ready] if (idx == 0 and ready) else []
                    if load_ready:
                        wl += list(load_ready(ti, dc))
                    if idx >= 3:
                        wl.append(f"{tag}_v{(idx-3)//NDC}_{(idx-3)%NDC}")

                    leng = ("sync" if (stats_for or idx % 2 == 0)
                            else "scalar")

                    @step(leng, waits=tuple(wl), sem=s_lx[idx % 3], k=16)
                    def _(e, ti=ti, dc=dc, idx=idx):
                        return e.dma_start(out=xr[idx % 3][:, :],
                                           in_=xtile(in_h, ti, dc))
                    mark(f"{tag}_l{ti}_{dc}", s_lx[idx % 3])

                    wl = [f"{tag}_l{ti}_{dc}"]
                    if kind == "gb2":
                        wl.append(f"lnS_bcs{ti}")
                    if tile_ready:
                        wl += list(tile_ready(ti, dc))
                    if idx >= 2:
                        wl.append(f"{tag}_s{(idx-2)//NDC}_{(idx-2)%NDC}")
                        if stats_for:
                            wl.append(
                                f"{stats_for}_cast"
                                f"{(idx-2)//NDC}_{(idx-2)%NDC}")

                    @step("vector", waits=tuple(wl), sem=tV)
                    def _(e, ti=ti, dc=dc, idx=idx):
                        sl = bigA[:, dc * NT + ti * TT:dc * NT + ti * TT + TT]
                        o = xw[idx % 2]
                        if kind == "half":
                            return e.scalar_tensor_tensor(
                                o[:, :], sl, 0.5, xr[idx % 3][:, :],
                                OP.mult, OP.add)
                        t1 = t1s[dc % 2]
                        chain(e, e.scalar_tensor_tensor(
                            t1[:, :], sl, gamv[:, dc:dc + 1],
                            rBs[ti % 2][:, :], OP.mult, OP.mult))
                        chain(e, e.scalar_tensor_tensor(
                            t1[:, :], aBs[ti % 2][:, :],
                            gamv[:, dc:dc + 1], t1[:, :],
                            OP.mult, OP.add))
                        return e.scalar_tensor_tensor(
                            o[:, :], t1[:, :], betv[:, dc:dc + 1],
                            xr[idx % 3][:, :], OP.add, OP.add)
                    mark(f"{tag}_v{ti}_{dc}", tV)

                    @step("gpsimd", waits=(f"{tag}_v{ti}_{dc}",),
                          sem=s_xw[idx % 2], k=16)
                    def _(e, ti=ti, dc=dc, idx=idx):
                        return e.dma_start(out=xtile(out_h2, ti, dc),
                                           in_=xw[idx % 2][:, :])
                    mark(f"{tag}_s{ti}_{dc}", s_xw[idx % 2])

                    if stats_for:
                        wl = [f"{tag}_v{ti}_{dc}"]
                        if idx >= 2:
                            wl.append(
                                f"{stats_for}_mm"
                                f"{(idx-2)//NDC}_{(idx-2)%NDC}")

                        csl = bigA[:, dc * NT + ti * TT:
                                   dc * NT + ti * TT + TT]

                        @step("scalar", waits=tuple(wl), sem=tS)
                        def _(e, idx=idx, csl=csl):
                            return e.activation(csl, xw[idx % 2][:, :],
                                                AF.Copy)
                        mark(f"{stats_for}_cast{ti}_{dc}", tS)

                        @step("scalar",
                              waits=(f"{stats_for}_cast{ti}_{dc}",),
                              sem=tS)
                        def _(e, dc=dc, csl=csl):
                            return e.activation(sqb2[dc % 2][:, :], csl,
                                                AF.Square)
                        mark(f"{stats_for}_sq{ti}_{dc}", tS)

                        wl = [f"{stats_for}_sq{ti}_{dc}"]
                        if dc == 0 and ti > 0:
                            wl.append(f"{stats_for}_rows{ti-1}")
                        if idx == 0:
                            wl += list(stats_gate)

                        @step("tensor", waits=tuple(wl), sem=tT)
                        def _(e, dc=dc, csl=csl):
                            e.matmul(pmm[0][0:1, :], onesc[:, 0:1],
                                     csl,
                                     start=(dc == 0), stop=(dc == NDC - 1))
                            return e.matmul(
                                pmm[1][0:1, :], onesc[:, 0:1],
                                sqb2[dc % 2][:, :],
                                start=(dc == 0), stop=(dc == NDC - 1))
                        mark(f"{stats_for}_mm{ti}_{dc}", tT)
                if stats_for:
                    @step("vector", waits=(f"{stats_for}_mm{ti}_{NDC-1}",),
                          sem=tV)
                    def _(e, ti=ti):
                        e.tensor_copy(rowS[:, ti * TT:ti * TT + TT],
                                      pmm[0][0:1, :])
                        return e.tensor_copy(rowQ[:, ti * TT:ti * TT + TT],
                                             pmm[1][0:1, :])
                    mark(f"{stats_for}_rows{ti}", tV)
            @step("sync", waits=(f"{tag}_s{NTT-1}_{NDC-2}",
                                 f"{tag}_s{NTT-1}_{NDC-1}"))
            def _(e):
                return None
            P.cp[f"{tag}_done"] = P.cp[f"{tag}_s{NTT-1}_{NDC-1}"]
            P.cp[f"{tag}_done2"] = P.cp[f"{tag}_s{NTT-1}_{NDC-2}"]

        def emit_prep_layer(l):
            buf = wsb[l % 2]

            qg_prev = s_qg.n
            _lw = ["ld_params"]
            if l > 0:
                _lw.append(f"prep_lora{l-1}_{NDC-1}")

            @step("gpsimd", waits=tuple(_lw))
            def _(e, l=l, qg_prev=qg_prev):
                e.wait_ge(s_qg.h, qg_prev)
                e.dma_start(out=lbb[:, :],
                            in_=lbT_h.ap()[l]).then_inc(s_qg.h, 16)
                return None
            s_qg.n += 16
            mark(f"lora_ld{l}", s_qg)
            for dc in range(NDC):
                qbb, scbb = qb_sl[dc % 2], scb_sl[dc % 2]
                wl = []
                if dc >= 2:
                    wl.append(f"prep_asm{l}_{dc-2}")
                    wl.append(f"prep_lora{l}_{dc-2}")
                elif l > 0:
                    wl.append(f"prep_asm{l-1}_{NDC-2+dc}")
                    wl.append(f"prep_lora{l-1}_{NDC-2+dc}")

                sem2 = s_qg2[dc % 2]

                @step("gpsimd", waits=tuple(wl))
                def _(e, l=l, dc=dc, qbb=qbb, scbb=scbb, sem2=sem2):
                    e.dma_start(out=qbb,
                                in_=qT_h.ap()[l, dc * 128:dc * 128 + 128, :]
                                ).then_inc(sem2.h, 16)
                    e.dma_start(
                        out=scbb,
                        in_=bass.AP(scT_h, l * 64 * D + dc * 8 * D,
                                    [[D, 8], [0, 16], [1, D]])
                        ).then_inc(sem2.h, 16)
                    e.dma_start(
                        out=lab[dc % 2][:, :],
                        in_=la_h.ap()[l, :, dc * 128:dc * 128 + 128]
                        ).then_inc(sem2.h, 16)
                    return None
                sem2.n += 48
                mark(f"prep_ld{l}_{dc}", sem2)

                wl = [f"lora_ld{l}", f"prep_ld{l}_{dc}"]
                if dc >= 1:
                    wl.append(f"prep_asm{l}_{dc-1}")
                elif l > 0:
                    wl.append(f"prep_asm{l-1}_{NDC-1}")

                @step("tensor", waits=tuple(wl), sem=tT)
                def _(e, dc=dc):
                    e.matmul(pwA[:, 0:512], lab[dc % 2][:, :],
                             lbb[:, 0:512], start=True, stop=True)
                    return e.matmul(pwA[:, 512:1024],
                                    lab[dc % 2][:, :],
                                    lbb[:, 512:1024], start=True, stop=True)
                mark(f"prep_lora{l}_{dc}", tT)

                @step("scalar", waits=(f"prep_ld{l}_{dc}",), sem=tS)
                def _(e, qbb=qbb):
                    return e.activation(qbb, qbb, AF.Copy,
                                        scale=1.0 / 7.5, bias=-1.0)
                mark(f"prep_qf{l}_{dc}", tS)

                wl = [f"prep_qf{l}_{dc}", f"prep_lora{l}_{dc}"]
                if l >= 2 and dc == 0:
                    wl.append(f"ffn_wfree{l-2}")
                    wl.append(f"wscr{l-2}")

                @step("vector", waits=tuple(wl), sem=tV)
                def _(e, l=l, dc=dc, buf=buf, qbb=qbb, scbb=scbb):
                    chain(e, e.tensor_mul(qbb, qbb, scbb))
                    sl = buf[:, dc * 1024:dc * 1024 + 1024]
                    last = e.tensor_add(sl, qbb, pwA[:, :])
                    if l % 2 == 0:
                        chain(e, last)
                        chain(e, e.tensor_reduce(wrow[:, :], sl, AX.X,
                                                 OP.add))
                        chain(e, e.tensor_scalar(wrow[:, :], wrow[:, :],
                                                 1.0 / D, None, OP.mult))
                        last = e.tensor_scalar(sl, sl, wrow[:, 0:1],
                                               None, OP.subtract)
                    return last
                mark(f"prep_asm{l}_{dc}", tV)

            P.cp[f"wready{l}_A"] = P.cp[f"prep_asm{l}_{NDC-1}"]

            sw_prev = s_ws.n

            @step("sync", waits=(f"prep_asm{l}_{NDC-1}",), sem=s_ws, k=16)
            def _(e, l=l, buf=buf, sw_prev=sw_prev):
                e.wait_ge(s_ws.h, sw_prev)
                return e.dma_start(
                    out=bass.AP(wscr, l * D * D,
                                [[D, 128], [128 * D, 8], [1, D]]),
                    in_=buf[:, :])
            mark(f"wscr{l}", s_ws)

        def emit_stream_layer(l):
            buf = wsb[l % 2]
            wl = [f"wscr{l}", "wscr5", f"ffn_wfree{l+4}" if l < 2 else
                  f"ffnC_wfree{l-2}"]

            sw_prev = s_w.n

            @step("sync", waits=tuple(wl), sem=s_w, k=16)
            def _(e, l=l, buf=buf, sw_prev=sw_prev):
                e.wait_ge(s_w.h, sw_prev)
                return e.dma_start(
                    out=buf[:, 0:4096],
                    in_=bass.AP(wscr, l * D * D,
                                [[D, 128], [128 * D, 4], [1, D]]))
            mark(f"wready{l}_C", s_w)

            swg_prev = s_wg.n

            @step("gpsimd", waits=tuple(wl))
            def _(e, l=l, buf=buf, swg_prev=swg_prev):
                e.wait_ge(s_wg.h, swg_prev)
                e.dma_start(
                    out=buf[:, 4096:8192],
                    in_=bass.AP(wscr, l * D * D + 4 * 128 * D,
                                [[D, 128], [128 * D, 4], [1, D]])
                    ).then_inc(s_wg.h, 16)
                return None
            s_wg.n += 16
            mark(f"wready{l}_Cg", s_wg)

        def emit_ffn(call, hook=None):
            tag = f"f{call}"
            m2b = [pwB[:, 0:512], pwB[:, 512:1024]]

            def gb2(ti):
                if ti % 2 == 0:
                    return bigB[:, 4096:8192]
                return gbodd[:, :]

            def emit_mm1(blk, ti):
                i0 = 2 * blk
                for oc in range(NDC):
                    idx = ti * NDC + oc
                    wl = [f"wready{i0}_{call}"]
                    if call == "C":
                        wl.append(f"wready{i0}_Cg")
                    elif idx == 0:
                        wl.append(f"prep_bce{i0}")
                    if blk == 0:
                        wl.append(f"{tag}_hready{ti}")
                    else:
                        wl.append(f"{tag}{blk-1}_h{ti}")
                    if oc >= 2:
                        wl.append(f"{tag}{blk}_e1_{idx-2}")
                    elif ti >= 1:
                        wl.append(f"{tag}{blk}_e1_{(ti-1)*NDC+6+oc}")
                    elif blk > 0:
                        wl.append(f"{tag}{blk-1}_e1_{3*NDC+6+oc}")

                    @step("tensor", waits=tuple(wl), sem=tT)
                    def _(e, i0=i0, ti=ti, oc=oc, idx=idx):
                        buf = wsb[i0 % 2]
                        bank = pmm[idx % 2]
                        last = None
                        for dc in range(NDC):
                            last = e.matmul(
                                bank[:, :],
                                buf[:, dc * 1024 + oc * 128:
                                    dc * 1024 + oc * 128 + 128],
                                bigA[:, dc * NT + ti * TT:
                                     dc * NT + ti * TT + TT],
                                start=(dc == 0), stop=(dc == NDC - 1))
                        return last
                    mark(f"{tag}{blk}_m1_{idx}", tT)

                    wl = [f"{tag}{blk}_m1_{idx}"]
                    if oc == 0 and ti >= 2:
                        wl.append(f"{tag}{blk}_gin{ti-2}")
                    elif oc == 0 and blk > 0 and ti < 2:
                        wl.append(f"{tag}{blk-1}_gin{ti+2}")

                    @step("scalar", waits=tuple(wl), sem=tS)
                    def _(e, i0=i0, ti=ti, oc=oc, idx=idx):
                        return e.activation(
                            y1c(ti)[:, oc * TT:oc * TT + TT],
                            pmm[idx % 2][:, :], AF.Identity,
                            bias=biasv[:, i0 * 8 + oc:i0 * 8 + oc + 1])
                    mark(f"{tag}{blk}_e1_{idx}", tS)

            def emit_stats(blk, ti):
                gb = gb2(ti)
                wl = [f"{tag}{blk}_e1_{ti*NDC+NDC-1}"]
                if ti >= 2:
                    wl.append(f"{tag}{blk}_m2_{(ti-2)*NDC+NDC-1}")
                elif blk > 0:
                    wl.append(f"{tag}{blk-1}_m2_{(ti+2)*NDC+NDC-1}")

                @step("vector", waits=tuple(wl), sem=tV)
                def _(e, ti=ti, gb=gb):
                    return e.tensor_mul(gb, y1c(ti), y1c(ti))
                mark(f"{tag}{blk}_sq{ti}", tV)

                @step("tensor", waits=(f"{tag}{blk}_sq{ti}",), sem=tT)
                def _(e, gb=gb):
                    last = None
                    for gc in range(NDC):
                        last = e.matmul(prow[0:1, :], onesc[:, 0:1],
                                        gb[:, gc * TT:gc * TT + TT],
                                        start=(gc == 0),
                                        stop=(gc == NDC - 1))
                    return last
                mark(f"{tag}{blk}_smm{ti}", tT)

                @step("scalar", waits=(f"{tag}{blk}_smm{ti}",), sem=tS)
                def _(e):
                    return e.activation(rowB[0:1, :], prow[0:1, :], AF.Ln,
                                        scale=1.0 / D, bias=EPS)
                mark(f"{tag}{blk}_lnr{ti}", tS)

                @step("scalar", waits=(f"{tag}{blk}_lnr{ti}",), sem=tS)
                def _(e):
                    return e.activation(rowC[0:1, :], rowB[0:1, :],
                                        AF.Exp, scale=-0.5)
                mark(f"{tag}{blk}_rst{ti}", tS)

                @step("tensor", waits=(f"{tag}{blk}_rst{ti}",), sem=tT)
                def _(e):
                    return e.matmul(pbc[:, :], onesr[0:1, :],
                                    rowC[0:1, :], start=True, stop=True)
                mark(f"{tag}{blk}_rbc{ti}", tT)

                @step("vector", waits=(f"{tag}{blk}_rbc{ti}",), sem=tV)
                def _(e, ti=ti):
                    return e.tensor_copy(rstdB[ti % 2][:, :], pbc[:, :])
                mark(f"{tag}{blk}_rcp{ti}", tV)

                @step("vector", waits=(f"{tag}{blk}_rcp{ti}",), sem=tV)
                def _(e, ti=ti, gb=gb):
                    return e.tensor_mul(
                        gb, y1c(ti),
                        bass.AP(rstdB[ti % 2], 0,
                                [[TT, 128], [0, NDC], [1, TT]]))
                mark(f"{tag}{blk}_gin{ti}", tV)

                for oc in range(NDC):
                    @step("scalar", waits=(f"{tag}{blk}_gin{ti}",), sem=tS)
                    def _(e, blk=blk, oc=oc, gb=gb):
                        sl = gb[:, oc * TT:oc * TT + TT]
                        return e.activation(
                            sl, sl, AF.Gelu,
                            scale=lngv[:, blk * 8 + oc:blk * 8 + oc + 1],
                            bias=lnbv[:, blk * 8 + oc:blk * 8 + oc + 1])
                    mark(f"{tag}{blk}_gel{ti}_{oc}", tS)

            def emit_mm2(blk, ti):
                i1 = 2 * blk + 1
                gb = gb2(ti)
                for oc in range(NDC):
                    idx = ti * NDC + oc
                    wl = ([f"wready{i1}_Cg"] if call == "C" else []) + [
                        f"wready{i1}_{call}",
                          f"{tag}{blk}_gel{ti}_{NDC-1}"]
                    if oc >= 2:
                        wl.append(f"{tag}{blk}_e2_{idx-2}")
                    elif ti >= 1:
                        wl.append(f"{tag}{blk}_e2_{(ti-1)*NDC+6+oc}")
                    elif blk > 0:
                        wl.append(f"{tag}{blk-1}_e2_{3*NDC+6+oc}")

                    @step("tensor", waits=tuple(wl), sem=tT)
                    def _(e, i1=i1, oc=oc, gb=gb):
                        buf = wsb[i1 % 2]
                        last = None
                        for gc in range(NDC):
                            last = e.matmul(
                                m2b[oc % 2],
                                buf[:, gc * 1024 + oc * 128:
                                    gc * 1024 + oc * 128 + 128],
                                gb[:, gc * TT:gc * TT + TT],
                                start=(gc == 0), stop=(gc == NDC - 1))
                        return last
                    mark(f"{tag}{blk}_m2_{idx}", tT)

                    @step("vector", waits=(f"{tag}{blk}_m2_{idx}",), sem=tV)
                    def _(e, i1=i1, ti=ti, oc=oc):
                        hl = bigA[:, oc * NT + ti * TT:oc * NT + ti * TT + TT]
                        return e.scalar_tensor_tensor(
                            hl, m2b[oc % 2],
                            biasv[:, i1 * 8 + oc:i1 * 8 + oc + 1],
                            hl, OP.add, OP.add)
                    mark(f"{tag}{blk}_e2_{idx}", tV)
                mark(f"{tag}{blk}_h{ti}", tV)

            for blk in range(3):
                i0, i1 = 2 * blk, 2 * blk + 1
                emit_mm1(blk, 0)
                emit_mm1(blk, 1)
                emit_stats(blk, 0)
                emit_mm1(blk, 2)
                emit_stats(blk, 1)
                emit_mm1(blk, 3)
                emit_mm2(blk, 0)
                emit_stats(blk, 2)
                emit_mm2(blk, 1)
                emit_stats(blk, 3)
                emit_mm2(blk, 2)
                emit_mm2(blk, 3)
                key = "ffn_wfree" if call == "A" else "ffnC_wfree"
                P.cp[f"{key}{i0}"] = P.cp[f"{tag}{blk}_m1_{NTT*NDC-1}"]
                P.cp[f"{key}{i1}"] = P.cp[f"{tag}{blk}_m2_{NTT*NDC-1}"]
                if hook:
                    hook(blk)
                if stop_after == f"{tag}{blk}":
                    P.stop()
                    break
            mark(f"{tag}_done", tV)


        def emit_bias_center(l):
            _bw = ["ld_params"]
            if l >= 2:
                _bw.append(f"prep_bcb{l-2}")

            @step("scalar", waits=tuple(_bw), sem=tS)
            def _(e, l=l):
                return e.activation(bb16[:, :],
                                    biasv[:, l * 8:l * 8 + 8],
                                    AF.Copy)
            mark(f"prep_bcc{l}", tS)

            @step("tensor", waits=(f"prep_bcc{l}", "setup_g"), sem=tT)
            def _(e):
                return e.matmul(pwB[0:1, 512:520], onesc[:, 0:1],
                                bb16[:, :], start=True, stop=True)
            mark(f"prep_bcm{l}", tT)

            @step("vector", waits=(f"prep_bcm{l}",), sem=tV)
            def _(e):
                chain(e, e.tensor_reduce(tmp64[0:1, 30:31],
                                         pwB[0:1, 512:520], AX.X,
                                         OP.add))
                chain(e, e.tensor_scalar(tmp64[0:1, 30:31],
                                         tmp64[0:1, 30:31],
                                         1.0 / D, None, OP.mult))
                return e.tensor_copy(bb16[0:1, 0:1], tmp64[0:1, 30:31])
            mark(f"prep_bcr{l}", tV)

            @step("tensor", waits=(f"prep_bcr{l}",), sem=tT)
            def _(e):
                return e.matmul(pwB[:, 520:521], onesr[0:1, :],
                                bb16[0:1, 0:1], start=True, stop=True)
            mark(f"prep_bcb{l}", tT)

            @step("vector", waits=(f"prep_bcb{l}",), sem=tV)
            def _(e, l=l):
                chain(e, e.tensor_copy(bmean[:, :], pwB[:, 520:521]))
                return e.tensor_scalar(biasv[:, l * 8:l * 8 + 8],
                                       biasv[:, l * 8:l * 8 + 8],
                                       bmean[:, 0:1], None, OP.subtract)
            mark(f"prep_bce{l}", tV)

        for _l in (0, 2, 4):
            emit_bias_center(_l)

        emit_prep_layer(0)
        emit_outer_ln("ln1", ("hbm", u_h), bigA, "ld_params",
                      sbuf_apply=True)
        for ti in range(NTT):
            P.cp[f"fA_hready{ti}"] = P.cp[f"ln1_ap{ti}_{NDC-1}"]
        if stop_after == "ln1":
            P.stop()
        emit_prep_layer(1)

        def hookA(blk):
            if blk == 0:
                emit_prep_layer(2)
                emit_prep_layer(3)
            elif blk == 1:
                emit_prep_layer(4)
                emit_prep_layer(5)
        emit_ffn("A", hookA)

        emit_update("x1", u_h, xscr, None, "half",
                    tile_ready=lambda ti, dc: [f"fA2_e2_{ti*NDC+dc}"],
                    stats_for="ln2", stats_gate=("fA2_e1_31",))
        if stop_after == "x1":
            P.stop()

        emit_outer_ln("ln2", ("hbm", xscr), bigA, None,
                      ext_stats=True, sbuf_apply=True)

        if stop_after == "ln2":
            P.stop()

        @step("vector", waits=("ln2_done", "pglob_done"), sem=tV)
        def _(e):
            e.memset(sig[0:64, :], 0.0)
            return e.memset(sinit[0:64, :], 0.0)
        mark("sig0", tV)

        for a in range(NA):
            ti = (a * TA) // TT
            wl = [f"ln2_ap{ti}_{dg}" for dg in range(NDC)]
            if a >= 3:
                wl.append(f"sbmm{a-3}")

            @step("sync", waits=tuple(wl))
            def _(e, a=a):
                for dg in range(4):
                    e.dma_start(
                        out=utm[a % 3][:, dg * 128:dg * 128 + 128],
                        in_=bigA[:, dg * NT + a * TA:dg * NT + a * TA + TA],
                        transpose=True).then_inc(s_xb[a % 3].h, 16)
                return None

            @step("scalar", waits=tuple(wl))
            def _(e, a=a):
                for dg in range(4, NDC):
                    e.dma_start(
                        out=utm[a % 3][:, dg * 128:dg * 128 + 128],
                        in_=bigA[:, dg * NT + a * TA:dg * NT + a * TA + TA],
                        transpose=True).then_inc(s_xb[a % 3].h, 16)
                return None
            s_xb[a % 3].n += 16 * NDC
            mark(f"xbar{a}", s_xb[a % 3])

            @step("vector", waits=(("sig0",) if a == 0 else
                                   (f"scan{a-1}", f"pfl{a-1}")), sem=tV)
            def _(e):
                return e.tensor_copy(sigb[0:64, :], sig[0:64, :])
            mark(f"sigb{a}", tV)

            def emit_sbp(a, p):
                pw = (pwA, pwB)[p % 2]
                wl = [f"xbar{a}", "sbL"]
                if a == 0:
                    wl.append("ln2_done")
                if p >= 2:
                    wl.append(f"cvev{a}_{p-2}")
                elif a > 0:
                    wl.append(f"cvev{a-1}_{4-p}")

                @step("tensor", waits=tuple(wl), sem=tT)
                def _(e, a=a, p=p, pw=pw):
                    if p < 4:
                        lhs = sbL[:, p * 128:p * 128 + 128]
                        e.matmul(pw[:, 0:512], lhs, utm[a % 3][:, 0:512],
                                 start=True, stop=True)
                        return e.matmul(pw[:, 512:1024], lhs,
                                        utm[a % 3][:, 512:1024],
                                        start=True, stop=True)
                    lhs = sbL[:, 512:576]
                    e.matmul(pw[0:64, 0:512], lhs, utm[a % 3][:, 0:512],
                             start=True, stop=True)
                    return e.matmul(pw[0:64, 512:1024], lhs,
                                    utm[a % 3][:, 512:1024],
                                    start=True, stop=True)
                mark(f"sbp{a}_{p}", tT)

                wl = [f"sbp{a}_{p}", "ld_ct"]
                if p < 4:
                    if p >= 2:
                        wl.append(f"ptau{a}_{p-2}")
                    elif a > 0:
                        wl.append(f"ptau{a-1}_{p+2}")

                    @step("vector", waits=tuple(wl), sem=tV)
                    def _(e, p=p, pw=pw):
                        return e.tensor_mul(
                            cv2[:, (p % 2) * 1024:(p % 2) * 1024 + 1024],
                            pw[:, :], ctrep[:, :])
                    mark(f"cvev{a}_{p}", tV)
                else:
                    @step("vector", waits=tuple(wl), sem=tV)
                    def _(e, pw=pw):
                        return e.tensor_mul(cvqa[0:64, :], pw[0:64, :],
                                            ctrep[0:64, :])
                    mark(f"cvev{a}_{p}", tV)

            def emit_pfl(a):
                wl = [f"sigb{a}", "lams"]
                if a >= 1:
                    wl.append(f"yfe{a-1}")

                @step("tensor", waits=tuple(wl), sem=tT)
                def _(e):
                    e.matmul(pmm[0][:, :], p128L[0:64, :], sigb[0:64, 0:512],
                             start=True, stop=False, skip_group_check=True)
                    return e.matmul(pmm[1][:, :], p128L[0:64, :],
                                    sigb[0:64, 512:1024], start=True,
                                    stop=False, skip_group_check=True)
                mark(f"pfl{a}", tT)

            def emit_ptau(a, p):
                @step("tensor", waits=(f"cvev{a}_{p}", "p16bd"), sem=tT)
                def _(e, p=p):
                    lhs = p16bd[:, 0:32] if p == 0 else p16bd[:, 32:64]
                    cvs = cv2[:, (p % 2) * 1024:(p % 2) * 1024 + 1024]
                    e.matmul(pmm[0][32 * p:32 * p + 32, :], lhs,
                             cvs[:, 0:512], start=False, stop=True,
                             tile_position=(0, 32 * p),
                             skip_group_check=True)
                    return e.matmul(pmm[1][32 * p:32 * p + 32, :], lhs,
                                    cvs[:, 512:1024], start=False,
                                    stop=True,
                                    tile_position=(0, 32 * p),
                                    skip_group_check=True)
                mark(f"ptau{a}_{p}", tT)

            emit_sbp(a, 0)
            emit_sbp(a, 1)
            emit_pfl(a)
            emit_ptau(a, 0)
            emit_sbp(a, 2)
            emit_ptau(a, 1)
            emit_sbp(a, 3)
            emit_ptau(a, 2)
            emit_sbp(a, 4)
            emit_ptau(a, 3)
            mark(f"sbmm{a}", tT)

            @step("vector", waits=(f"cvev{a}_4", f"sigb{a}"), sem=tV)
            def _(e):
                return e.scalar_tensor_tensor(sig[0:64, :], sig[0:64, :],
                                              lam128[0:64, 0:1],
                                              cvqa[0:64, :], OP.mult, OP.add)
            mark(f"scan{a}", tV)

            @step("scalar", waits=(f"ptau{a}_3",), sem=tS)
            def _(e, a=a):
                e.activation(bigB[:, a * 1024:a * 1024 + 512], pmm[0][:, :],
                             AF.Copy)
                return e.activation(bigB[:, a * 1024 + 512:a * 1024 + 1024],
                                    pmm[1][:, :], AF.Copy)
            mark(f"yfe{a}", tS)

        if stop_after == "pass1":
            P.stop()
        def emit_taps(dg):
            pbufs = [xbt[0], xbt[1], sqb2[0], sqb2[1]]
            wl = ["ln2_done"] if dg == 0 else [f"tapm{dg-1}_1"]

            @step("vector", waits=tuple(wl), sem=tV)
            def _(e, dg=dg):
                last = None
                for w in range(4):
                    last = e.tensor_copy(
                        pbufs[w][:, :],
                        bass.AP(bigA, dg * NT + w * 512,
                                [[NDC * NT, 128], [1, 16], [16, 32]]))
                return last
            mark(f"perm{dg}", tV)
            for kb in range(2):
                wl = ["k16d", "ld_params"]
                if kb == 1:
                    wl.append(f"tapm{dg}_0")
                elif dg > 0:
                    wl.append(f"tapm{dg-1}_1")

                @step("vector", waits=tuple(wl), sem=tV)
                def _(e, dg=dg, kb=kb):
                    last = None
                    for k8 in range(8):
                        k = kb * 8 + k8
                        last = e.tensor_scalar(
                            tdiag[:, k8 * 128:k8 * 128 + 128], ident[:, :],
                            k16[:, dg * 16 + k:dg * 16 + k + 1], None,
                            OP.mult)
                    return last
                mark(f"tdiag{dg}_{kb}", tV)

                wl = [f"tdiag{dg}_{kb}", f"perm{dg}"]
                if dg == 0 and kb == 0:
                    wl.append(f"cvev{NA-1}_4")
                    wl.append(f"scan{NA-1}")
                if kb == 0 and dg > 0:
                    wl.append(f"mrg{dg-1}_3")

                @step("tensor", waits=tuple(wl), sem=tT)
                def _(e, dg=dg, kb=kb):
                    last = None
                    for k8 in range(8):
                        k = kb * 8 + k8
                        lhs = tdiag[:, k8 * 128:k8 * 128 + 128]
                        for w in range(4):
                            pw = (pwA, pwB)[w // 2]
                            col0 = (w % 2) * 512
                            out = pw[:, col0 + 32 * k:col0 + 512]
                            rhs = pbufs[w][:, 0:512 - 32 * k]
                            last = e.matmul(out, lhs, rhs, start=(k == 0),
                                            stop=False)
                    if kb == 1:
                        for w in range(4):
                            pw = (pwA, pwB)[w // 2]
                            col0 = (w % 2) * 512
                            last = e.matmul(pw[:, col0:col0 + 512],
                                            zrow[0:1, :], zrow512[0:1, :],
                                            start=False, stop=True)
                    return last
                mark(f"tapm{dg}_{kb}", tT)
            P.cp[f"tapm{dg}"] = P.cp[f"tapm{dg}_1"]

        def emit_merge(dg):
            for w in range(4):
                wl = [f"tapm{dg}", f"gfe{NA-1}"]
                if w >= 2:
                    wl.append(f"mrg{dg}_{w-2}")
                elif dg > 0:
                    wl.append(f"mrg{dg-1}_{w+2}")

                @step("tensor", waits=tuple(wl), sem=tT)
                def _(e, dg=dg, w=w):
                    last = None
                    for j in range(4):
                        a = 4 * w + j
                        last = e.matmul(
                            pmm[w % 2][:, j * 128:j * 128 + 128],
                            bigB[:, a * 1024 + dg * 128:
                                 a * 1024 + dg * 128 + 128],
                            ident[:, :], start=True, stop=True)
                    return last
                mark(f"trans{dg}_{w}", tT)

                @step("vector", waits=(f"trans{dg}_{w}",), sem=tV)
                def _(e, dg=dg, w=w):
                    pw = (pwA, pwB)[w // 2]
                    col0 = (w % 2) * 512
                    t1 = t1s[w % 2]
                    chain(e, e.tensor_scalar(t1[:, :], pmm[w % 2][:, :],
                                             fav[:, dg:dg + 1], None,
                                             OP.mult))
                    unp = bass.AP(pw, col0, [[1024, 128], [1, 32], [32, 16]])
                    return e.scalar_tensor_tensor(
                        bigA[:, dg * NT + w * 512:dg * NT + w * 512 + 512],
                        unp, fav[:, dg:dg + 1], t1[:, :], OP.mult, OP.add)
                mark(f"mrg{dg}_{w}", tV)

        emit_taps(0)

        @step("gpsimd", waits=(f"scan{NA-1}",), sem=s_pg, k=16)
        def _(e):
            return e.dma_start(out=cc_in.ap(), in_=sig[0:64, :])
        mark("ccin", s_pg)

        @step("gpsimd", waits=("ccin",), sem=s_cc, k=1)
        def _(e):
            return e.collective_compute(
                "AllGather", OP.bypass,
                replica_groups=[[0, 1, 2, 3], [4, 5, 6, 7]],
                ins=[cc_in.ap().opt()], outs=[cc_out.ap().opt()])
        mark("cc", s_cc)

        for m in range(3):
            @step("gpsimd", waits=(("cc",) if m == 0 else (f"sacc{m-1}",)),
                  sem=s_pg, k=16)
            def _(e, m=m):
                return e.dma_start(out=cvqa[0:64, :],
                                   in_=cc_out.ap()[m * 64:m * 64 + 64, :])
            mark(f"peer{m}", s_pg)

            @step("vector", waits=(f"peer{m}", "coefs"), sem=tV)
            def _(e, m=m):
                return e.scalar_tensor_tensor(
                    sinit[0:64, :], cvqa[0:64, :],
                    coefs[0:64, m:m + 1], sinit[0:64, :], OP.mult, OP.add)
            mark(f"sacc{m}", tV)

        @step("vector", waits=("sacc2",), sem=tV)
        def _(e):
            return e.tensor_copy(sinitb[0:64, :], sinit[0:64, :])
        mark("sinitb", tV)

        def emit_global():
          for a in range(NA):
            wl = ["lams", "sinitb", f"yfe{NA-1}"]
            if a >= 1:
                wl.append(f"gfeA{a-1}")

            @step("tensor", waits=tuple(wl), sem=tT)
            def _(e, a=a):
                return e.matmul(pmm[0][:, :],
                                pglob[0:64, a * 128:a * 128 + 128],
                                sinitb[0:64, 0:512], start=True, stop=True)
            mark(f"pgA{a}", tT)

            wl = ["lams", "sinitb", f"yfe{NA-1}"]
            if a >= 1:
                wl.append(f"gfe{a-1}")

            @step("tensor", waits=tuple(wl), sem=tT)
            def _(e, a=a):
                return e.matmul(pmm[1][:, :],
                                pglob[0:64, a * 128:a * 128 + 128],
                                sinitb[0:64, 512:1024], start=True,
                                stop=True)
            mark(f"pgB{a}", tT)

            @step("vector", waits=(f"pgA{a}",), sem=tV)
            def _(e, a=a):
                sl = bigB[:, a * 1024:a * 1024 + 512]
                return e.scalar_tensor_tensor(sl, pmm[0][:, :], 1.0,
                                              sl, OP.mult, OP.add)
            mark(f"gfeA{a}", tV)

            @step("vector", waits=(f"pgB{a}",), sem=tV)
            def _(e, a=a):
                s2 = bigB[:, a * 1024 + 512:a * 1024 + 1024]
                return e.scalar_tensor_tensor(s2, pmm[1][:, :], 1.0, s2,
                                              OP.mult, OP.add)
            mark(f"gfe{a}", tV)

        emit_global()
        mark("yfar_done", tV)
        if stop_after == "yfar":
            P.stop()
        emit_merge(0)
        for dg in range(1, NDC):
            emit_taps(dg)
            emit_merge(dg)
        mark("y2_done", tV)

        @step("vector", waits=("y2_done", "coefs"), sem=tV)
        def _(e):
            last = None
            for dg in range(NDC):
                last = e.tensor_add(bigA[:, dg * NT:dg * NT + 1],
                                    bigA[:, dg * NT:dg * NT + 1],
                                    rw2[:, dg:dg + 1])
            return last
        mark("rw_done", tV)

        if stop_after == "y2":
            P.stop()
        emit_outer_ln("lnS", ("sbuf", bigA), bigA, "rw_done",
                      src_is_b16=True, bcast_apply=False)
        emit_update("x2", xscr, xscr, None, "gb2",
                    stats_for="ln3", stats_gate=(f"lnS_rows{NTT-1}",))

        if stop_after == "x2":
            P.stop()
        emit_stream_layer(0)
        emit_stream_layer(1)
        emit_outer_ln("ln3", ("hbm", xscr), bigA, None,
                      ext_stats=True, sbuf_apply=True)
        for ti in range(NTT):
            P.cp[f"fC_hready{ti}"] = P.cp[f"ln3_ap{ti}_{NDC-1}"]

        def hookC(blk):
            if blk == 0:
                emit_stream_layer(2)
                emit_stream_layer(3)
            elif blk == 1:
                emit_stream_layer(4)
                emit_stream_layer(5)
        emit_ffn("C", hookC)

        emit_update("fout", xscr, out_h, None, "half",
                    tile_ready=lambda ti, dc: [f"fC2_e2_{ti*NDC+dc}"],
                    load_ready=lambda ti, dc: [f"x2_s{ti}_{dc}"])

        @step("sync")
        def _(e):
            e.wait_ge(s_xw[0].h, s_xw[0].n)
            e.wait_ge(s_xw[1].h, s_xw[1].n)
            return None

        with nc.Block() as blk:
            for eng in ENGS:
                def make(eng):
                    def body(e):
                        for fn in P.ops[eng]:
                            fn(e)
                    return body
                getattr(blk, eng)(make(eng))

    return nc


def _host_constants():
    c = {}
    c["c_ident"] = np.eye(128, dtype=BFnp)
    BIG = 1e9
    esb = np.zeros((128, 576), np.float32)
    s = np.arange(128)
    for p in range(4):
        for js in range(2):
            j = 2 * p + js
            col = p * 128 + js * 64
            ecol = np.where(s < 16 * j, (16.0 * j - s), BIG)
            esb[:, col:col + 64] = ecol[:, None]
    esb[:, 512:576] = (128.0 - s)[:, None]
    c["c_esb"] = esb
    etau = np.zeros((128, 64), np.float32)
    r = np.arange(128)
    for ci in range(32):
        jsub = ci // 16
        tau = float(ci % 16)
        onblk = (r // 64) == jsub
        etau[:, ci] = np.where(onblk & (jsub != 0), tau, BIG)
        etau[:, 32 + ci] = np.where(onblk, tau, BIG)
    c["c_etau"] = etau
    c["c_eglob"] = np.broadcast_to(np.arange(NT, dtype=np.float32)[None, :],
                                   (NST, NT)).copy()
    c["c_e16"] = np.broadcast_to(np.arange(16, dtype=np.float32)[None, :],
                                 (NST, 16)).copy()
    return c


def _vec_tile(v):
    return np.ascontiguousarray(v.reshape(8, 128).T)


def kernel(**inputs):
    u = np.asarray(inputs["L0_attn_u"], np.float32)
    qw = np.asarray(inputs["L0_ffn_qweights"], np.int32)
    scales = np.asarray(inputs["L0_ffn_scales"], np.float32)
    biases = np.asarray(inputs["L0_ffn_biases"], np.float32)
    lora_a = np.asarray(inputs["L0_ffn_lora_a"], np.float32)
    lora_b = np.asarray(inputs["L0_ffn_lora_b"], np.float32)
    ln_g = np.asarray(inputs["L0_ffn_ln_gamma"], np.float32)
    ln_b = np.asarray(inputs["L0_ffn_ln_beta"], np.float32)
    A_log = np.asarray(inputs["L0_attn_A_log"], np.float32)
    C = np.asarray(inputs["L0_attn_C"], np.float32)
    Dv = np.asarray(inputs["L0_attn_D"], np.float32)
    delta = np.asarray(inputs["L0_attn_delta"], np.float32)
    gamma = np.asarray(inputs["L0_attn_gamma"], np.float32)
    beta = np.asarray(inputs["L0_attn_beta"], np.float32)
    fa = np.asarray(inputs["L0_attn_freq_align"], np.float32)
    reward = np.asarray(inputs["L0_attn_reward"], np.float32)

    shared = {
        "qT": np.ascontiguousarray(qw.transpose(0, 2, 1).astype(np.int8)),
        "scT": np.ascontiguousarray(scales.transpose(0, 2, 1)),
        "la": np.ascontiguousarray(lora_a),
        "lbT": np.ascontiguousarray(lora_b.transpose(0, 2, 1)),
        "bias_t": np.concatenate([_vec_tile(biases[l]) for l in range(6)],
                                 axis=1),
        "lng_t": np.concatenate([_vec_tile(ln_g[i]) for i in range(3)],
                                axis=1),
        "lnb_t": np.concatenate([_vec_tile(ln_b[i]) for i in range(3)],
                                axis=1),
        "d_t": _vec_tile(Dv),
        "gamma_t": _vec_tile(gamma),
        "beta_t": _vec_tile(beta),
        "fa_t": _vec_tile(fa),
        "rw_t": _vec_tile(reward),
        "CT": np.ascontiguousarray(C.T),
        "A_log_t": np.ascontiguousarray(A_log.reshape(NST, 1)),
        "delta0": np.ascontiguousarray(
            np.broadcast_to(delta[0:1], (NST,)).reshape(NST, 1)),
    }
    shared.update(_host_constants())

    nc = build_nc()
    in_maps = []
    for core in range(8):
        b, seg = core // 4, core % 4
        m = dict(shared)
        m["u_fm"] = np.ascontiguousarray(u[b, seg * NT:(seg + 1) * NT, :].T)
        m["is_first"] = np.full((128, 1), 1.0 if seg == 0 else 0.0,
                                np.float32)
        pmk = np.zeros((NST, 4), np.float32)
        e2048 = np.zeros((NST, 4), np.float32)
        for p in range(3):
            if p < seg:
                pmk[:, p] = 1.0
                e2048[:, p] = 2048.0 * (seg - 1 - p)
        m["peer_mask"] = pmk
        m["c_e2048"] = e2048
        in_maps.append(m)

    res = run_bass_kernel_spmd(nc, in_maps, core_ids=list(range(8)))
    out = np.empty((2, 8192, D), np.float32)
    for core in range(8):
        b, seg = core // 4, core % 4
        out[b, seg * NT:(seg + 1) * NT, :] = \
            np.asarray(res.results[core]["out_fm"]).T
    return out
